# version 22
# baseline (speedup 1.0000x reference)
import math
from contextlib import ExitStack

import numpy as np
import ml_dtypes

import concourse.bass as bass
import concourse.mybir as mybir
from concourse.bass_utils import run_bass_kernel_spmd

F32 = mybir.dt.float32
BF16 = mybir.dt.bfloat16
I8 = mybir.dt.int8
AF = mybir.ActivationFunctionType
ALU = mybir.AluOpType
AX = mybir.AxisListType

S = 2048
D = 1024
NT = 16
NCH = 8
H = 16
DHR = 32
DLAT = 256
DFF = 4096
EPS_LN = 1e-5
EPS_RMS = 1e-8
BF = ml_dtypes.bfloat16

ENGS = ("sync", "gpsimd", "tensor", "vector", "scalar")
QSEM = {"sync": "LD", "gpsimd": "G", "tensor": "T", "vector": "V",
        "scalar": "Sc"}


class Prog:
    def __init__(self):
        self.nc = bass.Bass("TRN2", target_bir_lowering=False)
        self.ops = {e: [] for e in ENGS}
        self.ctx = ExitStack()
        self.sems = {}
        for name in ("T", "V", "Sc", "G"):
            self.sems[name] = [self.ctx.enter_context(self.nc.semaphore(name)), 0]
        self.NSLOT = 4
        for q in ("LD", "GD", "ST", "AL"):
            for i in range(self.NSLOT):
                n = f"{q}{i}"
                self.sems[n] = [self.ctx.enter_context(self.nc.semaphore(n)), 0]
        self.dma_idx = {"LD": 0, "GD": 0, "ST": 0, "AL": 0}
        self.consts = {}
        self.suppress_same_engine = False
        self._names = {}
        self.last_write = {}
        self.readers = {}

    def _uniq(self, name):
        n = self._names.get(name, 0)
        self._names[name] = n + 1
        return name if n == 0 else f"{name}_{n}"

    def sb(self, name, shape, dt, stack=None, side=None):
        return (stack or self.ctx).enter_context(
            self.nc.sbuf_tensor(self._uniq(name), shape, dt, side=side))

    def ps(self, name, shape, dt, stack=None):
        return (stack or self.ctx).enter_context(
            self.nc.psum_tensor(self._uniq(name), shape, dt))

    def dram_in(self, name, shape, dt):
        return self.nc.dram_tensor(name, shape, dt, kind="ExternalInput").ap()

    def dram_out(self, name, shape, dt):
        return self.nc.dram_tensor(name, shape, dt, kind="ExternalOutput").ap()

    def _compute_waits(self, mysem, reads, writes, sreads):
        need = {}

        def add(dep):
            if dep is None:
                return
            sn, v = dep
            if need.get(sn, 0) < v:
                need[sn] = v

        sup = getattr(self, "suppress_same_engine", True)
        for r in tuple(reads) + tuple(writes):
            lw = self.last_write.get(r)
            if lw is not None and (not sup or lw[0] != mysem):
                add(lw)
        for r in tuple(sreads):
            add(self.last_write.get(r))
        for r in tuple(writes):
            for sn, v in self.readers.get(r, {}).items():
                if not sup or sn != mysem:
                    add((sn, v))
        return sorted(need.items())

    def _record(self, mysem, count, reads, writes, sreads):
        for r in tuple(reads) + tuple(sreads):
            self.readers.setdefault(r, {})[mysem] = count
        for r in tuple(writes):
            self.last_write[r] = (mysem, count)
            self.readers[r] = {}

    def op(self, eng, f, reads=(), writes=(), sreads=(), dma=False,
           extra=(), dmaq=None):
        if dma:
            q = dmaq or {"sync": "LD", "gpsimd": "GD", "scalar": "ST"}[eng]
            slot = self.dma_idx[q] % self.NSLOT
            self.dma_idx[q] += 1
            sem = f"{q}{slot}"
            incv = 16
            gate = (sem, self.sems[sem][1])
        else:
            sem = QSEM[eng]
            incv = 1
            gate = None
        waits = self._compute_waits(sem if not dma else "__dma__",
                                    reads, writes, sreads)
        if gate is not None and gate[1] > 0:
            waits = list(waits) + [gate]
        for sn, v in extra:
            waits = list(waits) + [(sn, v)]
        self.sems[sem][1] += incv
        count = self.sems[sem][1]
        sems = self.sems

        def run(e, f=f, waits=tuple(waits), sem=sem, incv=incv):
            for sn, v in waits:
                e.wait_ge(sems[sn][0], v)
            f(e).then_inc(sems[sem][0], incv)

        self.ops[eng].append(run)
        self._record(sem, count, reads, writes, sreads)
        return (sem, count)

    def group(self, eng, fns, reads=(), writes=(), sreads=()):
        sem = QSEM[eng]
        waits = self._compute_waits(sem, reads, writes, sreads)
        self.sems[sem][1] += 1
        count = self.sems[sem][1]
        sems = self.sems

        def run(e, fns=tuple(fns), waits=tuple(waits), sem=sem):
            for sn, v in waits:
                e.wait_ge(sems[sn][0], v)
            for f in fns[:-1]:
                f(e)
            fns[-1](e).then_inc(sems[sem][0], 1)

        self.ops[eng].append(run)
        self._record(sem, count, reads, writes, sreads)
        return (sem, count)

    def barrier_wait(self, eng, deps):
        sems = self.sems
        wl = tuple(d for d in deps if d is not None)

        def run(e, wl=wl):
            for sn, v in wl:
                e.wait_ge(sems[sn][0], v)

        self.ops[eng].append(run)

    def dep(self, res):
        return self.last_write.get(res)

    def full_barrier(self, include_bg=False):
        deps = [(n, s[1]) for n, s in self.sems.items()
                if s[1] > 0 and (include_bg or not n.startswith("AL"))]
        for eng in ENGS:
            self.barrier_wait(eng, deps)

    def finalize(self):
        with self.nc.Block() as block:
            for eng in ENGS:
                fns = self.ops[eng]
                getattr(block, eng)(lambda e, fns=fns: [f(e) for f in fns])
        self.ctx.close()
        return self.nc



def _rope_tables():
    j = np.arange(16, dtype=np.float64)
    inv = 10000.0 ** (-j / 16.0)
    t = np.arange(S, dtype=np.float64)
    ang = inv[:, None] * t[None, :]
    cos = np.cos(ang)
    sin = np.sin(ang)
    cos32 = np.ascontiguousarray(np.concatenate([cos, cos], 0).astype(BF))
    sig32 = np.ascontiguousarray(np.concatenate([-sin, sin], 0).astype(BF))
    P = np.zeros((32, 32), dtype=BF)
    for k in range(16):
        P[16 + k, k] = 1.0
        P[k, 16 + k] = 1.0
    return cos32, sig32, P


def _qr_perm():
    perm = []
    for h in range(H):
        perm += [h * 32 + 2 * k for k in range(16)]
        perm += [h * 32 + 2 * k + 1 for k in range(16)]
    return np.array(perm)


def _kr_perm():
    return np.array([2 * k for k in range(16)] + [2 * k + 1 for k in range(16)])


def _mask_tri():
    k = np.arange(128)[:, None]
    q = np.arange(128)[None, :]
    return np.ascontiguousarray((q >= k).astype(BF))


def _pieces(lo, hi, grid=512):
    out = []
    while lo < hi:
        n = min(hi - lo, grid - (lo % grid))
        out.append((lo, n))
        lo += n
    return out



def emit_consts(P, specs):
    for name, shape, dt, valv in specs:
        t = P.sb("c_" + name, shape, dt)
        P.op("vector", lambda e, t=t, v=valv: e.memset(t[:], v),
             writes=("c_" + name,))
        P.consts[name] = t


def emit_load_x(P, x_h, xs, res="xs"):
    P.op("sync", lambda e: e.dma_start(
        out=xs[:].rearrange("p (t d) -> p t d", d=D),
        in_=x_h.rearrange("(t p) d -> p t d", p=128)),
        writes=(res,), dma=True)


def make_stats(P, stack, tag, with_scratch=True):
    d = {"tag": tag}
    for n in ("sum", "ssq", "mean", "var", "m2", "sqv", "rstd", "nmr", "r2"):
        d[n] = P.sb(f"{tag}_{n}", [128, NT], F32, stack)
    if with_scratch:
        d["sq0"] = P.sb(f"{tag}_sq0", [128, D], F32, stack)
        d["sq1"] = P.sb(f"{tag}_sq1", [128, D], F32, stack)
    return d


def emit_preln_stats(P, xs_res, xs, st, pre=False):
    tg = st["tag"]
    if not pre:
        P.op("vector", lambda e: e.reduce_sum(
            st["sum"][:], xs[:].rearrange("p (t d) -> p t d", d=D), axis=AX.X),
            reads=(xs_res,), writes=(tg + ".sum",))
        for t in range(NT):
            scr = st["sq0"] if t % 2 == 0 else st["sq1"]
            r = tg + (".sq0" if t % 2 == 0 else ".sq1")
            P.op("scalar", lambda e, t=t, scr=scr: e.activation(
                scr[:], xs[:, t * D:(t + 1) * D], AF.Square),
                reads=(xs_res,), writes=(r,))
            P.op("vector", lambda e, t=t, scr=scr: e.reduce_sum(
                st["ssq"][:, t:t + 1], scr[:], axis=AX.X),
                reads=(r,), writes=(tg + f".ssq{t}",))
    P.op("vector", lambda e: e.tensor_scalar(
        st["mean"][:], st["sum"][:], 1.0 / D, None, ALU.mult),
        reads=(tg + ".sum",), writes=(tg + ".mean",))
    P.op("vector", lambda e: e.tensor_scalar(
        st["var"][:], st["ssq"][:], 1.0 / D, None, ALU.mult),
        reads=tuple(tg + f".ssq{t}" for t in range(NT)),
        writes=(tg + ".var",))
    P.op("vector", lambda e: e.tensor_tensor(
        st["m2"][:], st["mean"][:], st["mean"][:], ALU.mult),
        reads=(tg + ".mean",), writes=(tg + ".m2",))
    P.op("vector", lambda e: e.tensor_tensor(
        st["var"][:], st["var"][:], st["m2"][:], ALU.subtract),
        reads=(tg + ".var", tg + ".m2"), writes=(tg + ".var",))
    P.op("scalar", lambda e: e.activation(
        st["sqv"][:], st["var"][:], AF.Sqrt, bias=P.consts["eps_ln"][:]),
        reads=(tg + ".var",), sreads=("c_eps_ln",), writes=(tg + ".sqv",))
    P.op("vector", lambda e: e.reciprocal(st["rstd"][:], st["sqv"][:]),
         reads=(tg + ".sqv",), writes=(tg + ".rstd",))
    P.op("vector", lambda e: e.tensor_tensor(
        st["nmr"][:], st["mean"][:], st["rstd"][:], ALU.mult),
        reads=(tg + ".mean", tg + ".rstd"), writes=(tg + ".nmr",))
    P.op("vector", lambda e: e.tensor_scalar(
        st["nmr"][:], st["nmr"][:], -1.0, None, ALU.mult),
        reads=(tg + ".nmr",), writes=(tg + ".nmr",))


def mk_ln_apply(P, xs_res, xs, st, bias_key="nmr", scale_key="rstd"):
    tg = st["tag"]

    def apply_tile(t, dst, dres):
        P.op("scalar", lambda e, t=t, dst=dst: e.activation(
            dst[:], xs[:, t * D:(t + 1) * D], AF.Identity,
            bias=st[bias_key][:, t:t + 1], scale=st[scale_key][:, t:t + 1]),
            reads=(xs_res,), sreads=(tg + "." + bias_key, tg + "." + scale_key),
            writes=(dres,))

    return apply_tile


def emit_ln_transpose(P, apply_tile, fmT, fm_res, tp_stack):
    pt = [P.ps(f"tp_pt{i}", [128, 1024], BF16, tp_stack) for i in range(2)]
    lnr = [P.sb(f"tp_ln{i}", [128, D], BF16, tp_stack) for i in range(2)]
    identB = P.consts["identB"]
    for t in range(NT):
        s = t % 2
        lt, lres = lnr[s], f"tp_ln{s}"
        pp, pres = pt[s], f"tp_pt{s}"
        apply_tile(t, lt, lres)
        fns = []
        for c in range(NCH):
            fns.append(lambda e, c=c, lt=lt, pp=pp: e.transpose(
                pp[:, c * 128:(c + 1) * 128],
                lt[:, c * 128:(c + 1) * 128], identB[:]))
        P.group("tensor", fns, reads=(lres, "c_identB"), writes=(pres,))
        P.op("vector", lambda e, t=t, pp=pp: e.tensor_copy(
            fmT[:].rearrange("p (c k) -> p c k", k=S)[:, :, t * 128:(t + 1) * 128],
            pp[:].rearrange("p (c k) -> p c k", k=128)),
            reads=(pres,), writes=(fm_res,))


def emit_store_x(P, xs_res, xs, out_h):
    P.op("scalar", lambda e: e.dma_start(
        out=out_h.rearrange("(t p) d -> p t d", p=128),
        in_=xs[:].rearrange("p (t d) -> p t d", d=D)),
        reads=(xs_res,), dma=True)
    dep = P.dep_st()
    P.barrier_wait("sync", dep)
    P.barrier_wait("tensor", dep)
    P.barrier_wait("vector", dep)


def _dep_st(self):
    return [(f"ST{i}", self.sems[f"ST{i}"][1]) for i in range(self.NSLOT)
            if self.sems[f"ST{i}"][1] > 0]


Prog.dep_st = _dep_st


class PostLN:

    def __init__(self, P, xs_res, xs, stack, tag, nxt=None):
        self.P, self.xs, self.xs_res = P, xs, xs_res
        self.st = make_stats(P, stack, tag)
        self.o = [P.sb(tag + "_o0", [128, D], F32, stack),
                  P.sb(tag + "_o1", [128, D], F32, stack)]
        self.tag = tag
        self.nxt = nxt

    def chunk(self, i, psum_ap, psum_res, bias_bcast=None, bias_res=None,
              scale=None, scale_res=None):
        P, st, tg = self.P, self.st, self.tag
        if isinstance(psum_res, str):
            psum_res = (psum_res,)
        o = self.o[i % 2]
        ores = f"{tg}.o{i % 2}"
        if bias_bcast is not None:
            P.op("vector", lambda e: e.tensor_tensor(
                o[:], psum_ap, bias_bcast[:], ALU.add),
                reads=psum_res + (bias_res,), writes=(ores,))
        elif scale is not None:
            P.op("scalar", lambda e: e.activation(
                o[:], psum_ap, AF.Identity, scale=scale),
                reads=psum_res, sreads=(scale_res,), writes=(ores,))
        else:
            P.op("vector", lambda e: e.tensor_copy(o[:], psum_ap),
                 reads=psum_res, writes=(ores,))
        P.op("vector", lambda e, i=i: e.reduce_sum(
            st["sum"][:, i:i + 1], o[:], axis=AX.X),
            reads=(ores,), writes=(tg + ".sum",))
        P.op("scalar", lambda e: e.activation(
            st["sq0"][:], o[:], AF.Square),
            reads=(ores,), writes=(tg + ".sq0",))
        P.op("vector", lambda e, i=i: e.reduce_sum(
            st["ssq"][:, i:i + 1], st["sq0"][:], axis=AX.X),
            reads=(tg + ".sq0",), writes=(tg + ".ssq",))
        P.op("vector", lambda e, i=i: e.tensor_scalar(
            st["mean"][:, i:i + 1], st["sum"][:, i:i + 1], 1.0 / D, None,
            ALU.mult), reads=(tg + ".sum",), writes=(tg + ".mean",))
        P.op("vector", lambda e, i=i: e.tensor_scalar(
            st["var"][:, i:i + 1], st["ssq"][:, i:i + 1], 1.0 / D, None,
            ALU.mult), reads=(tg + ".ssq",), writes=(tg + ".var",))
        P.op("vector", lambda e, i=i: e.tensor_tensor(
            st["m2"][:, i:i + 1], st["mean"][:, i:i + 1],
            st["mean"][:, i:i + 1], ALU.mult),
            reads=(tg + ".mean",), writes=(tg + ".m2",))
        P.op("vector", lambda e, i=i: e.tensor_tensor(
            st["var"][:, i:i + 1], st["var"][:, i:i + 1],
            st["m2"][:, i:i + 1], ALU.subtract),
            reads=(tg + ".var", tg + ".m2"), writes=(tg + ".var",))
        P.op("scalar", lambda e, i=i: e.activation(
            st["sqv"][:, i:i + 1], st["var"][:, i:i + 1], AF.Sqrt,
            bias=P.consts["eps_ln"][:]),
            reads=(tg + ".var",), sreads=("c_eps_ln",), writes=(tg + ".sqv",))
        P.op("vector", lambda e, i=i: e.reciprocal(
            st["rstd"][:, i:i + 1], st["sqv"][:, i:i + 1]),
            reads=(tg + ".sqv",), writes=(tg + ".rstd",))
        P.op("vector", lambda e, o=o, i=i: e.tensor_scalar(
            st["sq1"][:], o[:], st["mean"][:, i:i + 1],
            st["rstd"][:, i:i + 1], ALU.subtract, ALU.mult),
            reads=(ores,), sreads=(tg + ".mean", tg + ".rstd"),
            writes=(tg + ".sq1",))
        P.op("vector", lambda e, i=i: e.tensor_tensor(
            self.xs[:, i * D:(i + 1) * D], self.xs[:, i * D:(i + 1) * D],
            st["sq1"][:], ALU.add),
            reads=(tg + ".sq1", self.xs_res), writes=(self.xs_res,))
        if self.nxt is not None:
            nx, ng = self.nxt, self.nxt["tag"]
            P.op("vector", lambda e, i=i: e.reduce_sum(
                nx["sum"][:, i:i + 1], self.xs[:, i * D:(i + 1) * D],
                axis=AX.X), reads=(self.xs_res,), writes=(ng + ".sum",))
            P.op("scalar", lambda e, i=i: e.activation(
                st["sq0"][:], self.xs[:, i * D:(i + 1) * D], AF.Square,
                accum_out=nx["ssq"][:, i:i + 1]),
                reads=(self.xs_res,),
                writes=(tg + ".sq0", ng + f".ssq{i}"))



def emit_attention_head(P, h, qk_lhsT, qk_rhs, qk_res, v_km, attn, psA, psB,
                        pbuf, causal, exp_scale, exp_bias, exp_bias_res,
                        clamp, mask0, scrA, rrow, astage, extra_items=(),
                        evac_eng="scalar"):
    items = []
    half_last = {}
    for half in range(2):
        for kc in range(NT):
            q_lo, q_hi = half * 1024, half * 1024 + 1024
            q_start = max(q_lo, kc * 128) if causal else q_lo
            if q_start < q_hi:
                items.append((kc, half, q_start, q_hi))
                half_last[half] = len(items) - 1
    N = len(items)
    ex = list(extra_items)
    M = N + len(ex)

    def slot_of(n):
        s = n % 3
        half = items[n][1] if n < N else 1
        unsafe = (n < N and half == 1 and n < half_last[0] + 7)
        if s == 2 and unsafe:
            s = n % 2
        if s < 2:
            return psA, s * 1024, (f"psA.s{s}",)
        w = (1 - half) * 1024
        return psB, w, (f"psB.b{2 * (1 - half)}", f"psB.b{2 * (1 - half) + 1}")

    def emit_scores(n):
        kc, half, q_start, q_hi = items[n]
        pst, sbase, pres = slot_of(n)
        base = sbase + (q_start - half * 1024)
        end = sbase + 1024
        fns = []
        for (lo, nn) in _pieces(base, end):
            gq = half * 1024 + (lo - sbase)
            fns.append(lambda e, kc=kc, lo=lo, nn=nn, gq=gq, pst=pst: e.matmul(
                pst[:, lo:lo + nn], qk_lhsT(kc), qk_rhs(gq, nn),
                start=True, stop=True))
        P.group("tensor", fns, reads=qk_res, writes=pres)

    def emit_expmask(n):
        kc, half, q_start, q_hi = items[n]
        pst, sbase, pres = slot_of(n)
        base = sbase + (q_start - half * 1024)
        end = sbase + 1024
        pslot = n % 2
        pbres = f"pbuf.s{pslot}"
        pb = pbuf[:, pslot * 1024 + (q_start - half * 1024): pslot * 1024 + 1024]
        sc_ap = pst[:, base:end]
        kw = {"scale": exp_scale}
        sr = ()
        if exp_bias is not None:
            kw["bias"] = exp_bias
            sr = (exp_bias_res,)
        P.op("scalar", lambda e, sc_ap=sc_ap, pb=pb, kw=kw: e.activation(
            pb, sc_ap, AF.Exp, **kw),
            reads=pres, sreads=sr, writes=(pbres,))
        if clamp is not None:
            P.op("vector", lambda e, pb=pb: e.tensor_scalar(
                pb, pb, clamp[1], clamp[0], ALU.min, ALU.max),
                reads=(pbres,), writes=(pbres,))
        if causal:
            dlo = kc * 128
            if q_start <= dlo < q_hi:
                loc = dlo - q_start
                P.op("vector", lambda e, pb=pb, loc=loc: e.tensor_tensor(
                    pb[:, loc:loc + 128], pb[:, loc:loc + 128], mask0[:],
                    ALU.mult), reads=(pbres, "c_mask0"), writes=(pbres,))

    def emit_pv(n):
        kc, half, q_start, q_hi = items[n]
        pslot = n % 2
        pbres = f"pbuf.s{pslot}"
        pb = pbuf[:, pslot * 1024 + (q_start - half * 1024): pslot * 1024 + 1024]
        fns = []
        wr = []
        for (gq, nn) in _pieces(q_start, q_hi):
            blk = gq // 512
            if causal:
                firstkc = 0
                lastkc = min(15, (blk * 512 + 511) // 128)
            else:
                firstkc, lastkc = 0, 15
            wr.append(f"psB.b{blk}")
            fns.append(lambda e, kc=kc, gq=gq, nn=nn, h=h, pb=pb,
                       q_start=q_start, lastkc=lastkc: e.matmul(
                           psB[:, gq:gq + nn][0:65, :],
                           v_km[:, kc * (H * 65) + h * 65:
                                kc * (H * 65) + h * 65 + 65],
                           pb[:, gq - q_start: gq - q_start + nn],
                           start=(kc == 0), stop=(kc == lastkc)))
        P.group("tensor", fns, reads=(pbres, "v_km"), writes=tuple(wr))

    odd = (h % 2 == 1)

    def emit_norm_s1(b):
        bres = f"psB.b{b}"
        rr = rrow[b % 2]
        P.op("vector", lambda e, b=b, rr=rr: e.reciprocal(
            rr[:], psB[64:65, b * 512:(b + 1) * 512]),
            reads=(bres,), writes=(f"rrow{b % 2}",))
        scr = scrA[b % 2]
        sres = f"scrA{b % 2}"
        if evac_eng == "scalar":
            P.op("scalar", lambda e, b=b, scr=scr: e.activation(
                scr[:], psB[0:64, b * 512:(b + 1) * 512], AF.Identity),
                reads=(bres,), writes=(sres,))
        else:
            P.op("vector", lambda e, b=b, scr=scr: e.tensor_copy(
                scr[:], psB[0:64, b * 512:(b + 1) * 512]),
                reads=(bres,), writes=(sres,))

    def emit_norm_s2(b):
        bres = f"psB.b{b}"
        rr = rrow[b % 2]
        scr = scrA[b % 2]
        sres = f"scrA{b % 2}"
        P.op("tensor", lambda e, b=b, rr=rr: e.matmul(
            psB[0:64, b * 512:(b + 1) * 512], P.consts["ones_f"][0:1, 0:64],
            rr[:], start=True, stop=True),
            reads=(f"rrow{b % 2}", "c_ones_f", sres), writes=(bres,))
        col0 = (h // 2) * S + b * 512
        if not odd:
            P.op("vector", lambda e, scr=scr, col0=col0, b=b:
                 e.tensor_tensor(attn[0:64, col0:col0 + 512],
                                 scr[:], psB[0:64, b * 512:(b + 1) * 512],
                                 ALU.mult),
                 reads=(sres, bres), writes=("attn",))
        else:
            P.op("vector", lambda e, scr=scr, b=b:
                 e.tensor_tensor(astage[:, b * 512:(b + 1) * 512],
                                 scr[:], psB[0:64, b * 512:(b + 1) * 512],
                                 ALU.mult),
                 reads=(sres, bres), writes=("astage",))

    norm_s1q = []
    norm_s2q = []

    def norm_tick(n):
        if norm_s2q and norm_s2q[0][0] <= n:
            emit_norm_s2(norm_s2q.pop(0)[1])
        if norm_s1q:
            b = norm_s1q.pop(0)
            emit_norm_s1(b)
            norm_s2q.append((n + 2, b))

    for n in range(M + 2):
        if 1 <= n < M + 1:
            m = n - 1
            if m < N:
                emit_expmask(m)
            else:
                ex[m - N]["post"](slot_of(m))
        if n < M:
            if n < N:
                emit_scores(n)
            else:
                ex[n - N]["mm"](slot_of(n))
        if n >= 2:
            if n - 2 < N:
                emit_pv(n - 2)
            norm_tick(n)
            if n - 2 == half_last.get(0):
                norm_s1q += [0, 1]
            if n - 2 == half_last.get(1):
                norm_s1q += [2, 3]
    for b in norm_s1q:
        emit_norm_s1(b)
        norm_s2q.append((0, b))
    for _, b in norm_s2q:
        emit_norm_s2(b)
    if odd:
        col0 = (h // 2) * S
        P.op("gpsimd", lambda e, col0=col0: e.dma_start(
            out=attn[64:128, col0:col0 + S], in_=astage[:]),
            reads=("astage",), writes=("attn",), dma=True)



def emit_block_l0(P, hw):
    xs = hw["xs"]
    w_dq, w_uq, w_qr = hw["w_dq"], hw["w_uq"], hw["w_qr"]
    w_dkv, w_uk, w_uv = hw["w_dkv"], hw["w_uk"], hw["w_uv"]
    w_kr, w_o = hw["w_kr"], hw["w_o"]
    st_blk = ExitStack()
    c_qb = P.sb("c_qb", [128, 2 * S], BF16, st_blk)
    c_kvb = P.sb("c_kvb", [128, 2 * S], BF16, st_blk)
    v_km = P.sb("v_km", [128, NT * H * 65], BF16, st_blk)
    kr_b = P.sb("kr_b", [32, S], BF16, st_blk)
    mask0 = P.sb("s_mask0", [128, 128], BF16, st_blk)
    cos32 = P.sb("s_cos32", [32, S], BF16, st_blk)
    sig32 = P.sb("s_sig32", [32, S], BF16, st_blk)
    swap32 = P.sb("s_swap32", [32, 32], BF16, st_blk)
    scr1 = P.sb("ropes1", [32, 1024], BF16, st_blk)
    P.op("vector", lambda e: e.memset(
        v_km[:].rearrange("p (k c) -> p k c", c=65)[:, :, 64:65], 1.0),
        writes=("v_km.ones",))
    for dst, srch, r in ((mask0, hw["mask0"], "c_mask0"),
                         (cos32, hw["cos32"], "cos32"),
                         (sig32, hw["sig32"], "sig32"),
                         (swap32, hw["swap32"], "swap32")):
        P.op("sync", lambda e, dst=dst, srch=srch: e.dma_start(
            out=dst[:], in_=srch), writes=(r,), dma=True)

    st_pre = ExitStack()
    w_dqb = P.sb("w_dqb", [128, 8 * DLAT], BF16, st_pre)
    w_dkvb = P.sb("w_dkvb", [128, 8 * DLAT], BF16, st_pre)
    w_krb = P.sb("w_krb", [128, 8 * DHR], BF16, st_pre)
    w_uvb = P.sb("w_uvb", [128, 2 * D], BF16, st_pre)
    for dst, src, dd, r in ((w_dqb, w_dq, DLAT, "w_dqb"),
                           (w_dkvb, w_dkv, DLAT, "w_dkvb"),
                           (w_krb, w_kr, DHR, "w_krb"),
                           (w_uvb, w_uv, D, "w_uvb")):
        P.op("gpsimd", lambda e, dst=dst, src=src, dd=dd: e.dma_start(
            out=dst[:].rearrange("p (k m) -> p k m", m=dd),
            in_=src.rearrange("(k p) m -> p k m", p=128)),
            writes=(r,), dma=True)

    stats = make_stats(P, st_pre, "ln0")
    emit_preln_stats(P, "xs", xs, stats)
    lnT = P.sb("lnT", [128, NCH * S], BF16, st_pre)
    tp_stack = ExitStack()
    emit_ln_transpose(P, mk_ln_apply(P, "xs", xs, stats), lnT, "lnT", tp_stack)
    tp_stack.close()
    P.full_barrier()

    psA = P.ps("psA", [128, 2048], F32, st_blk)
    psB = P.ps("psB", [128, 2048], F32, st_blk)
    PSA = ("psA.s0", "psA.s1")
    PSB = tuple(f"psB.b{b}" for b in range(4))

    for (wt, wres, dstb, dres) in ((w_dqb, "w_dqb", c_qb, "c_qb"),
                                   (w_dkvb, "w_dkvb", c_kvb, "c_kvb")):
        for m in range(2):
            fns = []
            for nq in range(4):
                for k in range(8):
                    fns.append(lambda e, m=m, nq=nq, k=k, wt=wt: e.matmul(
                        psA[:, nq * 512:(nq + 1) * 512],
                        wt[:, k * DLAT + m * 128: k * DLAT + (m + 1) * 128],
                        lnT[:, k * S + nq * 512: k * S + (nq + 1) * 512],
                        start=(k == 0), stop=(k == 7)))
            P.group("tensor", fns, reads=(wres, "lnT"), writes=PSA)
            P.op("vector", lambda e, dstb=dstb, m=m: e.tensor_copy(
                dstb[:, m * S:(m + 1) * S], psA[:]),
                reads=PSA, writes=(dres,))

    krr = P.sb("krraw", [32, S], BF16, st_pre)
    fns = []
    for nq in range(4):
        for k in range(8):
            fns.append(lambda e, nq=nq, k=k: e.matmul(
                psA[:, nq * 512:(nq + 1) * 512][0:32, :],
                w_krb[:, k * DHR:(k + 1) * DHR],
                lnT[:, k * S + nq * 512: k * S + (nq + 1) * 512],
                start=(k == 0), stop=(k == 7)))
    P.group("tensor", fns, reads=("w_krb", "lnT"), writes=PSA)
    P.op("vector", lambda e: e.tensor_copy(krr[:], psA[:, 0:2048][0:32, :]),
         reads=PSA, writes=("krraw",))
    fns = [lambda e, nq=nq: e.matmul(
        psB[:, nq * 512:(nq + 1) * 512][0:32, :],
        swap32[:], krr[:, nq * 512:(nq + 1) * 512],
        start=True, stop=True) for nq in range(4)]
    P.group("tensor", fns, reads=("krraw", "swap32"), writes=PSB)
    for hf in range(2):
        c0, c1 = hf * 1024, (hf + 1) * 1024
        P.op("vector", lambda e, c0=c0, c1=c1: e.tensor_tensor(
            scr1[:, 0:1024], krr[:, c0:c1], cos32[:, c0:c1], ALU.mult),
            reads=("krraw", "cos32"), writes=("rp",))
        P.op("vector", lambda e, c0=c0, c1=c1: e.tensor_tensor(
            kr_b[:, c0:c1], psB[:, c0:c1][0:32, :], sig32[:, c0:c1],
            ALU.mult),
            reads=PSB + ("sig32",), writes=("kr_b",))
        P.op("vector", lambda e, c0=c0, c1=c1: e.tensor_tensor(
            kr_b[:, c0:c1], scr1[:, 0:1024], kr_b[:, c0:c1], ALU.add),
            reads=("rp", "kr_b"), writes=("kr_b",))

    for kc in range(NT):
        fns = []
        for lk in range(2):
            for n2 in range(2):
                fns.append(lambda e, kc=kc, lk=lk, n2=n2: e.matmul(
                    psA[:, n2 * 512:(n2 + 1) * 512],
                    c_kvb[:, lk * S + kc * 128: lk * S + (kc + 1) * 128],
                    w_uvb[:, lk * D + n2 * 512: lk * D + (n2 + 1) * 512],
                    start=(lk == 0), stop=(lk == 1)))
        P.group("tensor", fns, reads=("c_kvb", "w_uvb"),
                writes=("psA.s0",))
        P.op("vector", lambda e, kc=kc: e.tensor_copy(
            v_km[:, kc * H * 65:(kc + 1) * H * 65].rearrange(
                "p (h c) -> p h c", c=65)[:, :, 0:64],
            psA[:, 0:1024].rearrange("p (h c) -> p h c", c=64)),
            reads=("psA.s0", "v_km.ones"), writes=("v_km",))
    st_pre.close()
    P.full_barrier()

    st_att = ExitStack()
    w_uqb = P.sb("w_uqb", [128, 2 * D], BF16, st_att)
    w_qrb = P.sb("w_qrb", [128, 2 * 512], BF16, st_att)
    w_ukb = P.sb("w_ukb", [128, 2 * D], BF16, st_att)
    for dst, srcw, dd, r in ((w_uqb, w_uq, D, "w_uqb"),
                            (w_qrb, w_qr, 512, "w_qrb"),
                            (w_ukb, w_uk, D, "w_ukb")):
        P.op("gpsimd", lambda e, dst=dst, srcw=srcw, dd=dd: e.dma_start(
            out=dst[:].rearrange("p (k m) -> p k m", m=dd),
            in_=srcw.rearrange("(k p) m -> p k m", p=128)),
            writes=(r,), dma=True)

    qh = [P.sb(f"qh{i}", [96, S], BF16, st_att) for i in range(2)]
    kh = [P.sb(f"kh{i}", [96, S], BF16, st_att) for i in range(2)]
    qraw = P.sb("qraw", [32, S], BF16, st_att)
    qrstage = qraw
    st_attn = ExitStack()
    attn = P.sb("attn", [128, NCH * S], BF16, st_attn, side="right")
    astage = P.sb("astage", [64, S], BF16, st_att)
    pbuf = P.sb("pbuf", [128, 2 * 1024], BF16, st_att)
    scrA = (P.sb("scrA0", [64, 512], BF16, st_att),
            P.sb("scrA1", [64, 512], BF16, st_att))
    rrow = [P.sb(f"rrow{i}", [1, 512], F32, st_att) for i in range(2)]

    for i in range(2):
        P.op("gpsimd", lambda e, i=i: e.dma_start(
            out=kh[i][64:96, :], in_=kr_b[:]),
            reads=("kr_b",), writes=(f"kh{i}",), dma=True)

    EXP_SCALE = float(1.0 / math.sqrt(96.0))
    CL_HI = 1.0
    CL_LO = float(np.float32(np.exp(np.float64(-160.0) / math.sqrt(96.0))))
    nb80 = P.sb("nb80", [128, 1], F32, st_att)
    P.op("vector", lambda e: e.memset(nb80[:], float(-80.0 / math.sqrt(96.0))),
         writes=("nb80",))

    def prod_items(h):
        hb = h % 2
        qht, kht = qh[hb], kh[hb]
        qres, kres = f"qh{hb}", f"kh{hb}"

        def proj(wb, wres, wbase, wstep, wid, src, sres, rows, dst_fn, dres,
                 half):
            def mm(slot, half=half):
                pst, sbase, pres = slot
                fns = []
                for nqg in range(2):
                    nq = half * 2 + nqg
                    for lk in range(2):
                        fns.append(lambda e, nq=nq, nqg=nqg, lk=lk, pst=pst,
                                   sbase=sbase: e.matmul(
                            pst[:, sbase + nqg * 512:
                                sbase + (nqg + 1) * 512][0:rows, :],
                            wb[:, lk * wstep + wbase:
                               lk * wstep + wbase + wid],
                            src[:, lk * S + nq * 512: lk * S + (nq + 1) * 512],
                            start=(lk == 0), stop=(lk == 1)))
                P.group("tensor", fns, reads=(wres, sres), writes=pres)

            def post(slot, half=half):
                pst, sbase, pres = slot
                P.op("scalar", lambda e, pst=pst, sbase=sbase, half=half:
                     e.activation(dst_fn(half),
                                  pst[:, sbase:sbase + 1024][0:rows, :],
                                  AF.Identity),
                     reads=pres, writes=(dres,))
            return {"mm": mm, "post": post}

        def qc(half):
            return proj(w_uqb, "w_uqb", h * 64, D, 64, c_qb, "c_qb", 64,
                        lambda hf: qht[0:64, hf * 1024:(hf + 1) * 1024],
                        qres, half)

        def qr(half):
            return proj(w_qrb, "w_qrb", h * 32, 512, 32, c_qb, "c_qb", 32,
                        lambda hf: qraw[:, hf * 1024:(hf + 1) * 1024],
                        f"qrw{half}", half)

        def kc_(half):
            return proj(w_ukb, "w_ukb", h * 64, D, 64, c_kvb, "c_kvb", 64,
                        lambda hf: kht[0:64, hf * 1024:(hf + 1) * 1024],
                        kres, half)

        def sw(half):
            hc0, hc1 = half * 1024, (half + 1) * 1024

            def mm(slot):
                pst, sbase, pres = slot
                fns = []
                for nqg in range(2):
                    nq = half * 2 + nqg
                    fns.append(lambda e, nq=nq, nqg=nqg, pst=pst,
                               sbase=sbase: e.matmul(
                        pst[:, sbase + nqg * 512:
                            sbase + (nqg + 1) * 512][0:32, :],
                        swap32[:], qraw[:, nq * 512:(nq + 1) * 512],
                        start=True, stop=True))
                P.group("tensor", fns, reads=(f"qrw{half}", "swap32"),
                        writes=pres)

            def post(slot):
                pst, sbase, pres = slot
                P.op("vector", lambda e: e.tensor_tensor(
                    scr1[:, 0:1024], qraw[:, hc0:hc1], cos32[:, hc0:hc1],
                    ALU.mult),
                    reads=(f"qrw{half}", "cos32"), writes=("rp",))
                P.op("vector", lambda e, pst=pst, sbase=sbase:
                     e.tensor_tensor(
                         qraw[:, hc0:hc1], pst[:, sbase:sbase + 1024][0:32, :],
                         sig32[:, hc0:hc1], ALU.mult),
                     reads=pres + ("sig32", f"qrw{half}"),
                     writes=(f"qrw{half}",))
                P.op("vector", lambda e: e.tensor_tensor(
                    qraw[:, hc0:hc1], scr1[:, 0:1024], qraw[:, hc0:hc1],
                    ALU.add),
                    reads=("rp", f"qrw{half}"), writes=(f"qrw{half}",))
                P.op("gpsimd", lambda e: e.dma_start(
                    out=qht[64:96, hc0:hc1], in_=qraw[:, hc0:hc1]),
                    reads=(f"qrw{half}",), writes=(qres,), dma=True)
            return {"mm": mm, "post": post}

        return [qc(0), qr(0), qc(1), qr(1), sw(0), kc_(0), sw(1), kc_(1)]

    for i, it in enumerate(prod_items(0)):
        slot = (psA, (i % 2) * 1024, (f"psA.s{i % 2}",))
        it["mm"](slot)
        it["post"](slot)

    for h in range(H):
        hb = h % 2
        qht, kht = qh[hb], kh[hb]
        qres, kres = f"qh{hb}", f"kh{hb}"

        def qk_lhsT(kc, kht=kht):
            return kht[:, kc * 128:(kc + 1) * 128]

        def qk_rhs(lo, width, qht=qht):
            return qht[:, lo:lo + width]

        emit_attention_head(
            P, h, qk_lhsT, qk_rhs, (qres, kres, "v_km"),
            v_km, attn, psA, psB, pbuf, causal=True, exp_scale=EXP_SCALE,
            exp_bias=nb80[:], exp_bias_res="nb80", clamp=(CL_LO, CL_HI),
            mask0=mask0, scrA=scrA, rrow=rrow, astage=astage,
            extra_items=prod_items(h + 1) if h + 1 < H else ())

    st_att.close()
    P.full_barrier()
    st_o = ExitStack()
    w_ob = P.sb("w_ob", [128, 8 * D], BF16, st_o)
    P.op("gpsimd", lambda e: e.dma_start(
        out=w_ob[:].rearrange("p (k m) -> p k m", m=D),
        in_=w_o.rearrange("(k p) m -> p k m", p=128)),
        writes=("w_ob",), dma=True)
    pln = PostLN(P, "xs", xs, st_o, "po0",
                 nxt=hw.get("pre_stats", {}).get("lnq"))
    for i in range(NT):
        half = i % 2
        pres = f"psA.s{half}"
        fns = []
        for c in range(NCH):
            for n2 in range(2):
                fns.append(lambda e, i=i, c=c, n2=n2, half=half: e.matmul(
                    psA[:, half * 1024 + n2 * 512: half * 1024 + (n2 + 1) * 512],
                    attn[:, c * S + i * 128: c * S + (i + 1) * 128],
                    w_ob[:, c * D + n2 * 512: c * D + (n2 + 1) * 512],
                    start=(c == 0), stop=(c == 7)))
        P.group("tensor", fns, reads=("attn", "w_ob"), writes=(pres,))
        pln.chunk(i, psA[:, half * 1024:(half + 1) * 1024], pres)
    st_o.close()
    P.full_barrier()
    st_attn.close()
    P.full_barrier()
    st_blk.close()
    P.full_barrier()



def emit_quant_preproc(P, src_h, dst_h, dst_res, rows, cols, scale, st_w,
                       tag, ternary=None, bufs=None):
    if bufs is None:
        wf = [P.sb(f"{tag}_wf{i}", [128, cols], F32, st_w) for i in range(2)]
        wi = [P.sb(f"{tag}_wi{i}", [128, cols], I8, st_w) for i in range(2)]
        wb = [P.sb(f"{tag}_wb{i}", [128, cols], BF16, st_w) for i in range(2)]
    else:
        wf, wi, wb = bufs
    for c in range(rows // 128):
        s = c % 2
        f, ii, b = wf[s], wi[s], wb[s]
        rf, ri, rb = f"qp.wf{s}", f"qp.wi{s}", f"qp.wb{s}"
        P.op("sync", lambda e, c=c, f=f: e.dma_start(
            out=f[:, 0:cols], in_=src_h[c * 128:(c + 1) * 128, :]),
            writes=(rf,), dma=True)
        if ternary is None:
            P.op("vector", lambda e, f=f, ii=ii: e.tensor_scalar(
                ii[:, 0:cols], f[:, 0:cols], float(scale), None, ALU.mult),
                reads=(rf,), writes=(ri,))
            P.op("scalar", lambda e, ii=ii, b=b: e.activation(
                b[:, 0:cols], ii[:, 0:cols], AF.Identity,
                scale=float(1.0 / scale)),
                reads=(ri,), writes=(rb,))
        else:
            ainv_ap, ainv_res = ternary
            P.op("vector", lambda e, f=f, ii=ii: e.tensor_scalar(
                ii[:, 0:cols], f[:, 0:cols], ainv_ap, None, ALU.mult),
                reads=(rf,), sreads=(ainv_res,), writes=(ri,))
            P.op("vector", lambda e, ii=ii, b=b: e.tensor_scalar(
                b[:, 0:cols], ii[:, 0:cols], 1.0, -1.0, ALU.min, ALU.max),
                reads=(ri,), writes=(rb,))
        P.op("scalar", lambda e, c=c, b=b: e.dma_start(
            out=dst_h[c * 128:(c + 1) * 128, :], in_=b[:, 0:cols]),
            reads=(rb,), writes=(dst_res,), dma=True)


def emit_ffn_matmul_phase(P, xT, xT_res, hbuf, w1_hbm, w2_hbm, st_mm,
                          up_evac, down_pre, pln, down_bias=None,
                          down_bias_res=None, phase_out=None,
                          w1_src=None, w2_src=None, quant=None,
                          after_up=None):
    w1band = [P.sb(f"w1b{i}", [128, 8 * 512], BF16, st_mm) for i in range(2)]
    w2band = [P.sb(f"w2b{i}", [128, 2 * D], BF16, st_mm) for i in range(2)]
    wqf = [P.sb(f"wqf{i}", [128, 2048], F32, st_mm) for i in range(3)]
    wqi = [P.sb(f"wqi{i}", [128, 2048], I8, st_mm) for i in range(2)]
    ps_up = P.ps("ps_up", [128, 1024], F32, st_mm)
    ps_dn = [P.ps(f"ps_dn{i}", [128, 1024], F32, st_mm) for i in range(3)]
    if phase_out is not None:
        phase_out["ps_dn0"] = ps_dn[0]
        phase_out["ps_dn1"] = ps_dn[1]

    if isinstance(quant, tuple):
        quant1, quant2 = quant
    else:
        quant1 = quant2 = quant

    def dn_slot(qc):
        if qc < 3:
            return ps_dn[qc], (f"ps_dn{qc}",)
        return ps_up, ("ps_up.0", "ps_up.1")

    def load_w1_band(hg, bs, first):
        if not first:
            P.op("gpsimd", lambda e, hg=hg, bs=bs: e.dma_start(
                out=w1band[bs][:].rearrange("p (k m) -> p k m", m=512),
                in_=w1_hbm[:, hg * 512:(hg + 1) * 512].rearrange(
                    "(k p) m -> p k m", p=128)),
                reads=("w1q_hbm",), writes=(f"w1b{bs}",), dma=True)
            return
        for half in range(2):
            s = (hg * 2 + half) % 3
            si = (hg * 2 + half) % 2
            fr, ir = f"wqf{s}", f"wqi{si}"
            P.op("sync" if half == 0 else "gpsimd",
                 lambda e, hg=hg, half=half, s=s: e.dma_start(
                     out=wqf[s][:].rearrange("p (k m) -> p k m", m=256),
                     in_=w1_src[:, hg * 512 + half * 256:
                                hg * 512 + half * 256 + 256].rearrange(
                         "(k p) m -> p k m", p=128)),
                 writes=(fr,), dma=True)
            quant1(wqf[s][:], fr, wqi[si][:], ir,
                   w1band[bs][:].rearrange("p (k m) -> p k m", m=512)
                   [:, :, half * 256:(half + 1) * 256],
                   f"w1b{bs}")
        P.op("scalar", lambda e, hg=hg, bs=bs: e.dma_start(
            out=w1_hbm[:, hg * 512:(hg + 1) * 512].rearrange(
                "(k p) m -> p k m", p=128),
            in_=w1band[bs][:].rearrange("p (k m) -> p k m", m=512)),
            reads=(f"w1b{bs}",), writes=("w1q_hbm",), dma=True)

    def load_w2_chunk(hg, bs, first):
        if not first:
            P.op("gpsimd", lambda e, hg=hg, bs=bs: e.dma_start(
                out=w2band[bs][:].rearrange("p (k m) -> p k m", m=D),
                in_=w2_hbm[hg * 256:(hg + 1) * 256, :].rearrange(
                    "(k p) m -> p k m", p=128)),
                reads=("w2q_hbm",), writes=(f"w2b{bs}",), dma=True)
            return
        for half in range(2):
            s = (hg * 2 + half) % 3
            si = (hg * 2 + half) % 2
            fr, ir = f"wqf{s}", f"wqi{si}"
            P.op("sync" if half == 0 else "gpsimd",
                 lambda e, hg=hg, half=half, s=s: e.dma_start(
                     out=wqf[s][:, 0:1024],
                     in_=w2_src[hg * 256 + half * 128:
                                hg * 256 + half * 128 + 128, :]),
                 writes=(fr,), dma=True)
            quant2(wqf[s][:, 0:1024], fr, wqi[si][:, 0:1024], ir,
                   w2band[bs][:, half * D:(half + 1) * D], f"w2b{bs}")
        P.op("scalar", lambda e, hg=hg, bs=bs: e.dma_start(
            out=w2_hbm[hg * 256:(hg + 1) * 256, :].rearrange(
                "(k p) m -> p k m", p=128),
            in_=w2band[bs][:].rearrange("p (k m) -> p k m", m=D)),
            reads=(f"w2b{bs}",), writes=("w2q_hbm",), dma=True)

    for jt in range(4):
        for hg in range(8):
            bs = hg % 2
            load_w1_band(hg, bs, jt == 0)
            for hx in range(4):
                hc = hg * 4 + hx
                slot = hc % 2
                pres = f"ps_up.{slot}"
                fns = []
                for k in range(8):
                    fns.append(lambda e, jt=jt, k=k, slot=slot, bs=bs, hx=hx:
                               e.matmul(
                                   ps_up[:, slot * 512:(slot + 1) * 512],
                                   w1band[bs][:, k * 512 + hx * 128:
                                              k * 512 + hx * 128 + 128],
                                   xT[:, k * S + jt * 512:
                                      k * S + jt * 512 + 512],
                                   start=(k == 0), stop=(k == 7)))
                P.group("tensor", fns, reads=(f"w1b{bs}", xT_res),
                        writes=(pres,))
                up_evac(hc, ps_up[:, slot * 512:(slot + 1) * 512], pres)
                if after_up is not None:
                    after_up(hc)
        scale_maker = down_pre(jt) if down_pre is not None else None
        for hg in range(16):
            bs = hg % 2
            load_w2_chunk(hg, bs, jt == 0)
            for hx in range(2):
                hc = hg * 2 + hx
                for qc in range(4):
                    dnp, dres = dn_slot(qc)
                    fns = []
                    for n2 in range(2):
                        fns.append(lambda e, hc=hc, qc=qc, n2=n2, bs=bs,
                                   hx=hx, dnp=dnp: e.matmul(
                                       dnp[:, n2 * 512:(n2 + 1) * 512],
                                       hbuf[:, hc * 512 + qc * 128:
                                            hc * 512 + qc * 128 + 128],
                                       w2band[bs][:, hx * D + n2 * 512:
                                                  hx * D + (n2 + 1) * 512],
                                       start=(hc == 0), stop=(hc == 31)))
                    P.group("tensor", fns, reads=(f"w2b{bs}", f"hbuf.{hc}"),
                            writes=dres)
        for qc in (3, 0, 1, 2):
            i = jt * 4 + qc
            dnp, dres = dn_slot(qc)
            sc = scale_maker(qc) if scale_maker is not None else None
            if sc is not None:
                pln.chunk(i, dnp[:], dres, scale=sc[0], scale_res=sc[1])
            else:
                pln.chunk(i, dnp[:], dres, bias_bcast=down_bias,
                          bias_res=down_bias_res)


def mk_quant64(P):
    def quant(f_ap, f_res, i_ap, i_res, dst_ap, dst_res):
        P.op("vector", lambda e: e.tensor_scalar(
            i_ap, f_ap, 64.0, None, ALU.mult), reads=(f_res,), writes=(i_res,))
        P.op("scalar", lambda e: e.activation(
            dst_ap, i_ap, AF.Identity, scale=1.0 / 64.0),
            reads=(i_res,), writes=(dst_res,))
    return quant


def mk_ternary(P, ainv_ap, ainv_res):
    def quant(f_ap, f_res, i_ap, i_res, dst_ap, dst_res):
        P.op("vector", lambda e: e.tensor_scalar(
            i_ap, f_ap, ainv_ap, None, ALU.mult),
            reads=(f_res,), sreads=(ainv_res,), writes=(i_res,))
        P.op("vector", lambda e: e.tensor_scalar(
            dst_ap, i_ap, 1.0, -1.0, ALU.min, ALU.max),
            reads=(i_res,), writes=(dst_res,))
    return quant


def emit_block_ffn0(P, hw):
    xs = hw["xs"]
    w1t_h, b1_h, w2t_h, b2_h = hw["w1t"], hw["b1"], hw["w2t"], hw["b2"]
    identF = P.consts["identF"]
    w1q_hbm = P.nc.dram_tensor("w1q_scr", [D, DFF], BF16, kind="Internal").ap()
    w2q_hbm = P.nc.dram_tensor("w2q_scr", [DFF, D], BF16, kind="Internal").ap()
    st_blk = ExitStack()
    xqT = P.sb("xqT", [128, NCH * S], BF16, st_blk)
    B16 = P.sb("B16", [128, 32], F32, st_blk)
    B2C = P.sb("B2C", [128, D], BF16, st_blk)



    ps_small_stack = ExitStack()
    ps_small = P.ps("ps_small", [128, 512], F32, ps_small_stack)
    b1r = P.sb("b1r", [32, 128], F32, ps_small_stack)
    b1i = P.sb("b1i", [32, 128], I8, ps_small_stack)
    b1f = P.sb("b1f", [32, 128], F32, ps_small_stack)
    P.op("sync", lambda e: e.dma_start(
        out=b1r[:], in_=b1_h.rearrange("(c p) -> c p", p=128)),
        writes=("b1r",), dma=True)
    P.op("vector", lambda e: e.tensor_scalar(
        b1i[:], b1r[:], 64.0, None, ALU.mult), reads=("b1r",), writes=("b1i",))
    P.op("scalar", lambda e: e.activation(
        b1f[:], b1i[:], AF.Identity, scale=0.25), reads=("b1i",),
        writes=("b1f",))
    P.op("tensor", lambda e: e.transpose(
        ps_small[:, 0:32], b1f[:], identF[0:32, 0:32]),
        reads=("b1f", "c_identF"), writes=("ps_small",))
    P.op("vector", lambda e: e.tensor_copy(B16[:], ps_small[:, 0:32]),
         reads=("ps_small",), writes=("B16",))
    b2r = P.sb("b2r", [1, D], F32, ps_small_stack)
    b2i = P.sb("b2i", [1, D], I8, ps_small_stack)
    b2f = P.sb("b2f", [1, D], F32, ps_small_stack)
    P.op("sync", lambda e: e.dma_start(
        out=b2r[:], in_=b2_h.rearrange("(o d) -> o d", o=1)),
        writes=("b2r",), dma=True)
    P.op("vector", lambda e: e.tensor_scalar(
        b2i[:], b2r[:], 64.0, None, ALU.mult), reads=("b2r",), writes=("b2i",))
    P.op("scalar", lambda e: e.activation(
        b2f[:], b2i[:], AF.Identity, scale=1.0 / 64.0), reads=("b2i",),
        writes=("b2f",))
    for n2 in range(2):
        P.op("tensor", lambda e, n2=n2: e.matmul(
            ps_small[:, 0:512], P.consts["ones_f"][0:1, :],
            b2f[:, n2 * 512:(n2 + 1) * 512], start=True, stop=True),
            reads=("b2f", "c_ones_f"), writes=("ps_small",))
        P.op("vector", lambda e, n2=n2: e.tensor_copy(
            B2C[:, n2 * 512:(n2 + 1) * 512], ps_small[:, 0:512]),
            reads=("ps_small",), writes=("B2C",))
    ps_small_stack.close()
    P.full_barrier()

    st_ln = ExitStack()
    pool = hw.get("pre_stats")
    if pool:
        stats = pool["lnq"]
        emit_preln_stats(P, "xs", xs, stats, pre=True)
    else:
        stats = make_stats(P, st_ln, "lnq")
        emit_preln_stats(P, "xs", xs, stats)
    r16 = stats["r2"]
    P.op("vector", lambda e: e.tensor_scalar(
        r16[:], stats["rstd"][:], 16.0, None, ALU.mult),
        reads=("lnq.rstd",), writes=("lnq.r16",))
    xq_i8 = [P.sb(f"xq_i8{i}", [128, D], I8, st_ln) for i in range(2)]

    def xq_apply(t, dst, dres):
        ii = xq_i8[t % 2]
        ri = f"xqi8.{t % 2}"
        P.op("vector", lambda e, t=t, ii=ii: e.tensor_scalar(
            ii[:], xs[:, t * D:(t + 1) * D], stats["mean"][:, t:t + 1],
            r16[:, t:t + 1], ALU.subtract, ALU.mult),
            reads=("xs",), sreads=("lnq.mean", "lnq.r16"), writes=(ri,))
        P.op("scalar", lambda e, ii=ii, dst=dst: e.activation(
            dst[:], ii[:], AF.Identity, scale=1.0 / 16.0),
            reads=(ri,), writes=(dres,))

    tp_stack = ExitStack()
    emit_ln_transpose(P, xq_apply, xqT, "xqT", tp_stack)
    tp_stack.close()
    P.full_barrier()
    st_ln.close()
    P.full_barrier()

    st_mm = ExitStack()
    hbuf = P.sb("hbuf", [128, 32 * 512], BF16, st_mm)
    hscr = [P.sb(f"hscr{i}", [128, 512], I8, st_mm) for i in range(2)]
    pln = PostLN(P, "xs", xs, st_mm, "poq",
                 nxt=hw.get("pre_stats", {}).get("ln1"))

    def up_evac(hc, psum_ap, psum_res):
        ii = hscr[hc % 2]
        ri = f"hscr{hc % 2}"
        P.op("vector", lambda e, ii=ii, psum_ap=psum_ap, hc=hc:
             e.tensor_scalar(ii[:], psum_ap, 16.0, B16[:, hc:hc + 1],
                             ALU.mult, ALU.add),
             reads=(psum_res,), sreads=("B16",), writes=(ri,))
        P.op("scalar", lambda e, ii=ii, hc=hc: e.activation(
            hbuf[:, hc * 512:(hc + 1) * 512], ii[:], AF.Relu, scale=1.0 / 16.0),
            reads=(ri,), writes=(f"hbuf.{hc}",))

    emit_ffn_matmul_phase(P, xqT, "xqT", hbuf, w1q_hbm, w2q_hbm, st_mm,
                          up_evac, None, pln, down_bias=B2C,
                          down_bias_res="B2C", w1_src=w1t_h, w2_src=w2t_h,
                          quant=mk_quant64(P))
    st_mm.close()
    P.full_barrier()
    st_blk.close()
    P.full_barrier()



def emit_vec_to_partition(P, src_h, src_res, n, dst, dst_res, ps_ap, ps_res,
                          identF, post_scale=1.0, stack=None):
    nch = n // 128
    r = P.sb(dst_res + "_r", [nch, 128], F32, stack)
    P.op("sync", lambda e: e.dma_start(
        out=r[:], in_=src_h.rearrange("(c p) -> c p", p=128)),
        writes=(src_res,), dma=True)
    if post_scale != 1.0:
        P.op("vector", lambda e: e.tensor_scalar(
            r[:], r[:], float(post_scale), None, ALU.mult),
            reads=(src_res,), writes=(src_res,))
    P.op("tensor", lambda e: e.transpose(
        ps_ap[:, 0:nch], r[:], identF[0:nch, 0:nch]),
        reads=(src_res, "c_identF"), writes=(ps_res,))
    P.op("vector", lambda e: e.tensor_copy(dst[:], ps_ap[:, 0:nch]),
         reads=(ps_res,), writes=(dst_res,))


def emit_row_bcast(P, src_h, src_res, dst, dst_res, ps_ap, ps_res, n=D,
                   stack=None):
    row = P.sb(dst_res + "_row", [1, n], F32, stack)
    P.op("sync", lambda e: e.dma_start(
        out=row[:], in_=src_h.rearrange("(o d) -> o d", o=1)),
        writes=(src_res,), dma=True)
    for n2 in range(n // 512):
        P.op("tensor", lambda e, n2=n2: e.matmul(
            ps_ap[:, 0:512], P.consts["ones_f"][0:1, :],
            row[:, n2 * 512:(n2 + 1) * 512], start=True, stop=True),
            reads=(src_res, "c_ones_f"), writes=(ps_res,))
        P.op("vector", lambda e, n2=n2: e.tensor_copy(
            dst[:, n2 * 512:(n2 + 1) * 512], ps_ap[:, 0:512]),
            reads=(ps_res,), writes=(dst_res,))


def emit_block_l1(P, hw):
    xs = hw["xs"]
    qwt_h, kwt_h, vwt_h, owt_h = hw["qwt"], hw["kwt"], hw["vwt"], hw["owt"]
    qb_h, kb_h, vb_h, ob_h = hw["qb"], hw["kb"], hw["vb"], hw["ob"]
    identF = P.consts["identF"]
    st_blk = ExitStack()
    v_km = P.sb("v_km", [128, NT * H * 65], BF16, st_blk)
    P.op("vector", lambda e: e.memset(
        v_km[:].rearrange("p (k c) -> p k c", c=65)[:, :, 64:65], 1.0),
        writes=("v_km.ones",))

    qb8 = P.sb("qb8", [128, 8], F32, st_blk)
    kbp = P.sb("kbp", [128, 8], F32, st_blk)
    pss = ExitStack()
    ps_small = P.ps("ps_small", [128, 512], F32, pss)
    emit_vec_to_partition(P, qb_h, "qb_h", D, qb8, "qb8", ps_small, "ps_small",
                          identF, post_scale=0.125, stack=pss)
    emit_vec_to_partition(P, kb_h, "kb_h", D, kbp, "kbp", ps_small, "ps_small",
                          identF, stack=pss)
    pss.close()
    P.full_barrier()

    st_lnT = ExitStack()
    lnT = P.sb("lnT", [128, NCH * S], BF16, st_lnT)
    st_ln = ExitStack()
    pool = hw.get("pre_stats")
    if pool:
        stats = pool["ln1"]
        emit_preln_stats(P, "xs", xs, stats, pre=True)
    else:
        stats = make_stats(P, st_ln, "ln1")
        emit_preln_stats(P, "xs", xs, stats)
    tp_stack = ExitStack()
    emit_ln_transpose(P, mk_ln_apply(P, "xs", xs, stats), lnT, "lnT", tp_stack)
    tp_stack.close()
    P.full_barrier()
    st_ln.close()
    P.full_barrier()

    psA = P.ps("psA", [128, 2048], F32, st_blk)
    psB = P.ps("psB", [128, 2048], F32, st_blk)
    PSA = ("psA.s0", "psA.s1")
    PSB = tuple(f"psB.b{b}" for b in range(4))

    st_v = ExitStack()
    vwT = P.sb("vwT", [128, 8 * D], BF16, st_v)
    VBC = P.sb("VBC", [128, D], F32, st_v)
    emit_row_bcast(P, vb_h, "vb_h", VBC, "VBC", psB[:, 0:512], "psB.b0",
                   stack=st_v)
    P.op("gpsimd", lambda e: e.dma_start(
        out=vwT[:].rearrange("p (k m) -> p k m", m=D),
        in_=vwt_h.rearrange("(k p) m -> p k m", p=128)),
        writes=("vwT",), dma=True)
    for kc in range(NT):
        fns = []
        for dc in range(NCH):
            for n2 in range(2):
                fns.append(lambda e, kc=kc, dc=dc, n2=n2: e.matmul(
                    psA[:, n2 * 512:(n2 + 1) * 512],
                    lnT[:, dc * S + kc * 128: dc * S + (kc + 1) * 128],
                    vwT[:, dc * D + n2 * 512: dc * D + (n2 + 1) * 512],
                    start=(dc == 0), stop=(dc == 7)))
        P.group("tensor", fns, reads=("lnT", "vwT"), writes=("psA.s0",))
        P.op("vector", lambda e, kc=kc: e.tensor_tensor(
            v_km[:, kc * H * 65:(kc + 1) * H * 65].rearrange(
                "p (h c) -> p h c", c=65)[:, :, 0:64],
            psA[:, 0:1024].rearrange("p (h c) -> p h c", c=64),
            VBC[:].rearrange("p (h c) -> p h c", c=64), ALU.add),
            reads=("psA.s0", "v_km.ones", "VBC"), writes=("v_km",))
        if "alpha" in hw:
            for _ in range(2):
                hw["alpha"]["chunk"]()
    st_v.close()
    P.full_barrier()

    st_att = ExitStack()
    qwb = [P.sb(f"qwb{i}", [128, 8 * 128], BF16, st_att) for i in range(2)]
    kwb = [P.sb(f"kwb{i}", [128, 8 * 128], BF16, st_att) for i in range(2)]
    qpair_b = [P.sb(f"qpair{i}", [128, S], BF16, st_att) for i in range(2)]
    kpair_b = [P.sb(f"kpair{i}", [128, S], BF16, st_att) for i in range(2)]
    st_attn = ExitStack()
    attn = P.sb("attn", [128, NCH * S], BF16, st_attn, side="right")
    astage = P.sb("astage", [64, S], BF16, st_att)
    pbuf = P.sb("pbuf", [128, 2 * 1024], BF16, st_att)
    scrA = (P.sb("scrA0", [64, 512], BF16, st_att),
            P.sb("scrA1", [64, 512], BF16, st_att))
    rrow = [P.sb(f"rrow{i}", [1, 512], F32, st_att) for i in range(2)]

    mm2 = P.sb("mm2", [1, 4], F32, st_att)
    negb = [P.sb(f"negb{i}", [128, 1], F32, st_att) for i in range(2)]

    def pair_loads(j):
        bs = j % 2
        P.op("gpsimd", lambda e, j=j, bs=bs: e.dma_start(
            out=qwb[bs][:].rearrange("p (k m) -> p k m", m=128),
            in_=qwt_h[:, j * 128:(j + 1) * 128].rearrange(
                "(k p) m -> p k m", p=128)),
            writes=(f"qwb{bs}",), dma=True)
        P.op("gpsimd", lambda e, j=j, bs=bs: e.dma_start(
            out=kwb[bs][:].rearrange("p (k m) -> p k m", m=128),
            in_=kwt_h[:, j * 128:(j + 1) * 128].rearrange(
                "(k p) m -> p k m", p=128)),
            writes=(f"kwb{bs}",), dma=True)

    def pair_prod_items(j):
        bs = j % 2

        def proj(wb, wres, dst, dres, sc, bias, bres, half):
            def mm(slot, half=half, wb=wb):
                pst, sbase, pres = slot
                fns = []
                for nqg in range(2):
                    nq = half * 2 + nqg
                    for dc in range(NCH):
                        fns.append(lambda e, nq=nq, nqg=nqg, dc=dc, wb=wb,
                                   pst=pst, sbase=sbase: e.matmul(
                            pst[:, sbase + nqg * 512:
                                sbase + (nqg + 1) * 512],
                            wb[:, dc * 128:(dc + 1) * 128],
                            lnT[:, dc * S + nq * 512: dc * S + (nq + 1) * 512],
                            start=(dc == 0), stop=(dc == 7)))
                P.group("tensor", fns, reads=(wres, "lnT"), writes=pres)

            def post(slot, half=half, dst=dst):
                pst, sbase, pres = slot
                P.op("vector", lambda e, pst=pst, sbase=sbase, half=half:
                     e.tensor_scalar(dst[:, half * 1024:(half + 1) * 1024],
                                     pst[:, sbase:sbase + 1024], sc,
                                     bias[:, j:j + 1], ALU.mult, ALU.add),
                     reads=pres, sreads=(bres,), writes=(dres,))
            return {"mm": mm, "post": post}

        q0 = proj(qwb[bs], f"qwb{bs}", qpair_b[bs], f"qpair{bs}", 0.125,
                  qb8, "qb8", 0)
        q1 = proj(qwb[bs], f"qwb{bs}", qpair_b[bs], f"qpair{bs}", 0.125,
                  qb8, "qb8", 1)
        k0 = proj(kwb[bs], f"kwb{bs}", kpair_b[bs], f"kpair{bs}", 1.0,
                  kbp, "kbp", 0)
        k1 = proj(kwb[bs], f"kwb{bs}", kpair_b[bs], f"kpair{bs}", 1.0,
                  kbp, "kbp", 1)
        return [q0, k0, q1, k1]

    def pair_bounds(j):
        bs = j % 2
        qpair, kpair = qpair_b[bs], kpair_b[bs]
        qpr, kpr = f"qpair{bs}", f"kpair{bs}"
        bsq = pbuf
        BSQR = ("pbuf.s0", "pbuf.s1")
        for (srcp, sres, mcol) in ((qpair, qpr, 0), (kpair, kpr, 1)):
            P.op("vector", lambda e, srcp=srcp:
                 e.tensor_tensor(bsq[:], srcp[:], srcp[:], ALU.mult),
                 reads=(sres,), writes=BSQR)
            fns = []
            for nq in range(4):
                fns.append(lambda e, nq=nq: e.matmul(
                    psB[:, nq * 512:(nq + 1) * 512][0:1, :],
                    P.consts["ones_cb"][:],
                    bsq[:, nq * 512:(nq + 1) * 512],
                    start=True, stop=True))
            P.group("tensor", fns, reads=BSQR + ("c_ones_cb",), writes=PSB)
            P.op("vector", lambda e, mcol=mcol: e.reduce_max(
                mm2[:, mcol:mcol + 1], psB[:, 0:2048][0:1, :], axis=AX.X),
                reads=PSB, writes=("mm2",))
        P.op("vector", lambda e: e.tensor_tensor(
            mm2[:, 2:3], mm2[:, 0:1], mm2[:, 1:2], ALU.mult),
            reads=("mm2",), writes=("mm2",))
        P.op("scalar", lambda e: e.activation(
            mm2[:, 3:4], mm2[:, 2:3], AF.Sqrt),
            reads=("mm2",), writes=("mm2",))
        P.op("tensor", lambda e: e.matmul(
            psA[:, 0:1], P.consts["ones_f"][0:1, :],
            mm2[:, 3:4], start=True, stop=True),
            reads=("mm2", "c_ones_f"), writes=("psA.s0",))
        P.op("vector", lambda e, bs=bs: e.tensor_scalar(
            negb[bs][:], psA[:, 0:1], -1.0, None, ALU.mult),
            reads=("psA.s0",), writes=(f"negb{bs}",))

    pair_loads(0)
    for i, it in enumerate(pair_prod_items(0)):
        slot = (psA, (i % 2) * 1024, (f"psA.s{i % 2}",))
        it["mm"](slot)
        it["post"](slot)
    pair_bounds(0)
    for j in range(8):
        bs = j % 2
        qpair, kpair = qpair_b[bs], kpair_b[bs]
        qpr, kpr = f"qpair{bs}", f"kpair{bs}"
        if "alpha" in hw:
            for _ in range(12 if j < 7 else 14):
                hw["alpha"]["chunk"]()
        if j + 1 < 8:
            pair_loads(j + 1)
        for hh in range(2):
            h = 2 * j + hh
            base = hh * 64

            def qk_lhsT(kc, base=base, kpair=kpair):
                return kpair[base:base + 64, kc * 128:(kc + 1) * 128]

            def qk_rhs(lo, width, base=base, qpair=qpair):
                return qpair[base:base + 64, lo:lo + width]

            extra = pair_prod_items(j + 1) if (hh == 0 and j + 1 < 8) else ()
            emit_attention_head(
                P, h, qk_lhsT, qk_rhs, (qpr, kpr), v_km, attn,
                psA, psB, pbuf, causal=False, exp_scale=1.0,
                exp_bias=negb[bs][:], exp_bias_res=f"negb{bs}",
                clamp=None, mask0=None, scrA=scrA, rrow=rrow, astage=astage,
                extra_items=extra, evac_eng="vector")
            if hh == 0 and j + 1 < 8:
                pair_bounds(j + 1)

    st_att.close()
    P.full_barrier()
    st_lnT.close()
    P.full_barrier()
    st_o = ExitStack()
    owT = P.sb("owT", [128, 8 * D], BF16, st_o)
    P.op("gpsimd", lambda e: e.dma_start(
        out=owT[:].rearrange("p (k m) -> p k m", m=D),
        in_=owt_h.rearrange("(k p) m -> p k m", p=128)),
        writes=("owT",), dma=True)
    OBC = P.sb("OBC", [128, D], F32, st_o)
    emit_row_bcast(P, ob_h, "ob_h", OBC, "OBC", psB[:, 0:512], "psB.b0",
                   stack=st_o)
    pln = PostLN(P, "xs", xs, st_o, "po1",
                 nxt=hw.get("pre_stats", {}).get("lnd"))
    for i in range(NT):
        half = i % 2
        pres = f"psA.s{half}"
        fns = []
        for c in range(NCH):
            for n2 in range(2):
                fns.append(lambda e, i=i, c=c, n2=n2, half=half: e.matmul(
                    psA[:, half * 1024 + n2 * 512: half * 1024 + (n2 + 1) * 512],
                    attn[:, c * S + i * 128: c * S + (i + 1) * 128],
                    owT[:, c * D + n2 * 512: c * D + (n2 + 1) * 512],
                    start=(c == 0), stop=(c == 7)))
        P.group("tensor", fns, reads=("attn", "owT"), writes=(pres,))
        pln.chunk(i, psA[:, half * 1024:(half + 1) * 1024], pres,
                  bias_bcast=OBC, bias_res="OBC")
    st_o.close()
    P.full_barrier()
    st_attn.close()
    P.full_barrier()
    st_blk.close()
    P.full_barrier()



def mk_alpha(P, wupt_h, wdnt_h, st, st_ring):
    ainv_bc = [P.sb(f"ainv_bc{i}", [128, 1], F32, st, side="right")
               for i in range(2)]
    A = P.sb("alphaA", [128, 128], F32, st_ring, side="right")
    ring = [P.sb(f"al_wf{i}", [128, 512], F32, st_ring, side="right")
            for i in range(2)]
    idx = [0]

    def emit_abs(i):
        s = i % 2
        f = ring[s]
        rf = f"al_wf{s}"
        P.op("scalar", lambda e, f=f, i=i: e.activation(
            f[:], f[:], AF.Abs, accum_out=A[:, i:i + 1]),
            reads=(rf, "alphaA"), writes=(rf, "alphaA"))

    def emit_dma(i):
        s = i % 2
        f = ring[s]
        rf = f"al_wf{s}"
        if i < 64:
            c, q = divmod(i, 8)
            src = wupt_h[c * 128:(c + 1) * 128, q * 512:(q + 1) * 512]
        else:
            c, q = divmod(i - 64, 2)
            src = wdnt_h[c * 128:(c + 1) * 128, q * 512:(q + 1) * 512]
        P.op("scalar", lambda e, f=f, src=src: e.dma_start(
            out=f[:], in_=src), writes=(rf,), dma=True, dmaq="AL")

    def chunk():
        i = idx[0]
        if i >= 130:
            return
        idx[0] += 1
        if i >= 2:
            emit_abs(i - 2)
        if i < 128:
            emit_dma(i)

    return {"A": A, "ainv_bc": ainv_bc, "chunk": chunk, "idx": idx,
            "st_ring": st_ring}


def emit_block_ffn1(P, hw):
    xs = hw["xs"]
    wupt_h, wdnt_h = hw["wupt"], hw["wdnt"]
    identF = P.consts["identF"]
    wupq_hbm = P.nc.dram_tensor("wupq_scr", [D, DFF], BF16,
                                kind="Internal").ap()
    wdnq_hbm = P.nc.dram_tensor("wdnq_scr", [DFF, D], BF16,
                                kind="Internal").ap()
    r2_hbm = P.nc.dram_tensor("r2_scr", [512], F32, kind="Internal").ap()

    al = hw["alpha"]
    while al["idx"][0] < 130:
        al["chunk"]()
    ainv_bc = al["ainv_bc"]
    A = al["A"]
    st_fin = ExitStack()
    arow = P.sb("arow", [1, 128], F32, st_fin)
    atot = P.sb("atot", [1, 4], F32, st_fin)
    ps_a = P.ps("ps_a", [128, 512], F32, st_fin)

    def alpha_finish(wi, lo, n, rows, cols):
        P.group("tensor", [lambda e, lo=lo, n=n: e.matmul(
            ps_a[:, 0:n][0:1, :], P.consts["ones_cf"][:], A[:, lo:lo + n],
            start=True, stop=True)],
            reads=("alphaA", "c_ones_cf"), writes=("ps_a",))
        P.op("vector", lambda e, lo=lo, n=n: e.tensor_copy(
            arow[:, lo:lo + n], ps_a[:, 0:n][0:1, :]),
            reads=("ps_a",), writes=(f"arow{wi}",))
        P.op("vector", lambda e, lo=lo, n=n, wi=wi: e.reduce_sum(
            atot[:, wi:wi + 1], arow[:, lo:lo + n], axis=AX.X),
            reads=(f"arow{wi}",), writes=(f"atot{wi}",))
        P.op("vector", lambda e, wi=wi: e.reciprocal(
            atot[:, 2 + wi:3 + wi], atot[:, wi:wi + 1]),
            reads=(f"atot{wi}",), writes=(f"arcp{wi}",))
        P.op("vector", lambda e, wi=wi, rows=rows, cols=cols: e.tensor_scalar(
            atot[:, 2 + wi:3 + wi], atot[:, 2 + wi:3 + wi],
            float(rows * cols), None, ALU.mult),
            reads=(f"arcp{wi}",), writes=(f"arcp{wi}",))
        P.op("tensor", lambda e, wi=wi: e.matmul(
            ps_a[:, 0:1], P.consts["ones_f"][0:1, :], atot[:, 2 + wi:3 + wi],
            start=True, stop=True),
            reads=(f"arcp{wi}", "c_ones_f"), writes=("ps_a",))
        P.op("vector", lambda e, wi=wi: e.tensor_copy(
            ainv_bc[wi][:], ps_a[:, 0:1]),
            reads=("ps_a",), writes=(f"ainv_bc{wi}",))

    alpha_finish(0, 0, 64, D, DFF)
    alpha_finish(1, 64, 64, DFF, D)
    st_fin.close()
    al["st_ring"].close()
    P.full_barrier()

    st_blk = ExitStack()
    xnT = P.sb("xnT", [128, NCH * S], BF16, st_blk)

    st_ln = ExitStack()
    pool = hw.get("pre_stats")
    if pool:
        stats = pool["lnd"]
        emit_preln_stats(P, "xs", xs, stats, pre=True)
    else:
        stats = make_stats(P, st_ln, "lnd")
        emit_preln_stats(P, "xs", xs, stats)
    sv = P.sb("lnd_sv", [128, NT], F32, st_ln)
    scl = P.sb("lnd_scl", [128, NT], F32, st_ln)
    nms = P.sb("lnd_nms", [128, NT], F32, st_ln)
    P.op("scalar", lambda e: e.activation(sv[:], stats["var"][:], AF.Sqrt),
         reads=("lnd.var",), writes=("lnd.sv",))
    P.op("vector", lambda e: e.tensor_tensor(
        sv[:], sv[:], stats["rstd"][:], ALU.mult),
        reads=("lnd.sv", "lnd.rstd"), writes=("lnd.rms",))
    P.op("vector", lambda e: e.tensor_scalar(
        sv[:], sv[:], float(EPS_RMS), None, ALU.add),
        reads=("lnd.rms",), writes=("lnd.d2",))
    P.op("vector", lambda e: e.reciprocal(scl[:], sv[:]),
         reads=("lnd.d2",), writes=("lnd.f",))
    P.op("vector", lambda e: e.tensor_tensor(
        scl[:], stats["rstd"][:], scl[:], ALU.mult),
        reads=("lnd.f", "lnd.rstd"), writes=("lnd.scl",))
    P.op("vector", lambda e: e.tensor_tensor(
        nms[:], stats["mean"][:], scl[:], ALU.mult),
        reads=("lnd.scl", "lnd.mean"), writes=("lnd.nms",))
    P.op("vector", lambda e: e.tensor_scalar(
        nms[:], nms[:], -1.0, None, ALU.mult),
        reads=("lnd.nms",), writes=("lnd.nms",))
    def xn_apply(t, dst, dres):
        P.op("scalar", lambda e, t=t, dst=dst: e.activation(
            dst[:], xs[:, t * D:(t + 1) * D], AF.Identity,
            bias=nms[:, t:t + 1], scale=scl[:, t:t + 1]),
            reads=("xs",), sreads=("lnd.nms", "lnd.scl"), writes=(dres,))

    tp_stack = ExitStack()
    emit_ln_transpose(P, xn_apply, xnT, "xnT", tp_stack)
    tp_stack.close()
    P.full_barrier()
    st_ln.close()
    P.full_barrier()

    st_mm = ExitStack()
    hbuf = P.sb("hbuf", [128, 32 * 512], BF16, st_mm)
    sqd = [P.sb(f"sqd{i}", [128, 512], F32, st_mm) for i in range(2)]
    r2row = P.sb("r2row", [1, 512], F32, st_mm)
    rstd2T = P.sb("rstd2T", [128, 4], F32, st_mm)
    pln = PostLN(P, "xs", xs, st_mm, "pod")

    def up_evac(hc, psum_ap, psum_res):
        P.op("scalar", lambda e, psum_ap=psum_ap, hc=hc: e.activation(
            hbuf[:, hc * 512:(hc + 1) * 512], psum_ap, AF.Gelu),
            reads=(psum_res,), writes=(f"hbuf.{hc}",))

    phase = {}

    def rms_mm(hc):
        ps_dn0 = phase["ps_dn0"]
        sq = sqd[hc % 2]
        rq = f"sqd{hc % 2}"
        P.op("tensor", lambda e, sq=sq, hc=hc, ps_dn0=ps_dn0: e.matmul(
            ps_dn0[:, 0:512][0:1, :], P.consts["ones_cf"][:], sq[:],
            start=(hc == 0), stop=(hc == 31)),
            reads=(rq, "c_ones_cf"), writes=("ps_dn0",))

    def after_up(hc):
        if hc > 0:
            rms_mm(hc - 1)
        sq = sqd[hc % 2]
        rq = f"sqd{hc % 2}"
        P.op("scalar", lambda e, sq=sq, hc=hc: e.activation(
            sq[:], hbuf[:, hc * 512:(hc + 1) * 512], AF.Square),
            reads=(f"hbuf.{hc}",), writes=(rq,))

    def down_pre(jt, _ps=[None]):
        rms_mm(31)
        ps_dn0 = phase["ps_dn0"]
        P.op("vector", lambda e: e.tensor_scalar(
            r2row[:], ps_dn0[:, 0:512][0:1, :], float(1.0 / DFF), None,
            ALU.mult), reads=("ps_dn0",), writes=("r2row",))
        P.op("scalar", lambda e: e.activation(
            r2row[:], r2row[:], AF.Sqrt), reads=("r2row",), writes=("r2row",))
        P.op("vector", lambda e: e.tensor_scalar(
            r2row[:], r2row[:], float(EPS_RMS), None, ALU.add),
            reads=("r2row",), writes=("r2row",))
        P.op("vector", lambda e: e.reciprocal(r2row[:], r2row[:]),
             reads=("r2row",), writes=("r2row",))

        for qc in range(4):
            P.op("tensor", lambda e, qc=qc: e.matmul(
                phase["ps_dn1"][:, 0:1], r2row[0:1, qc * 128:(qc + 1) * 128],
                P.consts["ones_f"][0:1, 0:1], start=True, stop=True),
                reads=("r2row", "c_ones_f"), writes=("ps_dn1",))
            P.op("vector", lambda e, qc=qc: e.tensor_copy(
                rstd2T[:, qc:qc + 1], phase["ps_dn1"][:, 0:1]),
                reads=("ps_dn1",), writes=("rstd2T",))
        return lambda qc: (rstd2T[:, qc:qc + 1], "rstd2T")

    emit_ffn_matmul_phase(P, xnT, "xnT", hbuf, wupq_hbm, wdnq_hbm, st_mm,
                          up_evac, down_pre, pln, phase_out=phase,
                          w1_src=wupt_h, w2_src=wdnt_h,
                          quant=(mk_ternary(P, ainv_bc[0][:], "ainv_bc0"),
                                 mk_ternary(P, ainv_bc[1][:], "ainv_bc1")),
                          after_up=after_up)
    st_mm.close()
    P.full_barrier()
    st_blk.close()
    P.full_barrier()



def build_fused():
    P = Prog()
    hw = {}
    hw["x"] = P.dram_in("x", [S, D], F32)
    for n, shp in (("w_dq", [D, DLAT]), ("w_uq", [DLAT, D]),
                   ("w_qr", [DLAT, 512]), ("w_dkv", [D, DLAT]),
                   ("w_uk", [DLAT, D]), ("w_uv", [DLAT, D]),
                   ("w_kr", [D, DHR]), ("w_o", [D, D]),
                   ("w1t", [D, DFF]), ("b1", [DFF]), ("w2t", [DFF, D]),
                   ("b2", [D]), ("qwt", [D, D]), ("kwt", [D, D]),
                   ("vwt", [D, D]), ("owt", [D, D]), ("qb", [D]), ("kb", [D]),
                   ("vb", [D]), ("ob", [D]), ("wupt", [D, DFF]),
                   ("wdnt", [DFF, D]), ("identF", [128, 128])):
        hw[n] = P.dram_in(n, shp, F32)
    for n, shp in (("identB", [128, 128]), ("cos32", [32, S]),
                   ("sig32", [32, S]), ("swap32", [32, 32]),
                   ("mask0", [128, 128])):
        hw[n] = P.dram_in(n, shp, BF16)
    out_h = P.dram_out("xout", [S, D], F32)

    identB = P.sb("s_identB", [128, 128], BF16)
    identF = P.sb("s_identF", [128, 128], F32)
    P.consts["identB"] = identB
    P.consts["identF"] = identF
    emit_consts(P, [("eps_ln", [128, 1], F32, EPS_LN),
                    ("ones_f", [1, 128], F32, 1.0),
                    ("ones_cb", [128, 1], BF16, 1.0),
                    ("ones_cf", [128, 1], F32, 1.0)])
    P.op("sync", lambda e: e.dma_start(out=identB[:], in_=hw["identB"]),
         writes=("c_identB",), dma=True)
    P.op("sync", lambda e: e.dma_start(out=identF[:], in_=hw["identF"]),
         writes=("c_identF",), dma=True)

    xs = P.sb("xs", [128, NT * D], F32, side="right")
    hw["xs"] = xs
    hw["pre_stats"] = {t: make_stats(P, None, t, with_scratch=False)
                       for t in ("lnq", "ln1", "lnd")}
    emit_load_x(P, hw["x"], xs)

    emit_block_l0(P, hw)
    emit_block_ffn0(P, hw)
    st_alpha = ExitStack()
    st_ring = ExitStack()
    hw["alpha"] = mk_alpha(P, hw["wupt"], hw["wdnt"], st_alpha, st_ring)
    emit_block_l1(P, hw)
    emit_block_ffn1(P, hw)
    st_alpha.close()
    P.full_barrier(include_bg=True)
    emit_store_x(P, "xs", xs, out_h)
    return P.finalize()


_BUILD_CACHE = {}


def _get(builder, name):
    if name not in _BUILD_CACHE:
        _BUILD_CACHE[name] = builder()
    return _BUILD_CACHE[name]


def kernel(**inputs):
    f32 = lambda a: np.ascontiguousarray(np.asarray(a, dtype=np.float32))
    x = f32(inputs["L0_attn_x"])
    _ = inputs.get("L1_attn_mask")
    cos32, sig32, swap32 = _rope_tables()
    identB = np.ascontiguousarray(np.eye(128, dtype=BF))
    identF = np.ascontiguousarray(np.eye(128, dtype=np.float32))
    mask0 = _mask_tri()

    w_qr = f32(inputs["L0_attn_W_QR"])[:, _qr_perm()]
    w_kr = f32(inputs["L0_attn_W_KR"])[:, _kr_perm()]

    common = dict(
        w_dq=f32(inputs["L0_attn_W_DQ"]), w_uq=f32(inputs["L0_attn_W_UQ"]),
        w_qr=np.ascontiguousarray(w_qr), w_dkv=f32(inputs["L0_attn_W_DKV"]),
        w_uk=f32(inputs["L0_attn_W_UK"]), w_uv=f32(inputs["L0_attn_W_UV"]),
        w_kr=np.ascontiguousarray(w_kr), w_o=f32(inputs["L0_attn_W_O"]),
        identB=identB, cos32=cos32, sig32=sig32, swap32=swap32, mask0=mask0,
        w1t=np.ascontiguousarray(f32(inputs["L0_ffn_w1"]).T),
        b1=f32(inputs["L0_ffn_b1"]),
        w2t=np.ascontiguousarray(f32(inputs["L0_ffn_w2"]).T),
        b2=f32(inputs["L0_ffn_b2"]), identF=identF,
        qwt=np.ascontiguousarray(f32(inputs["L1_attn_q_w"]).T),
        kwt=np.ascontiguousarray(f32(inputs["L1_attn_k_w"]).T),
        vwt=np.ascontiguousarray(f32(inputs["L1_attn_v_w"]).T),
        owt=np.ascontiguousarray(f32(inputs["L1_attn_out_w"]).T),
        qb=f32(inputs["L1_attn_q_b"]), kb=f32(inputs["L1_attn_k_b"]),
        vb=f32(inputs["L1_attn_v_b"]), ob=f32(inputs["L1_attn_out_b"]),
        wupt=np.ascontiguousarray(f32(inputs["L1_ffn_w_up"]).T),
        wdnt=np.ascontiguousarray(f32(inputs["L1_ffn_w_down"]).T))

    nc = _get(build_fused, "fused")
    in_maps = [dict(x=np.ascontiguousarray(x[b]), **common) for b in range(8)]
    res = run_bass_kernel_spmd(nc, in_maps, core_ids=list(range(8)))
    out = np.stack([r["xout"] for r in res.results], axis=0)
    return np.ascontiguousarray(out.astype(np.float32))


# version 23
# speedup vs baseline: 1.0057x; 1.0057x over previous
import math
from contextlib import ExitStack

import numpy as np
import ml_dtypes

import concourse.bass as bass
import concourse.mybir as mybir
from concourse.bass_utils import run_bass_kernel_spmd

F32 = mybir.dt.float32
BF16 = mybir.dt.bfloat16
I8 = mybir.dt.int8
AF = mybir.ActivationFunctionType
ALU = mybir.AluOpType
AX = mybir.AxisListType

S = 2048
D = 1024
NT = 16
NCH = 8
H = 16
DHR = 32
DLAT = 256
DFF = 4096
EPS_LN = 1e-5
EPS_RMS = 1e-8
BF = ml_dtypes.bfloat16

ENGS = ("sync", "gpsimd", "tensor", "vector", "scalar")
QSEM = {"sync": "LD", "gpsimd": "G", "tensor": "T", "vector": "V",
        "scalar": "Sc"}


class Prog:
    def __init__(self):
        self.nc = bass.Bass("TRN2", target_bir_lowering=False)
        self.ops = {e: [] for e in ENGS}
        self.ctx = ExitStack()
        self.sems = {}
        for name in ("T", "V", "Sc", "G"):
            self.sems[name] = [self.ctx.enter_context(self.nc.semaphore(name)), 0]
        self.NSLOT = 4
        for q in ("LD", "GD", "ST", "AL"):
            for i in range(self.NSLOT):
                n = f"{q}{i}"
                self.sems[n] = [self.ctx.enter_context(self.nc.semaphore(n)), 0]
        self.dma_idx = {"LD": 0, "GD": 0, "ST": 0, "AL": 0}
        self.consts = {}
        self.suppress_same_engine = False
        self._names = {}
        self.last_write = {}
        self.readers = {}

    def _uniq(self, name):
        n = self._names.get(name, 0)
        self._names[name] = n + 1
        return name if n == 0 else f"{name}_{n}"

    def sb(self, name, shape, dt, stack=None, side=None):
        return (stack or self.ctx).enter_context(
            self.nc.sbuf_tensor(self._uniq(name), shape, dt, side=side))

    def ps(self, name, shape, dt, stack=None):
        return (stack or self.ctx).enter_context(
            self.nc.psum_tensor(self._uniq(name), shape, dt))

    def dram_in(self, name, shape, dt):
        return self.nc.dram_tensor(name, shape, dt, kind="ExternalInput").ap()

    def dram_out(self, name, shape, dt):
        return self.nc.dram_tensor(name, shape, dt, kind="ExternalOutput").ap()

    def _compute_waits(self, mysem, reads, writes, sreads):
        need = {}

        def add(dep):
            if dep is None:
                return
            sn, v = dep
            if need.get(sn, 0) < v:
                need[sn] = v

        sup = getattr(self, "suppress_same_engine", True)
        for r in tuple(reads) + tuple(writes):
            lw = self.last_write.get(r)
            if lw is not None and (not sup or lw[0] != mysem):
                add(lw)
        for r in tuple(sreads):
            add(self.last_write.get(r))
        for r in tuple(writes):
            for sn, v in self.readers.get(r, {}).items():
                if not sup or sn != mysem:
                    add((sn, v))
        return sorted(need.items())

    def _record(self, mysem, count, reads, writes, sreads):
        for r in tuple(reads) + tuple(sreads):
            self.readers.setdefault(r, {})[mysem] = count
        for r in tuple(writes):
            self.last_write[r] = (mysem, count)
            self.readers[r] = {}

    def op(self, eng, f, reads=(), writes=(), sreads=(), dma=False,
           extra=(), dmaq=None):
        if dma:
            q = dmaq or {"sync": "LD", "gpsimd": "GD", "scalar": "ST"}[eng]
            slot = self.dma_idx[q] % self.NSLOT
            self.dma_idx[q] += 1
            sem = f"{q}{slot}"
            incv = 16
            gate = (sem, self.sems[sem][1])
        else:
            sem = QSEM[eng]
            incv = 1
            gate = None
        waits = self._compute_waits(sem if not dma else "__dma__",
                                    reads, writes, sreads)
        if gate is not None and gate[1] > 0:
            waits = list(waits) + [gate]
        for sn, v in extra:
            waits = list(waits) + [(sn, v)]
        self.sems[sem][1] += incv
        count = self.sems[sem][1]
        sems = self.sems

        def run(e, f=f, waits=tuple(waits), sem=sem, incv=incv):
            for sn, v in waits:
                e.wait_ge(sems[sn][0], v)
            f(e).then_inc(sems[sem][0], incv)

        self.ops[eng].append(run)
        self._record(sem, count, reads, writes, sreads)
        return (sem, count)

    def group(self, eng, fns, reads=(), writes=(), sreads=()):
        sem = QSEM[eng]
        waits = self._compute_waits(sem, reads, writes, sreads)
        self.sems[sem][1] += 1
        count = self.sems[sem][1]
        sems = self.sems

        def run(e, fns=tuple(fns), waits=tuple(waits), sem=sem):
            for sn, v in waits:
                e.wait_ge(sems[sn][0], v)
            for f in fns[:-1]:
                f(e)
            fns[-1](e).then_inc(sems[sem][0], 1)

        self.ops[eng].append(run)
        self._record(sem, count, reads, writes, sreads)
        return (sem, count)

    def barrier_wait(self, eng, deps):
        sems = self.sems
        wl = tuple(d for d in deps if d is not None)

        def run(e, wl=wl):
            for sn, v in wl:
                e.wait_ge(sems[sn][0], v)

        self.ops[eng].append(run)

    def dep(self, res):
        return self.last_write.get(res)

    def full_barrier(self, include_bg=False):
        deps = [(n, s[1]) for n, s in self.sems.items()
                if s[1] > 0 and (include_bg or not n.startswith("AL"))]
        for eng in ENGS:
            self.barrier_wait(eng, deps)

    def finalize(self):
        with self.nc.Block() as block:
            for eng in ENGS:
                fns = self.ops[eng]
                getattr(block, eng)(lambda e, fns=fns: [f(e) for f in fns])
        self.ctx.close()
        return self.nc



def _rope_tables():
    j = np.arange(16, dtype=np.float64)
    inv = 10000.0 ** (-j / 16.0)
    t = np.arange(S, dtype=np.float64)
    ang = inv[:, None] * t[None, :]
    cos = np.cos(ang)
    sin = np.sin(ang)
    cos32 = np.ascontiguousarray(np.concatenate([cos, cos], 0).astype(BF))
    sig32 = np.ascontiguousarray(np.concatenate([-sin, sin], 0).astype(BF))
    P = np.zeros((32, 32), dtype=BF)
    for k in range(16):
        P[16 + k, k] = 1.0
        P[k, 16 + k] = 1.0
    return cos32, sig32, P


def _qr_perm():
    perm = []
    for h in range(H):
        perm += [h * 32 + 2 * k for k in range(16)]
        perm += [h * 32 + 2 * k + 1 for k in range(16)]
    return np.array(perm)


def _kr_perm():
    return np.array([2 * k for k in range(16)] + [2 * k + 1 for k in range(16)])


def _mask_tri():
    k = np.arange(128)[:, None]
    q = np.arange(128)[None, :]
    return np.ascontiguousarray((q >= k).astype(BF))


def _pieces(lo, hi, grid=512):
    out = []
    while lo < hi:
        n = min(hi - lo, grid - (lo % grid))
        out.append((lo, n))
        lo += n
    return out



def emit_consts(P, specs):
    for name, shape, dt, valv in specs:
        t = P.sb("c_" + name, shape, dt)
        P.op("vector", lambda e, t=t, v=valv: e.memset(t[:], v),
             writes=("c_" + name,))
        P.consts[name] = t


def emit_load_x(P, x_h, xs, res="xs"):
    P.op("sync", lambda e: e.dma_start(
        out=xs[:].rearrange("p (t d) -> p t d", d=D),
        in_=x_h.rearrange("(t p) d -> p t d", p=128)),
        writes=(res,), dma=True)


def make_stats(P, stack, tag, with_scratch=True):
    d = {"tag": tag}
    for n in ("sum", "ssq", "mean", "var", "m2", "sqv", "rstd", "nmr", "r2"):
        d[n] = P.sb(f"{tag}_{n}", [128, NT], F32, stack)
    if with_scratch:
        d["sq0"] = P.sb(f"{tag}_sq0", [128, D], F32, stack)
        d["sq1"] = P.sb(f"{tag}_sq1", [128, D], F32, stack)
    return d


def emit_preln_stats(P, xs_res, xs, st, pre=False):
    tg = st["tag"]
    if not pre:
        P.op("vector", lambda e: e.reduce_sum(
            st["sum"][:], xs[:].rearrange("p (t d) -> p t d", d=D), axis=AX.X),
            reads=(xs_res,), writes=(tg + ".sum",))
        for t in range(NT):
            scr = st["sq0"] if t % 2 == 0 else st["sq1"]
            r = tg + (".sq0" if t % 2 == 0 else ".sq1")
            P.op("scalar", lambda e, t=t, scr=scr: e.activation(
                scr[:], xs[:, t * D:(t + 1) * D], AF.Square),
                reads=(xs_res,), writes=(r,))
            P.op("vector", lambda e, t=t, scr=scr: e.reduce_sum(
                st["ssq"][:, t:t + 1], scr[:], axis=AX.X),
                reads=(r,), writes=(tg + f".ssq{t}",))
    P.op("vector", lambda e: e.tensor_scalar(
        st["mean"][:], st["sum"][:], 1.0 / D, None, ALU.mult),
        reads=(tg + ".sum",), writes=(tg + ".mean",))
    P.op("vector", lambda e: e.tensor_scalar(
        st["var"][:], st["ssq"][:], 1.0 / D, None, ALU.mult),
        reads=tuple(tg + f".ssq{t}" for t in range(NT)),
        writes=(tg + ".var",))
    P.op("vector", lambda e: e.tensor_tensor(
        st["m2"][:], st["mean"][:], st["mean"][:], ALU.mult),
        reads=(tg + ".mean",), writes=(tg + ".m2",))
    P.op("vector", lambda e: e.tensor_tensor(
        st["var"][:], st["var"][:], st["m2"][:], ALU.subtract),
        reads=(tg + ".var", tg + ".m2"), writes=(tg + ".var",))
    P.op("scalar", lambda e: e.activation(
        st["sqv"][:], st["var"][:], AF.Sqrt, bias=P.consts["eps_ln"][:]),
        reads=(tg + ".var",), sreads=("c_eps_ln",), writes=(tg + ".sqv",))
    P.op("vector", lambda e: e.reciprocal(st["rstd"][:], st["sqv"][:]),
         reads=(tg + ".sqv",), writes=(tg + ".rstd",))
    P.op("vector", lambda e: e.tensor_tensor(
        st["nmr"][:], st["mean"][:], st["rstd"][:], ALU.mult),
        reads=(tg + ".mean", tg + ".rstd"), writes=(tg + ".nmr",))
    P.op("vector", lambda e: e.tensor_scalar(
        st["nmr"][:], st["nmr"][:], -1.0, None, ALU.mult),
        reads=(tg + ".nmr",), writes=(tg + ".nmr",))


def mk_ln_apply(P, xs_res, xs, st, bias_key="nmr", scale_key="rstd"):
    tg = st["tag"]

    def apply_tile(t, dst, dres):
        P.op("scalar", lambda e, t=t, dst=dst: e.activation(
            dst[:], xs[:, t * D:(t + 1) * D], AF.Identity,
            bias=st[bias_key][:, t:t + 1], scale=st[scale_key][:, t:t + 1]),
            reads=(xs_res,), sreads=(tg + "." + bias_key, tg + "." + scale_key),
            writes=(dres,))

    return apply_tile


def emit_ln_transpose(P, apply_tile, fmT, fm_res, tp_stack):
    pt = [P.ps(f"tp_pt{i}", [128, 1024], BF16, tp_stack) for i in range(2)]
    lnr = [P.sb(f"tp_ln{i}", [128, D], BF16, tp_stack) for i in range(2)]
    identB = P.consts["identB"]
    for t in range(NT):
        s = t % 2
        lt, lres = lnr[s], f"tp_ln{s}"
        pp, pres = pt[s], f"tp_pt{s}"
        apply_tile(t, lt, lres)
        fns = []
        for c in range(NCH):
            fns.append(lambda e, c=c, lt=lt, pp=pp: e.transpose(
                pp[:, c * 128:(c + 1) * 128],
                lt[:, c * 128:(c + 1) * 128], identB[:]))
        P.group("tensor", fns, reads=(lres, "c_identB"), writes=(pres,))
        P.op("vector", lambda e, t=t, pp=pp: e.tensor_copy(
            fmT[:].rearrange("p (c k) -> p c k", k=S)[:, :, t * 128:(t + 1) * 128],
            pp[:].rearrange("p (c k) -> p c k", k=128)),
            reads=(pres,), writes=(fm_res,))


def emit_store_x(P, xs_res, xs, out_h):
    P.op("scalar", lambda e: e.dma_start(
        out=out_h.rearrange("(t p) d -> p t d", p=128),
        in_=xs[:].rearrange("p (t d) -> p t d", d=D)),
        reads=(xs_res,), dma=True)
    dep = P.dep_st()
    P.barrier_wait("sync", dep)
    P.barrier_wait("tensor", dep)
    P.barrier_wait("vector", dep)


def _dep_st(self):
    return [(f"ST{i}", self.sems[f"ST{i}"][1]) for i in range(self.NSLOT)
            if self.sems[f"ST{i}"][1] > 0]


Prog.dep_st = _dep_st


class PostLN:

    def __init__(self, P, xs_res, xs, stack, tag, nxt=None):
        self.P, self.xs, self.xs_res = P, xs, xs_res
        self.st = make_stats(P, stack, tag)
        self.o = [P.sb(tag + "_o0", [128, D], F32, stack),
                  P.sb(tag + "_o1", [128, D], F32, stack)]
        self.tag = tag
        self.nxt = nxt

    def chunk(self, i, psum_ap, psum_res, bias_bcast=None, bias_res=None,
              scale=None, scale_res=None):
        P, st, tg = self.P, self.st, self.tag
        if isinstance(psum_res, str):
            psum_res = (psum_res,)
        o = self.o[i % 2]
        ores = f"{tg}.o{i % 2}"
        if bias_bcast is not None:
            P.op("vector", lambda e: e.tensor_tensor(
                o[:], psum_ap, bias_bcast[:], ALU.add),
                reads=psum_res + (bias_res,), writes=(ores,))
        elif scale is not None:
            P.op("scalar", lambda e: e.activation(
                o[:], psum_ap, AF.Identity, scale=scale),
                reads=psum_res, sreads=(scale_res,), writes=(ores,))
        else:
            P.op("vector", lambda e: e.tensor_copy(o[:], psum_ap),
                 reads=psum_res, writes=(ores,))
        P.op("vector", lambda e, i=i: e.reduce_sum(
            st["sum"][:, i:i + 1], o[:], axis=AX.X),
            reads=(ores,), writes=(tg + ".sum",))
        P.op("scalar", lambda e: e.activation(
            st["sq0"][:], o[:], AF.Square),
            reads=(ores,), writes=(tg + ".sq0",))
        P.op("vector", lambda e, i=i: e.reduce_sum(
            st["ssq"][:, i:i + 1], st["sq0"][:], axis=AX.X),
            reads=(tg + ".sq0",), writes=(tg + ".ssq",))
        P.op("vector", lambda e, i=i: e.tensor_scalar(
            st["mean"][:, i:i + 1], st["sum"][:, i:i + 1], 1.0 / D, None,
            ALU.mult), reads=(tg + ".sum",), writes=(tg + ".mean",))
        P.op("vector", lambda e, i=i: e.tensor_scalar(
            st["var"][:, i:i + 1], st["ssq"][:, i:i + 1], 1.0 / D, None,
            ALU.mult), reads=(tg + ".ssq",), writes=(tg + ".var",))
        P.op("vector", lambda e, i=i: e.tensor_tensor(
            st["m2"][:, i:i + 1], st["mean"][:, i:i + 1],
            st["mean"][:, i:i + 1], ALU.mult),
            reads=(tg + ".mean",), writes=(tg + ".m2",))
        P.op("vector", lambda e, i=i: e.tensor_tensor(
            st["var"][:, i:i + 1], st["var"][:, i:i + 1],
            st["m2"][:, i:i + 1], ALU.subtract),
            reads=(tg + ".var", tg + ".m2"), writes=(tg + ".var",))
        P.op("scalar", lambda e, i=i: e.activation(
            st["sqv"][:, i:i + 1], st["var"][:, i:i + 1], AF.Sqrt,
            bias=P.consts["eps_ln"][:]),
            reads=(tg + ".var",), sreads=("c_eps_ln",), writes=(tg + ".sqv",))
        P.op("vector", lambda e, i=i: e.reciprocal(
            st["rstd"][:, i:i + 1], st["sqv"][:, i:i + 1]),
            reads=(tg + ".sqv",), writes=(tg + ".rstd",))
        P.op("vector", lambda e, o=o, i=i: e.tensor_scalar(
            st["sq1"][:], o[:], st["mean"][:, i:i + 1],
            st["rstd"][:, i:i + 1], ALU.subtract, ALU.mult),
            reads=(ores,), sreads=(tg + ".mean", tg + ".rstd"),
            writes=(tg + ".sq1",))
        P.op("vector", lambda e, i=i: e.tensor_tensor(
            self.xs[:, i * D:(i + 1) * D], self.xs[:, i * D:(i + 1) * D],
            st["sq1"][:], ALU.add),
            reads=(tg + ".sq1", self.xs_res), writes=(self.xs_res,))
        if self.nxt is not None:
            nx, ng = self.nxt, self.nxt["tag"]
            P.op("vector", lambda e, i=i: e.reduce_sum(
                nx["sum"][:, i:i + 1], self.xs[:, i * D:(i + 1) * D],
                axis=AX.X), reads=(self.xs_res,), writes=(ng + ".sum",))
            P.op("scalar", lambda e, i=i: e.activation(
                st["sq0"][:], self.xs[:, i * D:(i + 1) * D], AF.Square,
                accum_out=nx["ssq"][:, i:i + 1]),
                reads=(self.xs_res,),
                writes=(tg + ".sq0", ng + f".ssq{i}"))



def emit_attention_head(P, h, qk_lhsT, qk_rhs, qk_res, v_km, attn, psA, psB,
                        pbuf, causal, exp_scale, exp_bias, exp_bias_res,
                        clamp, mask0, scrA, rrow, astage, extra_items=(),
                        evac_eng="scalar"):
    items = []
    half_last = {}
    for half in range(2):
        for kc in range(NT):
            q_lo, q_hi = half * 1024, half * 1024 + 1024
            q_start = max(q_lo, kc * 128) if causal else q_lo
            if q_start < q_hi:
                items.append((kc, half, q_start, q_hi))
                half_last[half] = len(items) - 1
    N = len(items)
    ex = list(extra_items)
    M = N + len(ex)

    def slot_of(n):
        s = n % 3
        half = items[n][1] if n < N else 1
        unsafe = (n < N and half == 1 and n < half_last[0] + 7)
        if s == 2 and unsafe:
            s = n % 2
        if s < 2:
            return psA, s * 1024, (f"psA.s{s}",)
        w = (1 - half) * 1024
        return psB, w, (f"psB.b{2 * (1 - half)}", f"psB.b{2 * (1 - half) + 1}")

    def emit_scores(n):
        kc, half, q_start, q_hi = items[n]
        pst, sbase, pres = slot_of(n)
        base = sbase + (q_start - half * 1024)
        end = sbase + 1024
        fns = []
        for (lo, nn) in _pieces(base, end):
            gq = half * 1024 + (lo - sbase)
            fns.append(lambda e, kc=kc, lo=lo, nn=nn, gq=gq, pst=pst: e.matmul(
                pst[:, lo:lo + nn], qk_lhsT(kc), qk_rhs(gq, nn),
                start=True, stop=True))
        P.group("tensor", fns, reads=qk_res, writes=pres)

    def emit_expmask(n):
        kc, half, q_start, q_hi = items[n]
        pst, sbase, pres = slot_of(n)
        base = sbase + (q_start - half * 1024)
        end = sbase + 1024
        pslot = n % 2
        pbres = f"pbuf.s{pslot}"
        pb = pbuf[:, pslot * 1024 + (q_start - half * 1024): pslot * 1024 + 1024]
        sc_ap = pst[:, base:end]
        kw = {"scale": exp_scale}
        sr = ()
        if exp_bias is not None:
            kw["bias"] = exp_bias
            sr = (exp_bias_res,)
        P.op("scalar", lambda e, sc_ap=sc_ap, pb=pb, kw=kw: e.activation(
            pb, sc_ap, AF.Exp, **kw),
            reads=pres, sreads=sr, writes=(pbres,))
        if clamp is not None:
            P.op("vector", lambda e, pb=pb: e.tensor_scalar(
                pb, pb, clamp[1], clamp[0], ALU.min, ALU.max),
                reads=(pbres,), writes=(pbres,))
        if causal:
            dlo = kc * 128
            if q_start <= dlo < q_hi:
                loc = dlo - q_start
                P.op("vector", lambda e, pb=pb, loc=loc: e.tensor_tensor(
                    pb[:, loc:loc + 128], pb[:, loc:loc + 128], mask0[:],
                    ALU.mult), reads=(pbres, "c_mask0"), writes=(pbres,))

    def emit_pv(n):
        kc, half, q_start, q_hi = items[n]
        pslot = n % 2
        pbres = f"pbuf.s{pslot}"
        pb = pbuf[:, pslot * 1024 + (q_start - half * 1024): pslot * 1024 + 1024]
        fns = []
        wr = []
        for (gq, nn) in _pieces(q_start, q_hi):
            blk = gq // 512
            if causal:
                firstkc = 0
                lastkc = min(15, (blk * 512 + 511) // 128)
            else:
                firstkc, lastkc = 0, 15
            wr.append(f"psB.b{blk}")
            fns.append(lambda e, kc=kc, gq=gq, nn=nn, h=h, pb=pb,
                       q_start=q_start, lastkc=lastkc: e.matmul(
                           psB[:, gq:gq + nn][0:65, :],
                           v_km[:, kc * (H * 65) + h * 65:
                                kc * (H * 65) + h * 65 + 65],
                           pb[:, gq - q_start: gq - q_start + nn],
                           start=(kc == 0), stop=(kc == lastkc)))
        P.group("tensor", fns, reads=(pbres, "v_km"), writes=tuple(wr))

    odd = (h % 2 == 1)

    def emit_norm_s1(b):
        bres = f"psB.b{b}"
        rr = rrow[b % 2]
        P.op("vector", lambda e, b=b, rr=rr: e.reciprocal(
            rr[:], psB[64:65, b * 512:(b + 1) * 512]),
            reads=(bres,), writes=(f"rrow{b % 2}",))
        scr = scrA[b % 2]
        sres = f"scrA{b % 2}"
        if evac_eng == "scalar":
            P.op("scalar", lambda e, b=b, scr=scr: e.activation(
                scr[:], psB[0:64, b * 512:(b + 1) * 512], AF.Identity),
                reads=(bres,), writes=(sres,))
        else:
            P.op("vector", lambda e, b=b, scr=scr: e.tensor_copy(
                scr[:], psB[0:64, b * 512:(b + 1) * 512]),
                reads=(bres,), writes=(sres,))

    def emit_norm_s2(b):
        bres = f"psB.b{b}"
        rr = rrow[b % 2]
        scr = scrA[b % 2]
        sres = f"scrA{b % 2}"
        P.op("tensor", lambda e, b=b, rr=rr: e.matmul(
            psB[0:64, b * 512:(b + 1) * 512], P.consts["ones_f"][0:1, 0:64],
            rr[:], start=True, stop=True),
            reads=(f"rrow{b % 2}", "c_ones_f", sres), writes=(bres,))
        col0 = (h // 2) * S + b * 512
        if not odd:
            P.op("vector", lambda e, scr=scr, col0=col0, b=b:
                 e.tensor_tensor(attn[0:64, col0:col0 + 512],
                                 scr[:], psB[0:64, b * 512:(b + 1) * 512],
                                 ALU.mult),
                 reads=(sres, bres), writes=("attn",))
        else:
            P.op("vector", lambda e, scr=scr, b=b:
                 e.tensor_tensor(astage[:, b * 512:(b + 1) * 512],
                                 scr[:], psB[0:64, b * 512:(b + 1) * 512],
                                 ALU.mult),
                 reads=(sres, bres), writes=("astage",))

    norm_s1q = []
    norm_s2q = []

    def norm_tick(n):
        if norm_s2q and norm_s2q[0][0] <= n:
            emit_norm_s2(norm_s2q.pop(0)[1])
        if norm_s1q:
            b = norm_s1q.pop(0)
            emit_norm_s1(b)
            norm_s2q.append((n + 2, b))

    for n in range(M + 2):
        if 1 <= n < M + 1:
            m = n - 1
            if m < N:
                emit_expmask(m)
            else:
                ex[m - N]["post"](slot_of(m))
        if n < M:
            if n < N:
                emit_scores(n)
            else:
                ex[n - N]["mm"](slot_of(n))
        if n >= 2:
            if n - 2 < N:
                emit_pv(n - 2)
            norm_tick(n)
            if n - 2 == half_last.get(0):
                norm_s1q += [0, 1]
            if n - 2 == half_last.get(1):
                norm_s1q += [2, 3]
    for b in norm_s1q:
        emit_norm_s1(b)
        norm_s2q.append((0, b))
    for _, b in norm_s2q:
        emit_norm_s2(b)
    if odd:
        col0 = (h // 2) * S
        P.op("gpsimd", lambda e, col0=col0: e.dma_start(
            out=attn[64:128, col0:col0 + S], in_=astage[:]),
            reads=("astage",), writes=("attn",), dma=True)



def emit_block_l0(P, hw):
    xs = hw["xs"]
    w_dq, w_uq, w_qr = hw["w_dq"], hw["w_uq"], hw["w_qr"]
    w_dkv, w_uk, w_uv = hw["w_dkv"], hw["w_uk"], hw["w_uv"]
    w_kr, w_o = hw["w_kr"], hw["w_o"]
    st_blk = ExitStack()
    c_qb = P.sb("c_qb", [128, 2 * S], BF16, st_blk)
    c_kvb = P.sb("c_kvb", [128, 2 * S], BF16, st_blk)
    v_km = P.sb("v_km", [128, NT * H * 65], BF16, st_blk)
    kr_b = P.sb("kr_b", [32, S], BF16, st_blk)
    mask0 = P.sb("s_mask0", [128, 128], BF16, st_blk)
    cos32 = P.sb("s_cos32", [32, S], BF16, st_blk)
    sig32 = P.sb("s_sig32", [32, S], BF16, st_blk)
    swap32 = P.sb("s_swap32", [32, 32], BF16, st_blk)
    scr1 = P.sb("ropes1", [32, 1024], BF16, st_blk)
    P.op("vector", lambda e: e.memset(
        v_km[:].rearrange("p (k c) -> p k c", c=65)[:, :, 64:65], 1.0),
        writes=("v_km.ones",))
    for dst, srch, r in ((mask0, hw["mask0"], "c_mask0"),
                         (cos32, hw["cos32"], "cos32"),
                         (sig32, hw["sig32"], "sig32"),
                         (swap32, hw["swap32"], "swap32")):
        P.op("sync", lambda e, dst=dst, srch=srch: e.dma_start(
            out=dst[:], in_=srch), writes=(r,), dma=True)

    st_pre = ExitStack()
    w_dqb = P.sb("w_dqb", [128, 8 * DLAT], BF16, st_pre)
    w_dkvb = P.sb("w_dkvb", [128, 8 * DLAT], BF16, st_pre)
    w_krb = P.sb("w_krb", [128, 8 * DHR], BF16, st_pre)
    w_uvb = P.sb("w_uvb", [128, 2 * D], BF16, st_pre)
    for dst, src, dd, r in ((w_dqb, w_dq, DLAT, "w_dqb"),
                           (w_dkvb, w_dkv, DLAT, "w_dkvb"),
                           (w_krb, w_kr, DHR, "w_krb"),
                           (w_uvb, w_uv, D, "w_uvb")):
        P.op("gpsimd", lambda e, dst=dst, src=src, dd=dd: e.dma_start(
            out=dst[:].rearrange("p (k m) -> p k m", m=dd),
            in_=src.rearrange("(k p) m -> p k m", p=128)),
            writes=(r,), dma=True)

    stats = make_stats(P, st_pre, "ln0")
    emit_preln_stats(P, "xs", xs, stats)
    lnT = P.sb("lnT", [128, NCH * S], BF16, st_pre)
    tp_stack = ExitStack()
    emit_ln_transpose(P, mk_ln_apply(P, "xs", xs, stats), lnT, "lnT", tp_stack)
    tp_stack.close()
    P.full_barrier()

    psA = P.ps("psA", [128, 2048], F32, st_blk)
    psB = P.ps("psB", [128, 2048], F32, st_blk)
    PSA = ("psA.s0", "psA.s1")
    PSB = tuple(f"psB.b{b}" for b in range(4))

    for (wt, wres, dstb, dres) in ((w_dqb, "w_dqb", c_qb, "c_qb"),
                                   (w_dkvb, "w_dkvb", c_kvb, "c_kvb")):
        for m in range(2):
            fns = []
            for nq in range(4):
                for k in range(8):
                    fns.append(lambda e, m=m, nq=nq, k=k, wt=wt: e.matmul(
                        psA[:, nq * 512:(nq + 1) * 512],
                        wt[:, k * DLAT + m * 128: k * DLAT + (m + 1) * 128],
                        lnT[:, k * S + nq * 512: k * S + (nq + 1) * 512],
                        start=(k == 0), stop=(k == 7)))
            P.group("tensor", fns, reads=(wres, "lnT"), writes=PSA)
            P.op("vector", lambda e, dstb=dstb, m=m: e.tensor_copy(
                dstb[:, m * S:(m + 1) * S], psA[:]),
                reads=PSA, writes=(dres,))

    krr = P.sb("krraw", [32, S], BF16, st_pre)
    fns = []
    for nq in range(4):
        for k in range(8):
            fns.append(lambda e, nq=nq, k=k: e.matmul(
                psA[:, nq * 512:(nq + 1) * 512][0:32, :],
                w_krb[:, k * DHR:(k + 1) * DHR],
                lnT[:, k * S + nq * 512: k * S + (nq + 1) * 512],
                start=(k == 0), stop=(k == 7)))
    P.group("tensor", fns, reads=("w_krb", "lnT"), writes=PSA)
    P.op("vector", lambda e: e.tensor_copy(krr[:], psA[:, 0:2048][0:32, :]),
         reads=PSA, writes=("krraw",))
    fns = [lambda e, nq=nq: e.matmul(
        psB[:, nq * 512:(nq + 1) * 512][0:32, :],
        swap32[:], krr[:, nq * 512:(nq + 1) * 512],
        start=True, stop=True) for nq in range(4)]
    P.group("tensor", fns, reads=("krraw", "swap32"), writes=PSB)
    for hf in range(2):
        c0, c1 = hf * 1024, (hf + 1) * 1024
        P.op("vector", lambda e, c0=c0, c1=c1: e.tensor_tensor(
            scr1[:, 0:1024], krr[:, c0:c1], cos32[:, c0:c1], ALU.mult),
            reads=("krraw", "cos32"), writes=("rp",))
        P.op("vector", lambda e, c0=c0, c1=c1: e.tensor_tensor(
            kr_b[:, c0:c1], psB[:, c0:c1][0:32, :], sig32[:, c0:c1],
            ALU.mult),
            reads=PSB + ("sig32",), writes=("kr_b",))
        P.op("vector", lambda e, c0=c0, c1=c1: e.tensor_tensor(
            kr_b[:, c0:c1], scr1[:, 0:1024], kr_b[:, c0:c1], ALU.add),
            reads=("rp", "kr_b"), writes=("kr_b",))

    for kc in range(NT):
        fns = []
        for lk in range(2):
            for n2 in range(2):
                fns.append(lambda e, kc=kc, lk=lk, n2=n2: e.matmul(
                    psA[:, n2 * 512:(n2 + 1) * 512],
                    c_kvb[:, lk * S + kc * 128: lk * S + (kc + 1) * 128],
                    w_uvb[:, lk * D + n2 * 512: lk * D + (n2 + 1) * 512],
                    start=(lk == 0), stop=(lk == 1)))
        P.group("tensor", fns, reads=("c_kvb", "w_uvb"),
                writes=("psA.s0",))
        P.op("vector", lambda e, kc=kc: e.tensor_copy(
            v_km[:, kc * H * 65:(kc + 1) * H * 65].rearrange(
                "p (h c) -> p h c", c=65)[:, :, 0:64],
            psA[:, 0:1024].rearrange("p (h c) -> p h c", c=64)),
            reads=("psA.s0", "v_km.ones"), writes=("v_km",))
    st_pre.close()
    P.full_barrier()

    st_att = ExitStack()
    w_uqb = P.sb("w_uqb", [128, 2 * D], BF16, st_att)
    w_qrb = P.sb("w_qrb", [128, 2 * 512], BF16, st_att)
    w_ukb = P.sb("w_ukb", [128, 2 * D], BF16, st_att)
    for dst, srcw, dd, r in ((w_uqb, w_uq, D, "w_uqb"),
                            (w_qrb, w_qr, 512, "w_qrb"),
                            (w_ukb, w_uk, D, "w_ukb")):
        P.op("gpsimd", lambda e, dst=dst, srcw=srcw, dd=dd: e.dma_start(
            out=dst[:].rearrange("p (k m) -> p k m", m=dd),
            in_=srcw.rearrange("(k p) m -> p k m", p=128)),
            writes=(r,), dma=True)

    qh = [P.sb(f"qh{i}", [96, S], BF16, st_att) for i in range(2)]
    kh = [P.sb(f"kh{i}", [96, S], BF16, st_att) for i in range(2)]
    qraw = P.sb("qraw", [32, S], BF16, st_att)
    qrstage = qraw
    st_attn = ExitStack()
    attn = P.sb("attn", [128, NCH * S], BF16, st_attn, side="right")
    astage = P.sb("astage", [64, S], BF16, st_att)
    pbuf = P.sb("pbuf", [128, 2 * 1024], BF16, st_att)
    scrA = (P.sb("scrA0", [64, 512], BF16, st_att),
            P.sb("scrA1", [64, 512], BF16, st_att))
    rrow = [P.sb(f"rrow{i}", [1, 512], F32, st_att) for i in range(2)]

    for i in range(2):
        P.op("gpsimd", lambda e, i=i: e.dma_start(
            out=kh[i][64:96, :], in_=kr_b[:]),
            reads=("kr_b",), writes=(f"kh{i}",), dma=True)

    EXP_SCALE = float(1.0 / math.sqrt(96.0))
    CL_HI = 1.0
    CL_LO = float(np.float32(np.exp(np.float64(-160.0) / math.sqrt(96.0))))
    nb80 = P.sb("nb80", [128, 1], F32, st_att)
    P.op("vector", lambda e: e.memset(nb80[:], float(-80.0 / math.sqrt(96.0))),
         writes=("nb80",))

    def prod_items(h):
        hb = h % 2
        qht, kht = qh[hb], kh[hb]
        qres, kres = f"qh{hb}", f"kh{hb}"

        def proj(wb, wres, wbase, wstep, wid, src, sres, rows, dst_fn, dres,
                 half):
            def mm(slot, half=half):
                pst, sbase, pres = slot
                fns = []
                for nqg in range(2):
                    nq = half * 2 + nqg
                    for lk in range(2):
                        fns.append(lambda e, nq=nq, nqg=nqg, lk=lk, pst=pst,
                                   sbase=sbase: e.matmul(
                            pst[:, sbase + nqg * 512:
                                sbase + (nqg + 1) * 512][0:rows, :],
                            wb[:, lk * wstep + wbase:
                               lk * wstep + wbase + wid],
                            src[:, lk * S + nq * 512: lk * S + (nq + 1) * 512],
                            start=(lk == 0), stop=(lk == 1)))
                P.group("tensor", fns, reads=(wres, sres), writes=pres)

            def post(slot, half=half):
                pst, sbase, pres = slot
                P.op("scalar", lambda e, pst=pst, sbase=sbase, half=half:
                     e.activation(dst_fn(half),
                                  pst[:, sbase:sbase + 1024][0:rows, :],
                                  AF.Identity),
                     reads=pres, writes=(dres,))
            return {"mm": mm, "post": post}

        def qc(half):
            return proj(w_uqb, "w_uqb", h * 64, D, 64, c_qb, "c_qb", 64,
                        lambda hf: qht[0:64, hf * 1024:(hf + 1) * 1024],
                        qres, half)

        def qr(half):
            return proj(w_qrb, "w_qrb", h * 32, 512, 32, c_qb, "c_qb", 32,
                        lambda hf: qraw[:, hf * 1024:(hf + 1) * 1024],
                        f"qrw{half}", half)

        def kc_(half):
            return proj(w_ukb, "w_ukb", h * 64, D, 64, c_kvb, "c_kvb", 64,
                        lambda hf: kht[0:64, hf * 1024:(hf + 1) * 1024],
                        kres, half)

        def sw(half):
            hc0, hc1 = half * 1024, (half + 1) * 1024

            def mm(slot):
                pst, sbase, pres = slot
                fns = []
                for nqg in range(2):
                    nq = half * 2 + nqg
                    fns.append(lambda e, nq=nq, nqg=nqg, pst=pst,
                               sbase=sbase: e.matmul(
                        pst[:, sbase + nqg * 512:
                            sbase + (nqg + 1) * 512][0:32, :],
                        swap32[:], qraw[:, nq * 512:(nq + 1) * 512],
                        start=True, stop=True))
                P.group("tensor", fns, reads=(f"qrw{half}", "swap32"),
                        writes=pres)

            def post(slot):
                pst, sbase, pres = slot
                P.op("vector", lambda e: e.tensor_tensor(
                    scr1[:, 0:1024], qraw[:, hc0:hc1], cos32[:, hc0:hc1],
                    ALU.mult),
                    reads=(f"qrw{half}", "cos32"), writes=("rp",))
                P.op("vector", lambda e, pst=pst, sbase=sbase:
                     e.tensor_tensor(
                         qraw[:, hc0:hc1], pst[:, sbase:sbase + 1024][0:32, :],
                         sig32[:, hc0:hc1], ALU.mult),
                     reads=pres + ("sig32", f"qrw{half}"),
                     writes=(f"qrw{half}",))
                P.op("vector", lambda e: e.tensor_tensor(
                    qraw[:, hc0:hc1], scr1[:, 0:1024], qraw[:, hc0:hc1],
                    ALU.add),
                    reads=("rp", f"qrw{half}"), writes=(f"qrw{half}",))
                P.op("gpsimd", lambda e: e.dma_start(
                    out=qht[64:96, hc0:hc1], in_=qraw[:, hc0:hc1]),
                    reads=(f"qrw{half}",), writes=(qres,), dma=True)
            return {"mm": mm, "post": post}

        return [qc(0), qr(0), qc(1), qr(1), sw(0), kc_(0), sw(1), kc_(1)]

    for i, it in enumerate(prod_items(0)):
        slot = (psA, (i % 2) * 1024, (f"psA.s{i % 2}",))
        it["mm"](slot)
        it["post"](slot)

    for h in range(H):
        hb = h % 2
        qht, kht = qh[hb], kh[hb]
        qres, kres = f"qh{hb}", f"kh{hb}"

        def qk_lhsT(kc, kht=kht):
            return kht[:, kc * 128:(kc + 1) * 128]

        def qk_rhs(lo, width, qht=qht):
            return qht[:, lo:lo + width]

        emit_attention_head(
            P, h, qk_lhsT, qk_rhs, (qres, kres, "v_km"),
            v_km, attn, psA, psB, pbuf, causal=True, exp_scale=EXP_SCALE,
            exp_bias=nb80[:], exp_bias_res="nb80", clamp=(CL_LO, CL_HI),
            mask0=mask0, scrA=scrA, rrow=rrow, astage=astage,
            extra_items=prod_items(h + 1) if h + 1 < H else ())

    st_att.close()
    P.full_barrier()
    st_o = ExitStack()
    w_ob = P.sb("w_ob", [128, 8 * D], BF16, st_o)
    P.op("gpsimd", lambda e: e.dma_start(
        out=w_ob[:].rearrange("p (k m) -> p k m", m=D),
        in_=w_o.rearrange("(k p) m -> p k m", p=128)),
        writes=("w_ob",), dma=True)
    pln = PostLN(P, "xs", xs, st_o, "po0",
                 nxt=hw.get("pre_stats", {}).get("lnq"))
    for i in range(NT):
        half = i % 2
        pres = f"psA.s{half}"
        fns = []
        for c in range(NCH):
            for n2 in range(2):
                fns.append(lambda e, i=i, c=c, n2=n2, half=half: e.matmul(
                    psA[:, half * 1024 + n2 * 512: half * 1024 + (n2 + 1) * 512],
                    attn[:, c * S + i * 128: c * S + (i + 1) * 128],
                    w_ob[:, c * D + n2 * 512: c * D + (n2 + 1) * 512],
                    start=(c == 0), stop=(c == 7)))
        P.group("tensor", fns, reads=("attn", "w_ob"), writes=(pres,))
        pln.chunk(i, psA[:, half * 1024:(half + 1) * 1024], pres)
    st_o.close()
    P.full_barrier()
    st_attn.close()
    P.full_barrier()
    st_blk.close()
    P.full_barrier()



def emit_quant_preproc(P, src_h, dst_h, dst_res, rows, cols, scale, st_w,
                       tag, ternary=None, bufs=None):
    if bufs is None:
        wf = [P.sb(f"{tag}_wf{i}", [128, cols], F32, st_w) for i in range(2)]
        wi = [P.sb(f"{tag}_wi{i}", [128, cols], I8, st_w) for i in range(2)]
        wb = [P.sb(f"{tag}_wb{i}", [128, cols], BF16, st_w) for i in range(2)]
    else:
        wf, wi, wb = bufs
    for c in range(rows // 128):
        s = c % 2
        f, ii, b = wf[s], wi[s], wb[s]
        rf, ri, rb = f"qp.wf{s}", f"qp.wi{s}", f"qp.wb{s}"
        P.op("sync", lambda e, c=c, f=f: e.dma_start(
            out=f[:, 0:cols], in_=src_h[c * 128:(c + 1) * 128, :]),
            writes=(rf,), dma=True)
        if ternary is None:
            P.op("vector", lambda e, f=f, ii=ii: e.tensor_scalar(
                ii[:, 0:cols], f[:, 0:cols], float(scale), None, ALU.mult),
                reads=(rf,), writes=(ri,))
            P.op("scalar", lambda e, ii=ii, b=b: e.activation(
                b[:, 0:cols], ii[:, 0:cols], AF.Identity,
                scale=float(1.0 / scale)),
                reads=(ri,), writes=(rb,))
        else:
            ainv_ap, ainv_res = ternary
            P.op("vector", lambda e, f=f, ii=ii: e.tensor_scalar(
                ii[:, 0:cols], f[:, 0:cols], ainv_ap, None, ALU.mult),
                reads=(rf,), sreads=(ainv_res,), writes=(ri,))
            P.op("vector", lambda e, ii=ii, b=b: e.tensor_scalar(
                b[:, 0:cols], ii[:, 0:cols], 1.0, -1.0, ALU.min, ALU.max),
                reads=(ri,), writes=(rb,))
        P.op("scalar", lambda e, c=c, b=b: e.dma_start(
            out=dst_h[c * 128:(c + 1) * 128, :], in_=b[:, 0:cols]),
            reads=(rb,), writes=(dst_res,), dma=True)


def emit_ffn_matmul_phase(P, xT, xT_res, hbuf, w1_hbm, w2_hbm, st_mm,
                          up_evac, down_pre, pln, down_bias=None,
                          down_bias_res=None, phase_out=None,
                          w1_src=None, w2_src=None, quant=None,
                          after_up=None):
    w1band = [P.sb(f"w1b{i}", [128, 8 * 512], BF16, st_mm) for i in range(2)]
    w2band = [P.sb(f"w2b{i}", [128, 2 * D], BF16, st_mm) for i in range(2)]
    wqf = [P.sb(f"wqf{i}", [128, 2048], F32, st_mm) for i in range(3)]
    wqi = [P.sb(f"wqi{i}", [128, 2048], I8, st_mm) for i in range(2)]
    ps_up = P.ps("ps_up", [128, 1024], F32, st_mm)
    ps_dn = [P.ps(f"ps_dn{i}", [128, 1024], F32, st_mm) for i in range(3)]
    if phase_out is not None:
        phase_out["ps_dn0"] = ps_dn[0]
        phase_out["ps_dn1"] = ps_dn[1]

    if isinstance(quant, tuple):
        quant1, quant2 = quant
    else:
        quant1 = quant2 = quant

    def dn_slot(qc):
        if qc < 3:
            return ps_dn[qc], (f"ps_dn{qc}",)
        return ps_up, ("ps_up.0", "ps_up.1")

    def load_w1_band(hg, bs, first):
        if not first:
            P.op("sync" if hg % 2 == 0 else "gpsimd",
                 lambda e, hg=hg, bs=bs: e.dma_start(
                     out=w1band[bs][:].rearrange("p (k m) -> p k m", m=512),
                     in_=w1_hbm[:, hg * 512:(hg + 1) * 512].rearrange(
                         "(k p) m -> p k m", p=128)),
                 reads=("w1q_hbm",), writes=(f"w1b{bs}",), dma=True)
            return
        for half in range(2):
            s = (hg * 2 + half) % 3
            si = (hg * 2 + half) % 2
            fr, ir = f"wqf{s}", f"wqi{si}"
            P.op("sync" if half == 0 else "gpsimd",
                 lambda e, hg=hg, half=half, s=s: e.dma_start(
                     out=wqf[s][:].rearrange("p (k m) -> p k m", m=256),
                     in_=w1_src[:, hg * 512 + half * 256:
                                hg * 512 + half * 256 + 256].rearrange(
                         "(k p) m -> p k m", p=128)),
                 writes=(fr,), dma=True)
            quant1(wqf[s][:], fr, wqi[si][:], ir,
                   w1band[bs][:].rearrange("p (k m) -> p k m", m=512)
                   [:, :, half * 256:(half + 1) * 256],
                   f"w1b{bs}")
        P.op("scalar", lambda e, hg=hg, bs=bs: e.dma_start(
            out=w1_hbm[:, hg * 512:(hg + 1) * 512].rearrange(
                "(k p) m -> p k m", p=128),
            in_=w1band[bs][:].rearrange("p (k m) -> p k m", m=512)),
            reads=(f"w1b{bs}",), writes=("w1q_hbm",), dma=True)

    def load_w2_chunk(hg, bs, first):
        if not first:
            P.op("sync" if hg % 2 == 0 else "gpsimd",
                 lambda e, hg=hg, bs=bs: e.dma_start(
                     out=w2band[bs][:].rearrange("p (k m) -> p k m", m=D),
                     in_=w2_hbm[hg * 256:(hg + 1) * 256, :].rearrange(
                         "(k p) m -> p k m", p=128)),
                 reads=("w2q_hbm",), writes=(f"w2b{bs}",), dma=True)
            return
        for half in range(2):
            s = (hg * 2 + half) % 3
            si = (hg * 2 + half) % 2
            fr, ir = f"wqf{s}", f"wqi{si}"
            P.op("sync" if half == 0 else "gpsimd",
                 lambda e, hg=hg, half=half, s=s: e.dma_start(
                     out=wqf[s][:, 0:1024],
                     in_=w2_src[hg * 256 + half * 128:
                                hg * 256 + half * 128 + 128, :]),
                 writes=(fr,), dma=True)
            quant2(wqf[s][:, 0:1024], fr, wqi[si][:, 0:1024], ir,
                   w2band[bs][:, half * D:(half + 1) * D], f"w2b{bs}")
        P.op("scalar", lambda e, hg=hg, bs=bs: e.dma_start(
            out=w2_hbm[hg * 256:(hg + 1) * 256, :].rearrange(
                "(k p) m -> p k m", p=128),
            in_=w2band[bs][:].rearrange("p (k m) -> p k m", m=D)),
            reads=(f"w2b{bs}",), writes=("w2q_hbm",), dma=True)

    for jt in range(4):
        for hg in range(8):
            bs = hg % 2
            load_w1_band(hg, bs, jt == 0)
            for hx in range(4):
                hc = hg * 4 + hx
                slot = hc % 2
                pres = f"ps_up.{slot}"
                fns = []
                for k in range(8):
                    fns.append(lambda e, jt=jt, k=k, slot=slot, bs=bs, hx=hx:
                               e.matmul(
                                   ps_up[:, slot * 512:(slot + 1) * 512],
                                   w1band[bs][:, k * 512 + hx * 128:
                                              k * 512 + hx * 128 + 128],
                                   xT[:, k * S + jt * 512:
                                      k * S + jt * 512 + 512],
                                   start=(k == 0), stop=(k == 7)))
                P.group("tensor", fns, reads=(f"w1b{bs}", xT_res),
                        writes=(pres,))
                up_evac(hc, ps_up[:, slot * 512:(slot + 1) * 512], pres)
                if after_up is not None:
                    after_up(hc)
        scale_maker = down_pre(jt) if down_pre is not None else None
        for hg in range(16):
            bs = hg % 2
            load_w2_chunk(hg, bs, jt == 0)
            for hx in range(2):
                hc = hg * 2 + hx
                for qc in range(4):
                    dnp, dres = dn_slot(qc)
                    fns = []
                    for n2 in range(2):
                        fns.append(lambda e, hc=hc, qc=qc, n2=n2, bs=bs,
                                   hx=hx, dnp=dnp: e.matmul(
                                       dnp[:, n2 * 512:(n2 + 1) * 512],
                                       hbuf[:, hc * 512 + qc * 128:
                                            hc * 512 + qc * 128 + 128],
                                       w2band[bs][:, hx * D + n2 * 512:
                                                  hx * D + (n2 + 1) * 512],
                                       start=(hc == 0), stop=(hc == 31)))
                    P.group("tensor", fns, reads=(f"w2b{bs}", f"hbuf.{hc}"),
                            writes=dres)
        for qc in (3, 0, 1, 2):
            i = jt * 4 + qc
            dnp, dres = dn_slot(qc)
            sc = scale_maker(qc) if scale_maker is not None else None
            if sc is not None:
                pln.chunk(i, dnp[:], dres, scale=sc[0], scale_res=sc[1])
            else:
                pln.chunk(i, dnp[:], dres, bias_bcast=down_bias,
                          bias_res=down_bias_res)


def mk_quant64(P):
    def quant(f_ap, f_res, i_ap, i_res, dst_ap, dst_res):
        P.op("vector", lambda e: e.tensor_scalar(
            i_ap, f_ap, 64.0, None, ALU.mult), reads=(f_res,), writes=(i_res,))
        P.op("scalar", lambda e: e.activation(
            dst_ap, i_ap, AF.Identity, scale=1.0 / 64.0),
            reads=(i_res,), writes=(dst_res,))
    return quant


def mk_ternary(P, ainv_ap, ainv_res):
    def quant(f_ap, f_res, i_ap, i_res, dst_ap, dst_res):
        P.op("vector", lambda e: e.tensor_scalar(
            i_ap, f_ap, ainv_ap, None, ALU.mult),
            reads=(f_res,), sreads=(ainv_res,), writes=(i_res,))
        P.op("vector", lambda e: e.tensor_scalar(
            dst_ap, i_ap, 1.0, -1.0, ALU.min, ALU.max),
            reads=(i_res,), writes=(dst_res,))
    return quant


def emit_block_ffn0(P, hw):
    xs = hw["xs"]
    w1t_h, b1_h, w2t_h, b2_h = hw["w1t"], hw["b1"], hw["w2t"], hw["b2"]
    identF = P.consts["identF"]
    w1q_hbm = P.nc.dram_tensor("w1q_scr", [D, DFF], BF16, kind="Internal").ap()
    w2q_hbm = P.nc.dram_tensor("w2q_scr", [DFF, D], BF16, kind="Internal").ap()
    st_blk = ExitStack()
    xqT = P.sb("xqT", [128, NCH * S], BF16, st_blk)
    B16 = P.sb("B16", [128, 32], F32, st_blk)
    B2C = P.sb("B2C", [128, D], BF16, st_blk)



    ps_small_stack = ExitStack()
    ps_small = P.ps("ps_small", [128, 512], F32, ps_small_stack)
    b1r = P.sb("b1r", [32, 128], F32, ps_small_stack)
    b1i = P.sb("b1i", [32, 128], I8, ps_small_stack)
    b1f = P.sb("b1f", [32, 128], F32, ps_small_stack)
    P.op("sync", lambda e: e.dma_start(
        out=b1r[:], in_=b1_h.rearrange("(c p) -> c p", p=128)),
        writes=("b1r",), dma=True)
    P.op("vector", lambda e: e.tensor_scalar(
        b1i[:], b1r[:], 64.0, None, ALU.mult), reads=("b1r",), writes=("b1i",))
    P.op("scalar", lambda e: e.activation(
        b1f[:], b1i[:], AF.Identity, scale=0.25), reads=("b1i",),
        writes=("b1f",))
    P.op("tensor", lambda e: e.transpose(
        ps_small[:, 0:32], b1f[:], identF[0:32, 0:32]),
        reads=("b1f", "c_identF"), writes=("ps_small",))
    P.op("vector", lambda e: e.tensor_copy(B16[:], ps_small[:, 0:32]),
         reads=("ps_small",), writes=("B16",))
    b2r = P.sb("b2r", [1, D], F32, ps_small_stack)
    b2i = P.sb("b2i", [1, D], I8, ps_small_stack)
    b2f = P.sb("b2f", [1, D], F32, ps_small_stack)
    P.op("sync", lambda e: e.dma_start(
        out=b2r[:], in_=b2_h.rearrange("(o d) -> o d", o=1)),
        writes=("b2r",), dma=True)
    P.op("vector", lambda e: e.tensor_scalar(
        b2i[:], b2r[:], 64.0, None, ALU.mult), reads=("b2r",), writes=("b2i",))
    P.op("scalar", lambda e: e.activation(
        b2f[:], b2i[:], AF.Identity, scale=1.0 / 64.0), reads=("b2i",),
        writes=("b2f",))
    for n2 in range(2):
        P.op("tensor", lambda e, n2=n2: e.matmul(
            ps_small[:, 0:512], P.consts["ones_f"][0:1, :],
            b2f[:, n2 * 512:(n2 + 1) * 512], start=True, stop=True),
            reads=("b2f", "c_ones_f"), writes=("ps_small",))
        P.op("vector", lambda e, n2=n2: e.tensor_copy(
            B2C[:, n2 * 512:(n2 + 1) * 512], ps_small[:, 0:512]),
            reads=("ps_small",), writes=("B2C",))
    ps_small_stack.close()
    P.full_barrier()

    st_ln = ExitStack()
    pool = hw.get("pre_stats")
    if pool:
        stats = pool["lnq"]
        emit_preln_stats(P, "xs", xs, stats, pre=True)
    else:
        stats = make_stats(P, st_ln, "lnq")
        emit_preln_stats(P, "xs", xs, stats)
    r16 = stats["r2"]
    P.op("vector", lambda e: e.tensor_scalar(
        r16[:], stats["rstd"][:], 16.0, None, ALU.mult),
        reads=("lnq.rstd",), writes=("lnq.r16",))
    xq_i8 = [P.sb(f"xq_i8{i}", [128, D], I8, st_ln) for i in range(2)]

    def xq_apply(t, dst, dres):
        ii = xq_i8[t % 2]
        ri = f"xqi8.{t % 2}"
        P.op("vector", lambda e, t=t, ii=ii: e.tensor_scalar(
            ii[:], xs[:, t * D:(t + 1) * D], stats["mean"][:, t:t + 1],
            r16[:, t:t + 1], ALU.subtract, ALU.mult),
            reads=("xs",), sreads=("lnq.mean", "lnq.r16"), writes=(ri,))
        P.op("scalar", lambda e, ii=ii, dst=dst: e.activation(
            dst[:], ii[:], AF.Identity, scale=1.0 / 16.0),
            reads=(ri,), writes=(dres,))

    tp_stack = ExitStack()
    emit_ln_transpose(P, xq_apply, xqT, "xqT", tp_stack)
    tp_stack.close()
    P.full_barrier()
    st_ln.close()
    P.full_barrier()

    st_mm = ExitStack()
    hbuf = P.sb("hbuf", [128, 32 * 512], BF16, st_mm)
    hscr = [P.sb(f"hscr{i}", [128, 512], I8, st_mm) for i in range(2)]
    pln = PostLN(P, "xs", xs, st_mm, "poq",
                 nxt=hw.get("pre_stats", {}).get("ln1"))

    def up_evac(hc, psum_ap, psum_res):
        ii = hscr[hc % 2]
        ri = f"hscr{hc % 2}"
        P.op("vector", lambda e, ii=ii, psum_ap=psum_ap, hc=hc:
             e.tensor_scalar(ii[:], psum_ap, 16.0, B16[:, hc:hc + 1],
                             ALU.mult, ALU.add),
             reads=(psum_res,), sreads=("B16",), writes=(ri,))
        P.op("scalar", lambda e, ii=ii, hc=hc: e.activation(
            hbuf[:, hc * 512:(hc + 1) * 512], ii[:], AF.Relu, scale=1.0 / 16.0),
            reads=(ri,), writes=(f"hbuf.{hc}",))

    emit_ffn_matmul_phase(P, xqT, "xqT", hbuf, w1q_hbm, w2q_hbm, st_mm,
                          up_evac, None, pln, down_bias=B2C,
                          down_bias_res="B2C", w1_src=w1t_h, w2_src=w2t_h,
                          quant=mk_quant64(P))
    st_mm.close()
    P.full_barrier()
    st_blk.close()
    P.full_barrier()



def emit_vec_to_partition(P, src_h, src_res, n, dst, dst_res, ps_ap, ps_res,
                          identF, post_scale=1.0, stack=None):
    nch = n // 128
    r = P.sb(dst_res + "_r", [nch, 128], F32, stack)
    P.op("sync", lambda e: e.dma_start(
        out=r[:], in_=src_h.rearrange("(c p) -> c p", p=128)),
        writes=(src_res,), dma=True)
    if post_scale != 1.0:
        P.op("vector", lambda e: e.tensor_scalar(
            r[:], r[:], float(post_scale), None, ALU.mult),
            reads=(src_res,), writes=(src_res,))
    P.op("tensor", lambda e: e.transpose(
        ps_ap[:, 0:nch], r[:], identF[0:nch, 0:nch]),
        reads=(src_res, "c_identF"), writes=(ps_res,))
    P.op("vector", lambda e: e.tensor_copy(dst[:], ps_ap[:, 0:nch]),
         reads=(ps_res,), writes=(dst_res,))


def emit_row_bcast(P, src_h, src_res, dst, dst_res, ps_ap, ps_res, n=D,
                   stack=None):
    row = P.sb(dst_res + "_row", [1, n], F32, stack)
    P.op("sync", lambda e: e.dma_start(
        out=row[:], in_=src_h.rearrange("(o d) -> o d", o=1)),
        writes=(src_res,), dma=True)
    for n2 in range(n // 512):
        P.op("tensor", lambda e, n2=n2: e.matmul(
            ps_ap[:, 0:512], P.consts["ones_f"][0:1, :],
            row[:, n2 * 512:(n2 + 1) * 512], start=True, stop=True),
            reads=(src_res, "c_ones_f"), writes=(ps_res,))
        P.op("vector", lambda e, n2=n2: e.tensor_copy(
            dst[:, n2 * 512:(n2 + 1) * 512], ps_ap[:, 0:512]),
            reads=(ps_res,), writes=(dst_res,))


def emit_block_l1(P, hw):
    xs = hw["xs"]
    qwt_h, kwt_h, vwt_h, owt_h = hw["qwt"], hw["kwt"], hw["vwt"], hw["owt"]
    qb_h, kb_h, vb_h, ob_h = hw["qb"], hw["kb"], hw["vb"], hw["ob"]
    identF = P.consts["identF"]
    st_blk = ExitStack()
    v_km = P.sb("v_km", [128, NT * H * 65], BF16, st_blk)
    P.op("vector", lambda e: e.memset(
        v_km[:].rearrange("p (k c) -> p k c", c=65)[:, :, 64:65], 1.0),
        writes=("v_km.ones",))

    qb8 = P.sb("qb8", [128, 8], F32, st_blk)
    kbp = P.sb("kbp", [128, 8], F32, st_blk)
    pss = ExitStack()
    ps_small = P.ps("ps_small", [128, 512], F32, pss)
    emit_vec_to_partition(P, qb_h, "qb_h", D, qb8, "qb8", ps_small, "ps_small",
                          identF, post_scale=0.125, stack=pss)
    emit_vec_to_partition(P, kb_h, "kb_h", D, kbp, "kbp", ps_small, "ps_small",
                          identF, stack=pss)
    pss.close()
    P.full_barrier()

    st_lnT = ExitStack()
    lnT = P.sb("lnT", [128, NCH * S], BF16, st_lnT)
    st_ln = ExitStack()
    pool = hw.get("pre_stats")
    if pool:
        stats = pool["ln1"]
        emit_preln_stats(P, "xs", xs, stats, pre=True)
    else:
        stats = make_stats(P, st_ln, "ln1")
        emit_preln_stats(P, "xs", xs, stats)
    tp_stack = ExitStack()
    emit_ln_transpose(P, mk_ln_apply(P, "xs", xs, stats), lnT, "lnT", tp_stack)
    tp_stack.close()
    P.full_barrier()
    st_ln.close()
    P.full_barrier()

    psA = P.ps("psA", [128, 2048], F32, st_blk)
    psB = P.ps("psB", [128, 2048], F32, st_blk)
    PSA = ("psA.s0", "psA.s1")
    PSB = tuple(f"psB.b{b}" for b in range(4))

    st_v = ExitStack()
    vwT = P.sb("vwT", [128, 8 * D], BF16, st_v)
    VBC = P.sb("VBC", [128, D], F32, st_v)
    emit_row_bcast(P, vb_h, "vb_h", VBC, "VBC", psB[:, 0:512], "psB.b0",
                   stack=st_v)
    P.op("gpsimd", lambda e: e.dma_start(
        out=vwT[:].rearrange("p (k m) -> p k m", m=D),
        in_=vwt_h.rearrange("(k p) m -> p k m", p=128)),
        writes=("vwT",), dma=True)
    for kc in range(NT):
        fns = []
        for dc in range(NCH):
            for n2 in range(2):
                fns.append(lambda e, kc=kc, dc=dc, n2=n2: e.matmul(
                    psA[:, n2 * 512:(n2 + 1) * 512],
                    lnT[:, dc * S + kc * 128: dc * S + (kc + 1) * 128],
                    vwT[:, dc * D + n2 * 512: dc * D + (n2 + 1) * 512],
                    start=(dc == 0), stop=(dc == 7)))
        P.group("tensor", fns, reads=("lnT", "vwT"), writes=("psA.s0",))
        P.op("vector", lambda e, kc=kc: e.tensor_tensor(
            v_km[:, kc * H * 65:(kc + 1) * H * 65].rearrange(
                "p (h c) -> p h c", c=65)[:, :, 0:64],
            psA[:, 0:1024].rearrange("p (h c) -> p h c", c=64),
            VBC[:].rearrange("p (h c) -> p h c", c=64), ALU.add),
            reads=("psA.s0", "v_km.ones", "VBC"), writes=("v_km",))
        if "alpha" in hw:
            for _ in range(2):
                hw["alpha"]["chunk"]()
    st_v.close()
    P.full_barrier()

    st_att = ExitStack()
    qwb = [P.sb(f"qwb{i}", [128, 8 * 128], BF16, st_att) for i in range(2)]
    kwb = [P.sb(f"kwb{i}", [128, 8 * 128], BF16, st_att) for i in range(2)]
    qpair_b = [P.sb(f"qpair{i}", [128, S], BF16, st_att) for i in range(2)]
    kpair_b = [P.sb(f"kpair{i}", [128, S], BF16, st_att) for i in range(2)]
    st_attn = ExitStack()
    attn = P.sb("attn", [128, NCH * S], BF16, st_attn, side="right")
    astage = P.sb("astage", [64, S], BF16, st_att)
    pbuf = P.sb("pbuf", [128, 2 * 1024], BF16, st_att)
    scrA = (P.sb("scrA0", [64, 512], BF16, st_att),
            P.sb("scrA1", [64, 512], BF16, st_att))
    rrow = [P.sb(f"rrow{i}", [1, 512], F32, st_att) for i in range(2)]

    mm2 = P.sb("mm2", [1, 4], F32, st_att)
    negb = [P.sb(f"negb{i}", [128, 1], F32, st_att) for i in range(2)]

    def pair_loads(j):
        bs = j % 2
        P.op("gpsimd", lambda e, j=j, bs=bs: e.dma_start(
            out=qwb[bs][:].rearrange("p (k m) -> p k m", m=128),
            in_=qwt_h[:, j * 128:(j + 1) * 128].rearrange(
                "(k p) m -> p k m", p=128)),
            writes=(f"qwb{bs}",), dma=True)
        P.op("gpsimd", lambda e, j=j, bs=bs: e.dma_start(
            out=kwb[bs][:].rearrange("p (k m) -> p k m", m=128),
            in_=kwt_h[:, j * 128:(j + 1) * 128].rearrange(
                "(k p) m -> p k m", p=128)),
            writes=(f"kwb{bs}",), dma=True)

    def pair_prod_items(j):
        bs = j % 2

        def proj(wb, wres, dst, dres, sc, bias, bres, half):
            def mm(slot, half=half, wb=wb):
                pst, sbase, pres = slot
                fns = []
                for nqg in range(2):
                    nq = half * 2 + nqg
                    for dc in range(NCH):
                        fns.append(lambda e, nq=nq, nqg=nqg, dc=dc, wb=wb,
                                   pst=pst, sbase=sbase: e.matmul(
                            pst[:, sbase + nqg * 512:
                                sbase + (nqg + 1) * 512],
                            wb[:, dc * 128:(dc + 1) * 128],
                            lnT[:, dc * S + nq * 512: dc * S + (nq + 1) * 512],
                            start=(dc == 0), stop=(dc == 7)))
                P.group("tensor", fns, reads=(wres, "lnT"), writes=pres)

            def post(slot, half=half, dst=dst):
                pst, sbase, pres = slot
                P.op("vector", lambda e, pst=pst, sbase=sbase, half=half:
                     e.tensor_scalar(dst[:, half * 1024:(half + 1) * 1024],
                                     pst[:, sbase:sbase + 1024], sc,
                                     bias[:, j:j + 1], ALU.mult, ALU.add),
                     reads=pres, sreads=(bres,), writes=(dres,))
            return {"mm": mm, "post": post}

        q0 = proj(qwb[bs], f"qwb{bs}", qpair_b[bs], f"qpair{bs}", 0.125,
                  qb8, "qb8", 0)
        q1 = proj(qwb[bs], f"qwb{bs}", qpair_b[bs], f"qpair{bs}", 0.125,
                  qb8, "qb8", 1)
        k0 = proj(kwb[bs], f"kwb{bs}", kpair_b[bs], f"kpair{bs}", 1.0,
                  kbp, "kbp", 0)
        k1 = proj(kwb[bs], f"kwb{bs}", kpair_b[bs], f"kpair{bs}", 1.0,
                  kbp, "kbp", 1)
        return [q0, k0, q1, k1]

    def pair_bounds(j):
        bs = j % 2
        qpair, kpair = qpair_b[bs], kpair_b[bs]
        qpr, kpr = f"qpair{bs}", f"kpair{bs}"
        bsq = pbuf
        BSQR = ("pbuf.s0", "pbuf.s1")
        for (srcp, sres, mcol) in ((qpair, qpr, 0), (kpair, kpr, 1)):
            P.op("vector", lambda e, srcp=srcp:
                 e.tensor_tensor(bsq[:], srcp[:], srcp[:], ALU.mult),
                 reads=(sres,), writes=BSQR)
            fns = []
            for nq in range(4):
                fns.append(lambda e, nq=nq: e.matmul(
                    psB[:, nq * 512:(nq + 1) * 512][0:1, :],
                    P.consts["ones_cb"][:],
                    bsq[:, nq * 512:(nq + 1) * 512],
                    start=True, stop=True))
            P.group("tensor", fns, reads=BSQR + ("c_ones_cb",), writes=PSB)
            P.op("vector", lambda e, mcol=mcol: e.reduce_max(
                mm2[:, mcol:mcol + 1], psB[:, 0:2048][0:1, :], axis=AX.X),
                reads=PSB, writes=("mm2",))
        P.op("vector", lambda e: e.tensor_tensor(
            mm2[:, 2:3], mm2[:, 0:1], mm2[:, 1:2], ALU.mult),
            reads=("mm2",), writes=("mm2",))
        P.op("scalar", lambda e: e.activation(
            mm2[:, 3:4], mm2[:, 2:3], AF.Sqrt),
            reads=("mm2",), writes=("mm2",))
        P.op("tensor", lambda e: e.matmul(
            psA[:, 0:1], P.consts["ones_f"][0:1, :],
            mm2[:, 3:4], start=True, stop=True),
            reads=("mm2", "c_ones_f"), writes=("psA.s0",))
        P.op("vector", lambda e, bs=bs: e.tensor_scalar(
            negb[bs][:], psA[:, 0:1], -1.0, None, ALU.mult),
            reads=("psA.s0",), writes=(f"negb{bs}",))

    pair_loads(0)
    for i, it in enumerate(pair_prod_items(0)):
        slot = (psA, (i % 2) * 1024, (f"psA.s{i % 2}",))
        it["mm"](slot)
        it["post"](slot)
    pair_bounds(0)
    for j in range(8):
        bs = j % 2
        qpair, kpair = qpair_b[bs], kpair_b[bs]
        qpr, kpr = f"qpair{bs}", f"kpair{bs}"
        if "alpha" in hw:
            for _ in range(12 if j < 7 else 14):
                hw["alpha"]["chunk"]()
        if j + 1 < 8:
            pair_loads(j + 1)
        for hh in range(2):
            h = 2 * j + hh
            base = hh * 64

            def qk_lhsT(kc, base=base, kpair=kpair):
                return kpair[base:base + 64, kc * 128:(kc + 1) * 128]

            def qk_rhs(lo, width, base=base, qpair=qpair):
                return qpair[base:base + 64, lo:lo + width]

            extra = pair_prod_items(j + 1) if (hh == 0 and j + 1 < 8) else ()
            emit_attention_head(
                P, h, qk_lhsT, qk_rhs, (qpr, kpr), v_km, attn,
                psA, psB, pbuf, causal=False, exp_scale=1.0,
                exp_bias=negb[bs][:], exp_bias_res=f"negb{bs}",
                clamp=None, mask0=None, scrA=scrA, rrow=rrow, astage=astage,
                extra_items=extra, evac_eng="vector")
            if hh == 0 and j + 1 < 8:
                pair_bounds(j + 1)

    st_att.close()
    P.full_barrier()
    st_lnT.close()
    P.full_barrier()
    st_o = ExitStack()
    owT = P.sb("owT", [128, 8 * D], BF16, st_o)
    P.op("gpsimd", lambda e: e.dma_start(
        out=owT[:].rearrange("p (k m) -> p k m", m=D),
        in_=owt_h.rearrange("(k p) m -> p k m", p=128)),
        writes=("owT",), dma=True)
    OBC = P.sb("OBC", [128, D], F32, st_o)
    emit_row_bcast(P, ob_h, "ob_h", OBC, "OBC", psB[:, 0:512], "psB.b0",
                   stack=st_o)
    pln = PostLN(P, "xs", xs, st_o, "po1",
                 nxt=hw.get("pre_stats", {}).get("lnd"))
    for i in range(NT):
        half = i % 2
        pres = f"psA.s{half}"
        fns = []
        for c in range(NCH):
            for n2 in range(2):
                fns.append(lambda e, i=i, c=c, n2=n2, half=half: e.matmul(
                    psA[:, half * 1024 + n2 * 512: half * 1024 + (n2 + 1) * 512],
                    attn[:, c * S + i * 128: c * S + (i + 1) * 128],
                    owT[:, c * D + n2 * 512: c * D + (n2 + 1) * 512],
                    start=(c == 0), stop=(c == 7)))
        P.group("tensor", fns, reads=("attn", "owT"), writes=(pres,))
        pln.chunk(i, psA[:, half * 1024:(half + 1) * 1024], pres,
                  bias_bcast=OBC, bias_res="OBC")
    st_o.close()
    P.full_barrier()
    st_attn.close()
    P.full_barrier()
    st_blk.close()
    P.full_barrier()



def mk_alpha(P, wupt_h, wdnt_h, st, st_ring):
    ainv_bc = [P.sb(f"ainv_bc{i}", [128, 1], F32, st, side="right")
               for i in range(2)]
    A = P.sb("alphaA", [128, 128], F32, st_ring, side="right")
    ring = [P.sb(f"al_wf{i}", [128, 512], F32, st_ring, side="right")
            for i in range(2)]
    idx = [0]

    def emit_abs(i):
        s = i % 2
        f = ring[s]
        rf = f"al_wf{s}"
        P.op("scalar", lambda e, f=f, i=i: e.activation(
            f[:], f[:], AF.Abs, accum_out=A[:, i:i + 1]),
            reads=(rf, "alphaA"), writes=(rf, "alphaA"))

    def emit_dma(i):
        s = i % 2
        f = ring[s]
        rf = f"al_wf{s}"
        if i < 64:
            c, q = divmod(i, 8)
            src = wupt_h[c * 128:(c + 1) * 128, q * 512:(q + 1) * 512]
        else:
            c, q = divmod(i - 64, 2)
            src = wdnt_h[c * 128:(c + 1) * 128, q * 512:(q + 1) * 512]
        P.op("scalar", lambda e, f=f, src=src: e.dma_start(
            out=f[:], in_=src), writes=(rf,), dma=True, dmaq="AL")

    def chunk():
        i = idx[0]
        if i >= 130:
            return
        idx[0] += 1
        if i >= 2:
            emit_abs(i - 2)
        if i < 128:
            emit_dma(i)

    return {"A": A, "ainv_bc": ainv_bc, "chunk": chunk, "idx": idx,
            "st_ring": st_ring}


def emit_block_ffn1(P, hw):
    xs = hw["xs"]
    wupt_h, wdnt_h = hw["wupt"], hw["wdnt"]
    identF = P.consts["identF"]
    wupq_hbm = P.nc.dram_tensor("wupq_scr", [D, DFF], BF16,
                                kind="Internal").ap()
    wdnq_hbm = P.nc.dram_tensor("wdnq_scr", [DFF, D], BF16,
                                kind="Internal").ap()
    r2_hbm = P.nc.dram_tensor("r2_scr", [512], F32, kind="Internal").ap()

    al = hw["alpha"]
    while al["idx"][0] < 130:
        al["chunk"]()
    ainv_bc = al["ainv_bc"]
    A = al["A"]
    st_fin = ExitStack()
    arow = P.sb("arow", [1, 128], F32, st_fin)
    atot = P.sb("atot", [1, 4], F32, st_fin)
    ps_a = P.ps("ps_a", [128, 512], F32, st_fin)

    def alpha_finish(wi, lo, n, rows, cols):
        P.group("tensor", [lambda e, lo=lo, n=n: e.matmul(
            ps_a[:, 0:n][0:1, :], P.consts["ones_cf"][:], A[:, lo:lo + n],
            start=True, stop=True)],
            reads=("alphaA", "c_ones_cf"), writes=("ps_a",))
        P.op("vector", lambda e, lo=lo, n=n: e.tensor_copy(
            arow[:, lo:lo + n], ps_a[:, 0:n][0:1, :]),
            reads=("ps_a",), writes=(f"arow{wi}",))
        P.op("vector", lambda e, lo=lo, n=n, wi=wi: e.reduce_sum(
            atot[:, wi:wi + 1], arow[:, lo:lo + n], axis=AX.X),
            reads=(f"arow{wi}",), writes=(f"atot{wi}",))
        P.op("vector", lambda e, wi=wi: e.reciprocal(
            atot[:, 2 + wi:3 + wi], atot[:, wi:wi + 1]),
            reads=(f"atot{wi}",), writes=(f"arcp{wi}",))
        P.op("vector", lambda e, wi=wi, rows=rows, cols=cols: e.tensor_scalar(
            atot[:, 2 + wi:3 + wi], atot[:, 2 + wi:3 + wi],
            float(rows * cols), None, ALU.mult),
            reads=(f"arcp{wi}",), writes=(f"arcp{wi}",))
        P.op("tensor", lambda e, wi=wi: e.matmul(
            ps_a[:, 0:1], P.consts["ones_f"][0:1, :], atot[:, 2 + wi:3 + wi],
            start=True, stop=True),
            reads=(f"arcp{wi}", "c_ones_f"), writes=("ps_a",))
        P.op("vector", lambda e, wi=wi: e.tensor_copy(
            ainv_bc[wi][:], ps_a[:, 0:1]),
            reads=("ps_a",), writes=(f"ainv_bc{wi}",))

    alpha_finish(0, 0, 64, D, DFF)
    alpha_finish(1, 64, 64, DFF, D)
    st_fin.close()
    al["st_ring"].close()
    P.full_barrier()

    st_blk = ExitStack()
    xnT = P.sb("xnT", [128, NCH * S], BF16, st_blk)

    st_ln = ExitStack()
    pool = hw.get("pre_stats")
    if pool:
        stats = pool["lnd"]
        emit_preln_stats(P, "xs", xs, stats, pre=True)
    else:
        stats = make_stats(P, st_ln, "lnd")
        emit_preln_stats(P, "xs", xs, stats)
    sv = P.sb("lnd_sv", [128, NT], F32, st_ln)
    scl = P.sb("lnd_scl", [128, NT], F32, st_ln)
    nms = P.sb("lnd_nms", [128, NT], F32, st_ln)
    P.op("scalar", lambda e: e.activation(sv[:], stats["var"][:], AF.Sqrt),
         reads=("lnd.var",), writes=("lnd.sv",))
    P.op("vector", lambda e: e.tensor_tensor(
        sv[:], sv[:], stats["rstd"][:], ALU.mult),
        reads=("lnd.sv", "lnd.rstd"), writes=("lnd.rms",))
    P.op("vector", lambda e: e.tensor_scalar(
        sv[:], sv[:], float(EPS_RMS), None, ALU.add),
        reads=("lnd.rms",), writes=("lnd.d2",))
    P.op("vector", lambda e: e.reciprocal(scl[:], sv[:]),
         reads=("lnd.d2",), writes=("lnd.f",))
    P.op("vector", lambda e: e.tensor_tensor(
        scl[:], stats["rstd"][:], scl[:], ALU.mult),
        reads=("lnd.f", "lnd.rstd"), writes=("lnd.scl",))
    P.op("vector", lambda e: e.tensor_tensor(
        nms[:], stats["mean"][:], scl[:], ALU.mult),
        reads=("lnd.scl", "lnd.mean"), writes=("lnd.nms",))
    P.op("vector", lambda e: e.tensor_scalar(
        nms[:], nms[:], -1.0, None, ALU.mult),
        reads=("lnd.nms",), writes=("lnd.nms",))
    def xn_apply(t, dst, dres):
        P.op("scalar", lambda e, t=t, dst=dst: e.activation(
            dst[:], xs[:, t * D:(t + 1) * D], AF.Identity,
            bias=nms[:, t:t + 1], scale=scl[:, t:t + 1]),
            reads=("xs",), sreads=("lnd.nms", "lnd.scl"), writes=(dres,))

    tp_stack = ExitStack()
    emit_ln_transpose(P, xn_apply, xnT, "xnT", tp_stack)
    tp_stack.close()
    P.full_barrier()
    st_ln.close()
    P.full_barrier()

    st_mm = ExitStack()
    hbuf = P.sb("hbuf", [128, 32 * 512], BF16, st_mm)
    sqd = [P.sb(f"sqd{i}", [128, 512], F32, st_mm) for i in range(2)]
    r2row = P.sb("r2row", [1, 512], F32, st_mm)
    rstd2T = P.sb("rstd2T", [128, 4], F32, st_mm)
    pln = PostLN(P, "xs", xs, st_mm, "pod")

    def up_evac(hc, psum_ap, psum_res):
        P.op("scalar", lambda e, psum_ap=psum_ap, hc=hc: e.activation(
            hbuf[:, hc * 512:(hc + 1) * 512], psum_ap, AF.Gelu),
            reads=(psum_res,), writes=(f"hbuf.{hc}",))

    phase = {}

    def rms_mm(hc):
        ps_dn0 = phase["ps_dn0"]
        sq = sqd[hc % 2]
        rq = f"sqd{hc % 2}"
        P.op("tensor", lambda e, sq=sq, hc=hc, ps_dn0=ps_dn0: e.matmul(
            ps_dn0[:, 0:512][0:1, :], P.consts["ones_cf"][:], sq[:],
            start=(hc == 0), stop=(hc == 31)),
            reads=(rq, "c_ones_cf"), writes=("ps_dn0",))

    def after_up(hc):
        if hc > 0:
            rms_mm(hc - 1)
        sq = sqd[hc % 2]
        rq = f"sqd{hc % 2}"
        P.op("scalar", lambda e, sq=sq, hc=hc: e.activation(
            sq[:], hbuf[:, hc * 512:(hc + 1) * 512], AF.Square),
            reads=(f"hbuf.{hc}",), writes=(rq,))

    def down_pre(jt, _ps=[None]):
        rms_mm(31)
        ps_dn0 = phase["ps_dn0"]
        P.op("vector", lambda e: e.tensor_scalar(
            r2row[:], ps_dn0[:, 0:512][0:1, :], float(1.0 / DFF), None,
            ALU.mult), reads=("ps_dn0",), writes=("r2row",))
        P.op("scalar", lambda e: e.activation(
            r2row[:], r2row[:], AF.Sqrt), reads=("r2row",), writes=("r2row",))
        P.op("vector", lambda e: e.tensor_scalar(
            r2row[:], r2row[:], float(EPS_RMS), None, ALU.add),
            reads=("r2row",), writes=("r2row",))
        P.op("vector", lambda e: e.reciprocal(r2row[:], r2row[:]),
             reads=("r2row",), writes=("r2row",))

        for qc in range(4):
            P.op("tensor", lambda e, qc=qc: e.matmul(
                phase["ps_dn1"][:, 0:1], r2row[0:1, qc * 128:(qc + 1) * 128],
                P.consts["ones_f"][0:1, 0:1], start=True, stop=True),
                reads=("r2row", "c_ones_f"), writes=("ps_dn1",))
            P.op("vector", lambda e, qc=qc: e.tensor_copy(
                rstd2T[:, qc:qc + 1], phase["ps_dn1"][:, 0:1]),
                reads=("ps_dn1",), writes=("rstd2T",))
        return lambda qc: (rstd2T[:, qc:qc + 1], "rstd2T")

    emit_ffn_matmul_phase(P, xnT, "xnT", hbuf, wupq_hbm, wdnq_hbm, st_mm,
                          up_evac, down_pre, pln, phase_out=phase,
                          w1_src=wupt_h, w2_src=wdnt_h,
                          quant=(mk_ternary(P, ainv_bc[0][:], "ainv_bc0"),
                                 mk_ternary(P, ainv_bc[1][:], "ainv_bc1")),
                          after_up=after_up)
    st_mm.close()
    P.full_barrier()
    st_blk.close()
    P.full_barrier()



def build_fused():
    P = Prog()
    hw = {}
    hw["x"] = P.dram_in("x", [S, D], F32)
    for n, shp in (("w_dq", [D, DLAT]), ("w_uq", [DLAT, D]),
                   ("w_qr", [DLAT, 512]), ("w_dkv", [D, DLAT]),
                   ("w_uk", [DLAT, D]), ("w_uv", [DLAT, D]),
                   ("w_kr", [D, DHR]), ("w_o", [D, D]),
                   ("w1t", [D, DFF]), ("b1", [DFF]), ("w2t", [DFF, D]),
                   ("b2", [D]), ("qwt", [D, D]), ("kwt", [D, D]),
                   ("vwt", [D, D]), ("owt", [D, D]), ("qb", [D]), ("kb", [D]),
                   ("vb", [D]), ("ob", [D]), ("wupt", [D, DFF]),
                   ("wdnt", [DFF, D]), ("identF", [128, 128])):
        hw[n] = P.dram_in(n, shp, F32)
    for n, shp in (("identB", [128, 128]), ("cos32", [32, S]),
                   ("sig32", [32, S]), ("swap32", [32, 32]),
                   ("mask0", [128, 128])):
        hw[n] = P.dram_in(n, shp, BF16)
    out_h = P.dram_out("xout", [S, D], F32)

    identB = P.sb("s_identB", [128, 128], BF16)
    identF = P.sb("s_identF", [128, 128], F32)
    P.consts["identB"] = identB
    P.consts["identF"] = identF
    emit_consts(P, [("eps_ln", [128, 1], F32, EPS_LN),
                    ("ones_f", [1, 128], F32, 1.0),
                    ("ones_cb", [128, 1], BF16, 1.0),
                    ("ones_cf", [128, 1], F32, 1.0)])
    P.op("sync", lambda e: e.dma_start(out=identB[:], in_=hw["identB"]),
         writes=("c_identB",), dma=True)
    P.op("sync", lambda e: e.dma_start(out=identF[:], in_=hw["identF"]),
         writes=("c_identF",), dma=True)

    xs = P.sb("xs", [128, NT * D], F32, side="right")
    hw["xs"] = xs
    hw["pre_stats"] = {t: make_stats(P, None, t, with_scratch=False)
                       for t in ("lnq", "ln1", "lnd")}
    emit_load_x(P, hw["x"], xs)

    emit_block_l0(P, hw)
    emit_block_ffn0(P, hw)
    st_alpha = ExitStack()
    st_ring = ExitStack()
    hw["alpha"] = mk_alpha(P, hw["wupt"], hw["wdnt"], st_alpha, st_ring)
    emit_block_l1(P, hw)
    emit_block_ffn1(P, hw)
    st_alpha.close()
    P.full_barrier(include_bg=True)
    emit_store_x(P, "xs", xs, out_h)
    return P.finalize()


_BUILD_CACHE = {}


def _get(builder, name):
    if name not in _BUILD_CACHE:
        _BUILD_CACHE[name] = builder()
    return _BUILD_CACHE[name]


def kernel(**inputs):
    f32 = lambda a: np.ascontiguousarray(np.asarray(a, dtype=np.float32))
    x = f32(inputs["L0_attn_x"])
    _ = inputs.get("L1_attn_mask")
    cos32, sig32, swap32 = _rope_tables()
    identB = np.ascontiguousarray(np.eye(128, dtype=BF))
    identF = np.ascontiguousarray(np.eye(128, dtype=np.float32))
    mask0 = _mask_tri()

    w_qr = f32(inputs["L0_attn_W_QR"])[:, _qr_perm()]
    w_kr = f32(inputs["L0_attn_W_KR"])[:, _kr_perm()]

    common = dict(
        w_dq=f32(inputs["L0_attn_W_DQ"]), w_uq=f32(inputs["L0_attn_W_UQ"]),
        w_qr=np.ascontiguousarray(w_qr), w_dkv=f32(inputs["L0_attn_W_DKV"]),
        w_uk=f32(inputs["L0_attn_W_UK"]), w_uv=f32(inputs["L0_attn_W_UV"]),
        w_kr=np.ascontiguousarray(w_kr), w_o=f32(inputs["L0_attn_W_O"]),
        identB=identB, cos32=cos32, sig32=sig32, swap32=swap32, mask0=mask0,
        w1t=np.ascontiguousarray(f32(inputs["L0_ffn_w1"]).T),
        b1=f32(inputs["L0_ffn_b1"]),
        w2t=np.ascontiguousarray(f32(inputs["L0_ffn_w2"]).T),
        b2=f32(inputs["L0_ffn_b2"]), identF=identF,
        qwt=np.ascontiguousarray(f32(inputs["L1_attn_q_w"]).T),
        kwt=np.ascontiguousarray(f32(inputs["L1_attn_k_w"]).T),
        vwt=np.ascontiguousarray(f32(inputs["L1_attn_v_w"]).T),
        owt=np.ascontiguousarray(f32(inputs["L1_attn_out_w"]).T),
        qb=f32(inputs["L1_attn_q_b"]), kb=f32(inputs["L1_attn_k_b"]),
        vb=f32(inputs["L1_attn_v_b"]), ob=f32(inputs["L1_attn_out_b"]),
        wupt=np.ascontiguousarray(f32(inputs["L1_ffn_w_up"]).T),
        wdnt=np.ascontiguousarray(f32(inputs["L1_ffn_w_down"]).T))

    nc = _get(build_fused, "fused")
    in_maps = [dict(x=np.ascontiguousarray(x[b]), **common) for b in range(8)]
    res = run_bass_kernel_spmd(nc, in_maps, core_ids=list(range(8)))
    out = np.stack([r["xout"] for r in res.results], axis=0)
    return np.ascontiguousarray(out.astype(np.float32))


# version 24
# speedup vs baseline: 1.0121x; 1.0064x over previous
import math
from contextlib import ExitStack

import numpy as np
import ml_dtypes

import concourse.bass as bass
import concourse.mybir as mybir
from concourse.bass_utils import run_bass_kernel_spmd

F32 = mybir.dt.float32
BF16 = mybir.dt.bfloat16
I8 = mybir.dt.int8
AF = mybir.ActivationFunctionType
ALU = mybir.AluOpType
AX = mybir.AxisListType

S = 2048
D = 1024
NT = 16
NCH = 8
H = 16
DHR = 32
DLAT = 256
DFF = 4096
EPS_LN = 1e-5
EPS_RMS = 1e-8
BF = ml_dtypes.bfloat16

ENGS = ("sync", "gpsimd", "tensor", "vector", "scalar")
QSEM = {"sync": "LD", "gpsimd": "G", "tensor": "T", "vector": "V",
        "scalar": "Sc"}


class Prog:
    def __init__(self):
        self.nc = bass.Bass("TRN2", target_bir_lowering=False)
        self.ops = {e: [] for e in ENGS}
        self.ctx = ExitStack()
        self.sems = {}
        for name in ("T", "V", "Sc", "G"):
            self.sems[name] = [self.ctx.enter_context(self.nc.semaphore(name)), 0]
        self.NSLOT = 4
        for q in ("LD", "GD", "ST", "AL"):
            for i in range(self.NSLOT):
                n = f"{q}{i}"
                self.sems[n] = [self.ctx.enter_context(self.nc.semaphore(n)), 0]
        self.dma_idx = {"LD": 0, "GD": 0, "ST": 0, "AL": 0}
        self.consts = {}
        self.suppress_same_engine = False
        self._names = {}
        self.last_write = {}
        self.readers = {}

    def _uniq(self, name):
        n = self._names.get(name, 0)
        self._names[name] = n + 1
        return name if n == 0 else f"{name}_{n}"

    def sb(self, name, shape, dt, stack=None, side=None):
        return (stack or self.ctx).enter_context(
            self.nc.sbuf_tensor(self._uniq(name), shape, dt, side=side))

    def ps(self, name, shape, dt, stack=None):
        return (stack or self.ctx).enter_context(
            self.nc.psum_tensor(self._uniq(name), shape, dt))

    def dram_in(self, name, shape, dt):
        return self.nc.dram_tensor(name, shape, dt, kind="ExternalInput").ap()

    def dram_out(self, name, shape, dt):
        return self.nc.dram_tensor(name, shape, dt, kind="ExternalOutput").ap()

    def _compute_waits(self, mysem, reads, writes, sreads):
        need = {}

        def add(dep):
            if dep is None:
                return
            sn, v = dep
            if need.get(sn, 0) < v:
                need[sn] = v

        sup = getattr(self, "suppress_same_engine", True)
        for r in tuple(reads) + tuple(writes):
            lw = self.last_write.get(r)
            if lw is not None and (not sup or lw[0] != mysem):
                add(lw)
        for r in tuple(sreads):
            add(self.last_write.get(r))
        for r in tuple(writes):
            for sn, v in self.readers.get(r, {}).items():
                if not sup or sn != mysem:
                    add((sn, v))
        return sorted(need.items())

    def _record(self, mysem, count, reads, writes, sreads):
        for r in tuple(reads) + tuple(sreads):
            self.readers.setdefault(r, {})[mysem] = count
        for r in tuple(writes):
            self.last_write[r] = (mysem, count)
            self.readers[r] = {}

    def op(self, eng, f, reads=(), writes=(), sreads=(), dma=False,
           extra=(), dmaq=None):
        if dma:
            q = dmaq or {"sync": "LD", "gpsimd": "GD", "scalar": "ST"}[eng]
            slot = self.dma_idx[q] % self.NSLOT
            self.dma_idx[q] += 1
            sem = f"{q}{slot}"
            incv = 16
            gate = (sem, self.sems[sem][1])
        else:
            sem = QSEM[eng]
            incv = 1
            gate = None
        waits = self._compute_waits(sem if not dma else "__dma__",
                                    reads, writes, sreads)
        if gate is not None and gate[1] > 0:
            waits = list(waits) + [gate]
        for sn, v in extra:
            waits = list(waits) + [(sn, v)]
        self.sems[sem][1] += incv
        count = self.sems[sem][1]
        sems = self.sems

        def run(e, f=f, waits=tuple(waits), sem=sem, incv=incv):
            for sn, v in waits:
                e.wait_ge(sems[sn][0], v)
            f(e).then_inc(sems[sem][0], incv)

        self.ops[eng].append(run)
        self._record(sem, count, reads, writes, sreads)
        return (sem, count)

    def group(self, eng, fns, reads=(), writes=(), sreads=()):
        sem = QSEM[eng]
        waits = self._compute_waits(sem, reads, writes, sreads)
        self.sems[sem][1] += 1
        count = self.sems[sem][1]
        sems = self.sems

        def run(e, fns=tuple(fns), waits=tuple(waits), sem=sem):
            for sn, v in waits:
                e.wait_ge(sems[sn][0], v)
            for f in fns[:-1]:
                f(e)
            fns[-1](e).then_inc(sems[sem][0], 1)

        self.ops[eng].append(run)
        self._record(sem, count, reads, writes, sreads)
        return (sem, count)

    def barrier_wait(self, eng, deps):
        sems = self.sems
        wl = tuple(d for d in deps if d is not None)

        def run(e, wl=wl):
            for sn, v in wl:
                e.wait_ge(sems[sn][0], v)

        self.ops[eng].append(run)

    def dep(self, res):
        return self.last_write.get(res)

    def full_barrier(self, include_bg=False):
        deps = [(n, s[1]) for n, s in self.sems.items()
                if s[1] > 0 and (include_bg or not n.startswith("AL"))]
        for eng in ENGS:
            self.barrier_wait(eng, deps)

    def finalize(self):
        with self.nc.Block() as block:
            for eng in ENGS:
                fns = self.ops[eng]
                getattr(block, eng)(lambda e, fns=fns: [f(e) for f in fns])
        self.ctx.close()
        return self.nc



def _rope_tables():
    j = np.arange(16, dtype=np.float64)
    inv = 10000.0 ** (-j / 16.0)
    t = np.arange(S, dtype=np.float64)
    ang = inv[:, None] * t[None, :]
    cos = np.cos(ang)
    sin = np.sin(ang)
    cos32 = np.ascontiguousarray(np.concatenate([cos, cos], 0).astype(BF))
    sig32 = np.ascontiguousarray(np.concatenate([-sin, sin], 0).astype(BF))
    P = np.zeros((32, 32), dtype=BF)
    for k in range(16):
        P[16 + k, k] = 1.0
        P[k, 16 + k] = 1.0
    return cos32, sig32, P


def _qr_perm():
    perm = []
    for h in range(H):
        perm += [h * 32 + 2 * k for k in range(16)]
        perm += [h * 32 + 2 * k + 1 for k in range(16)]
    return np.array(perm)


def _kr_perm():
    return np.array([2 * k for k in range(16)] + [2 * k + 1 for k in range(16)])


def _mask_tri():
    k = np.arange(128)[:, None]
    q = np.arange(128)[None, :]
    return np.ascontiguousarray((q >= k).astype(BF))


def _pieces(lo, hi, grid=512):
    out = []
    while lo < hi:
        n = min(hi - lo, grid - (lo % grid))
        out.append((lo, n))
        lo += n
    return out



def emit_consts(P, specs):
    for name, shape, dt, valv in specs:
        t = P.sb("c_" + name, shape, dt)
        P.op("vector", lambda e, t=t, v=valv: e.memset(t[:], v),
             writes=("c_" + name,))
        P.consts[name] = t


def emit_load_x(P, x_h, xs, res="xs"):
    P.op("sync", lambda e: e.dma_start(
        out=xs[:].rearrange("p (t d) -> p t d", d=D),
        in_=x_h.rearrange("(t p) d -> p t d", p=128)),
        writes=(res,), dma=True)


def make_stats(P, stack, tag, with_scratch=True):
    d = {"tag": tag}
    for n in ("sum", "ssq", "mean", "var", "m2", "sqv", "rstd", "nmr", "r2"):
        d[n] = P.sb(f"{tag}_{n}", [128, NT], F32, stack)
    if with_scratch:
        d["sq0"] = P.sb(f"{tag}_sq0", [128, D], F32, stack)
        d["sq1"] = P.sb(f"{tag}_sq1", [128, D], F32, stack)
    return d


def emit_preln_stats(P, xs_res, xs, st, pre=False):
    tg = st["tag"]
    if not pre:
        P.op("vector", lambda e: e.reduce_sum(
            st["sum"][:], xs[:].rearrange("p (t d) -> p t d", d=D), axis=AX.X),
            reads=(xs_res,), writes=(tg + ".sum",))
        for t in range(NT):
            scr = st["sq0"] if t % 2 == 0 else st["sq1"]
            r = tg + (".sq0" if t % 2 == 0 else ".sq1")
            P.op("scalar", lambda e, t=t, scr=scr: e.activation(
                scr[:], xs[:, t * D:(t + 1) * D], AF.Square),
                reads=(xs_res,), writes=(r,))
            P.op("vector", lambda e, t=t, scr=scr: e.reduce_sum(
                st["ssq"][:, t:t + 1], scr[:], axis=AX.X),
                reads=(r,), writes=(tg + f".ssq{t}",))
    P.op("vector", lambda e: e.tensor_scalar(
        st["mean"][:], st["sum"][:], 1.0 / D, None, ALU.mult),
        reads=(tg + ".sum",), writes=(tg + ".mean",))
    P.op("vector", lambda e: e.tensor_scalar(
        st["var"][:], st["ssq"][:], 1.0 / D, None, ALU.mult),
        reads=tuple(tg + f".ssq{t}" for t in range(NT)),
        writes=(tg + ".var",))
    P.op("vector", lambda e: e.tensor_tensor(
        st["m2"][:], st["mean"][:], st["mean"][:], ALU.mult),
        reads=(tg + ".mean",), writes=(tg + ".m2",))
    P.op("vector", lambda e: e.tensor_tensor(
        st["var"][:], st["var"][:], st["m2"][:], ALU.subtract),
        reads=(tg + ".var", tg + ".m2"), writes=(tg + ".var",))
    P.op("scalar", lambda e: e.activation(
        st["sqv"][:], st["var"][:], AF.Sqrt, bias=P.consts["eps_ln"][:]),
        reads=(tg + ".var",), sreads=("c_eps_ln",), writes=(tg + ".sqv",))
    P.op("vector", lambda e: e.reciprocal(st["rstd"][:], st["sqv"][:]),
         reads=(tg + ".sqv",), writes=(tg + ".rstd",))
    P.op("vector", lambda e: e.tensor_tensor(
        st["nmr"][:], st["mean"][:], st["rstd"][:], ALU.mult),
        reads=(tg + ".mean", tg + ".rstd"), writes=(tg + ".nmr",))
    P.op("vector", lambda e: e.tensor_scalar(
        st["nmr"][:], st["nmr"][:], -1.0, None, ALU.mult),
        reads=(tg + ".nmr",), writes=(tg + ".nmr",))


def mk_ln_apply(P, xs_res, xs, st, bias_key="nmr", scale_key="rstd"):
    tg = st["tag"]

    def apply_tile(t, dst, dres):
        P.op("scalar", lambda e, t=t, dst=dst: e.activation(
            dst[:], xs[:, t * D:(t + 1) * D], AF.Identity,
            bias=st[bias_key][:, t:t + 1], scale=st[scale_key][:, t:t + 1]),
            reads=(xs_res,), sreads=(tg + "." + bias_key, tg + "." + scale_key),
            writes=(dres,))

    return apply_tile


def emit_ln_transpose(P, apply_tile, fmT, fm_res, tp_stack):
    pt = [P.ps(f"tp_pt{i}", [128, 1024], BF16, tp_stack) for i in range(2)]
    lnr = [P.sb(f"tp_ln{i}", [128, D], BF16, tp_stack) for i in range(2)]
    identB = P.consts["identB"]
    for t in range(NT):
        s = t % 2
        lt, lres = lnr[s], f"tp_ln{s}"
        pp, pres = pt[s], f"tp_pt{s}"
        apply_tile(t, lt, lres)
        fns = []
        for c in range(NCH):
            fns.append(lambda e, c=c, lt=lt, pp=pp: e.transpose(
                pp[:, c * 128:(c + 1) * 128],
                lt[:, c * 128:(c + 1) * 128], identB[:]))
        P.group("tensor", fns, reads=(lres, "c_identB"), writes=(pres,))
        P.op("vector", lambda e, t=t, pp=pp: e.tensor_copy(
            fmT[:].rearrange("p (c k) -> p c k", k=S)[:, :, t * 128:(t + 1) * 128],
            pp[:].rearrange("p (c k) -> p c k", k=128)),
            reads=(pres,), writes=(fm_res,))


def emit_store_x(P, xs_res, xs, out_h):
    P.op("scalar", lambda e: e.dma_start(
        out=out_h.rearrange("(t p) d -> p t d", p=128),
        in_=xs[:].rearrange("p (t d) -> p t d", d=D)),
        reads=(xs_res,), dma=True)
    dep = P.dep_st()
    P.barrier_wait("sync", dep)
    P.barrier_wait("tensor", dep)
    P.barrier_wait("vector", dep)


def _dep_st(self):
    return [(f"ST{i}", self.sems[f"ST{i}"][1]) for i in range(self.NSLOT)
            if self.sems[f"ST{i}"][1] > 0]


Prog.dep_st = _dep_st


class PostLN:

    def __init__(self, P, xs_res, xs, stack, tag, nxt=None):
        self.P, self.xs, self.xs_res = P, xs, xs_res
        self.st = make_stats(P, stack, tag)
        self.o = [P.sb(tag + "_o0", [128, D], F32, stack),
                  P.sb(tag + "_o1", [128, D], F32, stack)]
        self.tag = tag
        self.nxt = nxt

    def chunk(self, i, psum_ap, psum_res, bias_bcast=None, bias_res=None,
              scale=None, scale_res=None):
        P, st, tg = self.P, self.st, self.tag
        if isinstance(psum_res, str):
            psum_res = (psum_res,)
        o = self.o[i % 2]
        ores = f"{tg}.o{i % 2}"
        if bias_bcast is not None:
            P.op("vector", lambda e: e.tensor_tensor(
                o[:], psum_ap, bias_bcast[:], ALU.add),
                reads=psum_res + (bias_res,), writes=(ores,))
        elif scale is not None:
            P.op("scalar", lambda e: e.activation(
                o[:], psum_ap, AF.Identity, scale=scale),
                reads=psum_res, sreads=(scale_res,), writes=(ores,))
        else:
            P.op("vector", lambda e: e.tensor_copy(o[:], psum_ap),
                 reads=psum_res, writes=(ores,))
        P.op("vector", lambda e, i=i: e.reduce_sum(
            st["sum"][:, i:i + 1], o[:], axis=AX.X),
            reads=(ores,), writes=(tg + ".sum",))
        P.op("scalar", lambda e: e.activation(
            st["sq0"][:], o[:], AF.Square),
            reads=(ores,), writes=(tg + ".sq0",))
        P.op("vector", lambda e, i=i: e.reduce_sum(
            st["ssq"][:, i:i + 1], st["sq0"][:], axis=AX.X),
            reads=(tg + ".sq0",), writes=(tg + ".ssq",))
        P.op("vector", lambda e, i=i: e.tensor_scalar(
            st["mean"][:, i:i + 1], st["sum"][:, i:i + 1], 1.0 / D, None,
            ALU.mult), reads=(tg + ".sum",), writes=(tg + ".mean",))
        P.op("vector", lambda e, i=i: e.tensor_scalar(
            st["var"][:, i:i + 1], st["ssq"][:, i:i + 1], 1.0 / D, None,
            ALU.mult), reads=(tg + ".ssq",), writes=(tg + ".var",))
        P.op("vector", lambda e, i=i: e.tensor_tensor(
            st["m2"][:, i:i + 1], st["mean"][:, i:i + 1],
            st["mean"][:, i:i + 1], ALU.mult),
            reads=(tg + ".mean",), writes=(tg + ".m2",))
        P.op("vector", lambda e, i=i: e.tensor_tensor(
            st["var"][:, i:i + 1], st["var"][:, i:i + 1],
            st["m2"][:, i:i + 1], ALU.subtract),
            reads=(tg + ".var", tg + ".m2"), writes=(tg + ".var",))
        P.op("scalar", lambda e, i=i: e.activation(
            st["sqv"][:, i:i + 1], st["var"][:, i:i + 1], AF.Sqrt,
            bias=P.consts["eps_ln"][:]),
            reads=(tg + ".var",), sreads=("c_eps_ln",), writes=(tg + ".sqv",))
        P.op("vector", lambda e, i=i: e.reciprocal(
            st["rstd"][:, i:i + 1], st["sqv"][:, i:i + 1]),
            reads=(tg + ".sqv",), writes=(tg + ".rstd",))
        P.op("vector", lambda e, o=o, i=i: e.tensor_scalar(
            st["sq1"][:], o[:], st["mean"][:, i:i + 1],
            st["rstd"][:, i:i + 1], ALU.subtract, ALU.mult),
            reads=(ores,), sreads=(tg + ".mean", tg + ".rstd"),
            writes=(tg + ".sq1",))
        P.op("vector", lambda e, i=i: e.tensor_tensor(
            self.xs[:, i * D:(i + 1) * D], self.xs[:, i * D:(i + 1) * D],
            st["sq1"][:], ALU.add),
            reads=(tg + ".sq1", self.xs_res), writes=(self.xs_res,))
        if self.nxt is not None:
            nx, ng = self.nxt, self.nxt["tag"]
            P.op("vector", lambda e, i=i: e.reduce_sum(
                nx["sum"][:, i:i + 1], self.xs[:, i * D:(i + 1) * D],
                axis=AX.X), reads=(self.xs_res,), writes=(ng + ".sum",))
            P.op("scalar", lambda e, i=i: e.activation(
                st["sq0"][:], self.xs[:, i * D:(i + 1) * D], AF.Square,
                accum_out=nx["ssq"][:, i:i + 1]),
                reads=(self.xs_res,),
                writes=(tg + ".sq0", ng + f".ssq{i}"))



def emit_attention_head(P, h, qk_lhsT, qk_rhs, qk_res, v_km, attn, psA, psB,
                        pbuf, causal, exp_scale, exp_bias, exp_bias_res,
                        clamp, mask0, scrA, rrow, astage, extra_items=(),
                        evac_eng="scalar"):
    items = []
    half_last = {}
    for half in range(2):
        for kc in range(NT):
            q_lo, q_hi = half * 1024, half * 1024 + 1024
            q_start = max(q_lo, kc * 128) if causal else q_lo
            if q_start < q_hi:
                items.append((kc, half, q_start, q_hi))
                half_last[half] = len(items) - 1
    N = len(items)
    ex = list(extra_items)
    M = N + len(ex)

    def slot_of(n):
        s = n % 3
        half = items[n][1] if n < N else 1
        unsafe = (n < N and half == 1 and n < half_last[0] + 7)
        if s == 2 and unsafe:
            s = n % 2
        if s < 2:
            return psA, s * 1024, (f"psA.s{s}",)
        w = (1 - half) * 1024
        return psB, w, (f"psB.b{2 * (1 - half)}", f"psB.b{2 * (1 - half) + 1}")

    def emit_scores(n):
        kc, half, q_start, q_hi = items[n]
        pst, sbase, pres = slot_of(n)
        base = sbase + (q_start - half * 1024)
        end = sbase + 1024
        fns = []
        for (lo, nn) in _pieces(base, end):
            gq = half * 1024 + (lo - sbase)
            fns.append(lambda e, kc=kc, lo=lo, nn=nn, gq=gq, pst=pst: e.matmul(
                pst[:, lo:lo + nn], qk_lhsT(kc), qk_rhs(gq, nn),
                start=True, stop=True))
        P.group("tensor", fns, reads=qk_res(kc, half), writes=pres)

    def emit_expmask(n):
        kc, half, q_start, q_hi = items[n]
        pst, sbase, pres = slot_of(n)
        base = sbase + (q_start - half * 1024)
        end = sbase + 1024
        pslot = n % 2
        pbres = f"pbuf.s{pslot}"
        pb = pbuf[:, pslot * 1024 + (q_start - half * 1024): pslot * 1024 + 1024]
        sc_ap = pst[:, base:end]
        kw = {"scale": exp_scale}
        sr = ()
        if exp_bias is not None:
            kw["bias"] = exp_bias
            sr = (exp_bias_res,)
        P.op("scalar", lambda e, sc_ap=sc_ap, pb=pb, kw=kw: e.activation(
            pb, sc_ap, AF.Exp, **kw),
            reads=pres, sreads=sr, writes=(pbres,))
        if clamp is not None:
            P.op("vector", lambda e, pb=pb: e.tensor_scalar(
                pb, pb, clamp[1], clamp[0], ALU.min, ALU.max),
                reads=(pbres,), writes=(pbres,))
        if causal:
            dlo = kc * 128
            if q_start <= dlo < q_hi:
                loc = dlo - q_start
                P.op("vector", lambda e, pb=pb, loc=loc: e.tensor_tensor(
                    pb[:, loc:loc + 128], pb[:, loc:loc + 128], mask0[:],
                    ALU.mult), reads=(pbres, "c_mask0"), writes=(pbres,))

    def emit_pv(n):
        kc, half, q_start, q_hi = items[n]
        pslot = n % 2
        pbres = f"pbuf.s{pslot}"
        pb = pbuf[:, pslot * 1024 + (q_start - half * 1024): pslot * 1024 + 1024]
        fns = []
        wr = []
        for (gq, nn) in _pieces(q_start, q_hi):
            blk = gq // 512
            if causal:
                firstkc = 0
                lastkc = min(15, (blk * 512 + 511) // 128)
            else:
                firstkc, lastkc = 0, 15
            wr.append(f"psB.b{blk}")
            fns.append(lambda e, kc=kc, gq=gq, nn=nn, h=h, pb=pb,
                       q_start=q_start, lastkc=lastkc: e.matmul(
                           psB[:, gq:gq + nn][0:65, :],
                           v_km[:, kc * (H * 65) + h * 65:
                                kc * (H * 65) + h * 65 + 65],
                           pb[:, gq - q_start: gq - q_start + nn],
                           start=(kc == 0), stop=(kc == lastkc)))
        P.group("tensor", fns, reads=(pbres, "v_km"), writes=tuple(wr))

    odd = (h % 2 == 1)

    def emit_norm_s1(b):
        bres = f"psB.b{b}"
        rr = rrow[b % 2]
        P.op("vector", lambda e, b=b, rr=rr: e.reciprocal(
            rr[:], psB[64:65, b * 512:(b + 1) * 512]),
            reads=(bres,), writes=(f"rrow{b % 2}",))
        scr = scrA[b % 2]
        sres = f"scrA{b % 2}"
        if evac_eng == "scalar":
            P.op("scalar", lambda e, b=b, scr=scr: e.activation(
                scr[:], psB[0:64, b * 512:(b + 1) * 512], AF.Identity),
                reads=(bres,), writes=(sres,))
        else:
            P.op("vector", lambda e, b=b, scr=scr: e.tensor_copy(
                scr[:], psB[0:64, b * 512:(b + 1) * 512]),
                reads=(bres,), writes=(sres,))

    def emit_norm_s2(b):
        bres = f"psB.b{b}"
        rr = rrow[b % 2]
        scr = scrA[b % 2]
        sres = f"scrA{b % 2}"
        P.op("tensor", lambda e, b=b, rr=rr: e.matmul(
            psB[0:64, b * 512:(b + 1) * 512], P.consts["ones_f"][0:1, 0:64],
            rr[:], start=True, stop=True),
            reads=(f"rrow{b % 2}", "c_ones_f", sres), writes=(bres,))
        col0 = (h // 2) * S + b * 512
        if not odd:
            P.op("vector", lambda e, scr=scr, col0=col0, b=b:
                 e.tensor_tensor(attn[0:64, col0:col0 + 512],
                                 scr[:], psB[0:64, b * 512:(b + 1) * 512],
                                 ALU.mult),
                 reads=(sres, bres), writes=("attn",))
        else:
            P.op("vector", lambda e, scr=scr, b=b:
                 e.tensor_tensor(astage[:, b * 512:(b + 1) * 512],
                                 scr[:], psB[0:64, b * 512:(b + 1) * 512],
                                 ALU.mult),
                 reads=(sres, bres), writes=("astage",))

    norm_s1q = []
    norm_s2q = []

    def norm_tick(n):
        if norm_s2q and norm_s2q[0][0] <= n:
            emit_norm_s2(norm_s2q.pop(0)[1])
        if norm_s1q:
            b = norm_s1q.pop(0)
            emit_norm_s1(b)
            norm_s2q.append((n + 2, b))

    for n in range(M + 2):
        if 1 <= n < M + 1:
            m = n - 1
            if m < N:
                emit_expmask(m)
            else:
                ex[m - N]["post"](slot_of(m))
        if n < M:
            if n < N:
                emit_scores(n)
            else:
                ex[n - N]["mm"](slot_of(n))
        if n >= 2:
            if n - 2 < N:
                emit_pv(n - 2)
            norm_tick(n)
            if n - 2 == half_last.get(0):
                norm_s1q += [0, 1]
            if n - 2 == half_last.get(1):
                norm_s1q += [2, 3]
    for b in norm_s1q:
        emit_norm_s1(b)
        norm_s2q.append((0, b))
    for _, b in norm_s2q:
        emit_norm_s2(b)
    if odd:
        col0 = (h // 2) * S
        P.op("gpsimd", lambda e, col0=col0: e.dma_start(
            out=attn[64:128, col0:col0 + S], in_=astage[:]),
            reads=("astage",), writes=("attn",), dma=True)



def emit_block_l0(P, hw):
    xs = hw["xs"]
    w_dq, w_uq, w_qr = hw["w_dq"], hw["w_uq"], hw["w_qr"]
    w_dkv, w_uk, w_uv = hw["w_dkv"], hw["w_uk"], hw["w_uv"]
    w_kr, w_o = hw["w_kr"], hw["w_o"]
    st_blk = ExitStack()
    c_qb = P.sb("c_qb", [128, 2 * S], BF16, st_blk)
    c_kvb = P.sb("c_kvb", [128, 2 * S], BF16, st_blk)
    v_km = P.sb("v_km", [128, NT * H * 65], BF16, st_blk)
    kr_b = P.sb("kr_b", [32, S], BF16, st_blk)
    mask0 = P.sb("s_mask0", [128, 128], BF16, st_blk)
    cos32 = P.sb("s_cos32", [32, S], BF16, st_blk)
    sig32 = P.sb("s_sig32", [32, S], BF16, st_blk)
    swap32 = P.sb("s_swap32", [32, 32], BF16, st_blk)
    scr1 = P.sb("ropes1", [32, 1024], BF16, st_blk)
    P.op("vector", lambda e: e.memset(
        v_km[:].rearrange("p (k c) -> p k c", c=65)[:, :, 64:65], 1.0),
        writes=("v_km.ones",))
    for dst, srch, r in ((mask0, hw["mask0"], "c_mask0"),
                         (cos32, hw["cos32"], "cos32"),
                         (sig32, hw["sig32"], "sig32"),
                         (swap32, hw["swap32"], "swap32")):
        P.op("sync", lambda e, dst=dst, srch=srch: e.dma_start(
            out=dst[:], in_=srch), writes=(r,), dma=True)

    st_pre = ExitStack()
    w_dqb = P.sb("w_dqb", [128, 8 * DLAT], BF16, st_pre)
    w_dkvb = P.sb("w_dkvb", [128, 8 * DLAT], BF16, st_pre)
    w_krb = P.sb("w_krb", [128, 8 * DHR], BF16, st_pre)
    w_uvb = P.sb("w_uvb", [128, 2 * D], BF16, st_pre)
    for dst, src, dd, r in ((w_dqb, w_dq, DLAT, "w_dqb"),
                           (w_dkvb, w_dkv, DLAT, "w_dkvb"),
                           (w_krb, w_kr, DHR, "w_krb"),
                           (w_uvb, w_uv, D, "w_uvb")):
        P.op("gpsimd", lambda e, dst=dst, src=src, dd=dd: e.dma_start(
            out=dst[:].rearrange("p (k m) -> p k m", m=dd),
            in_=src.rearrange("(k p) m -> p k m", p=128)),
            writes=(r,), dma=True)

    stats = make_stats(P, st_pre, "ln0")
    emit_preln_stats(P, "xs", xs, stats)
    lnT = P.sb("lnT", [128, NCH * S], BF16, st_pre)
    tp_stack = ExitStack()
    emit_ln_transpose(P, mk_ln_apply(P, "xs", xs, stats), lnT, "lnT", tp_stack)
    tp_stack.close()
    P.full_barrier()

    psA = P.ps("psA", [128, 2048], F32, st_blk)
    psB = P.ps("psB", [128, 2048], F32, st_blk)
    PSA = ("psA.s0", "psA.s1")
    PSB = tuple(f"psB.b{b}" for b in range(4))

    for (wt, wres, dstb, dres) in ((w_dqb, "w_dqb", c_qb, "c_qb"),
                                   (w_dkvb, "w_dkvb", c_kvb, "c_kvb")):
        for m in range(2):
            fns = []
            for nq in range(4):
                for k in range(8):
                    fns.append(lambda e, m=m, nq=nq, k=k, wt=wt: e.matmul(
                        psA[:, nq * 512:(nq + 1) * 512],
                        wt[:, k * DLAT + m * 128: k * DLAT + (m + 1) * 128],
                        lnT[:, k * S + nq * 512: k * S + (nq + 1) * 512],
                        start=(k == 0), stop=(k == 7)))
            P.group("tensor", fns, reads=(wres, "lnT"), writes=PSA)
            P.op("vector", lambda e, dstb=dstb, m=m: e.tensor_copy(
                dstb[:, m * S:(m + 1) * S], psA[:]),
                reads=PSA, writes=(dres,))

    krr = P.sb("krraw", [32, S], BF16, st_pre)
    fns = []
    for nq in range(4):
        for k in range(8):
            fns.append(lambda e, nq=nq, k=k: e.matmul(
                psA[:, nq * 512:(nq + 1) * 512][0:32, :],
                w_krb[:, k * DHR:(k + 1) * DHR],
                lnT[:, k * S + nq * 512: k * S + (nq + 1) * 512],
                start=(k == 0), stop=(k == 7)))
    P.group("tensor", fns, reads=("w_krb", "lnT"), writes=PSA)
    P.op("vector", lambda e: e.tensor_copy(krr[:], psA[:, 0:2048][0:32, :]),
         reads=PSA, writes=("krraw",))
    fns = [lambda e, nq=nq: e.matmul(
        psB[:, nq * 512:(nq + 1) * 512][0:32, :],
        swap32[:], krr[:, nq * 512:(nq + 1) * 512],
        start=True, stop=True) for nq in range(4)]
    P.group("tensor", fns, reads=("krraw", "swap32"), writes=PSB)
    for hf in range(2):
        c0, c1 = hf * 1024, (hf + 1) * 1024
        P.op("vector", lambda e, c0=c0, c1=c1: e.tensor_tensor(
            scr1[:, 0:1024], krr[:, c0:c1], cos32[:, c0:c1], ALU.mult),
            reads=("krraw", "cos32"), writes=("rp",))
        P.op("vector", lambda e, c0=c0, c1=c1: e.tensor_tensor(
            kr_b[:, c0:c1], psB[:, c0:c1][0:32, :], sig32[:, c0:c1],
            ALU.mult),
            reads=PSB + ("sig32",), writes=("kr_b",))
        P.op("vector", lambda e, c0=c0, c1=c1: e.tensor_tensor(
            kr_b[:, c0:c1], scr1[:, 0:1024], kr_b[:, c0:c1], ALU.add),
            reads=("rp", "kr_b"), writes=("kr_b",))

    for kc in range(NT):
        fns = []
        for lk in range(2):
            for n2 in range(2):
                fns.append(lambda e, kc=kc, lk=lk, n2=n2: e.matmul(
                    psA[:, n2 * 512:(n2 + 1) * 512],
                    c_kvb[:, lk * S + kc * 128: lk * S + (kc + 1) * 128],
                    w_uvb[:, lk * D + n2 * 512: lk * D + (n2 + 1) * 512],
                    start=(lk == 0), stop=(lk == 1)))
        P.group("tensor", fns, reads=("c_kvb", "w_uvb"),
                writes=("psA.s0",))
        P.op("vector", lambda e, kc=kc: e.tensor_copy(
            v_km[:, kc * H * 65:(kc + 1) * H * 65].rearrange(
                "p (h c) -> p h c", c=65)[:, :, 0:64],
            psA[:, 0:1024].rearrange("p (h c) -> p h c", c=64)),
            reads=("psA.s0", "v_km.ones"), writes=("v_km",))
    st_pre.close()
    P.full_barrier()

    st_att = ExitStack()
    w_uqb = P.sb("w_uqb", [128, 2 * D], BF16, st_att)
    w_qrb = P.sb("w_qrb", [128, 2 * 512], BF16, st_att)
    w_ukb = P.sb("w_ukb", [128, 2 * D], BF16, st_att)
    for dst, srcw, dd, r in ((w_uqb, w_uq, D, "w_uqb"),
                            (w_qrb, w_qr, 512, "w_qrb"),
                            (w_ukb, w_uk, D, "w_ukb")):
        P.op("gpsimd", lambda e, dst=dst, srcw=srcw, dd=dd: e.dma_start(
            out=dst[:].rearrange("p (k m) -> p k m", m=dd),
            in_=srcw.rearrange("(k p) m -> p k m", p=128)),
            writes=(r,), dma=True)

    qh = [P.sb(f"qh{i}", [96, S], BF16, st_att) for i in range(2)]
    kh = [P.sb(f"kh{i}", [96, S], BF16, st_att) for i in range(2)]
    qraw = P.sb("qraw", [32, S], BF16, st_att)
    qrstage = qraw
    st_attn = ExitStack()
    attn = P.sb("attn", [128, NCH * S], BF16, st_attn, side="right")
    astage = P.sb("astage", [64, S], BF16, st_att)
    pbuf = P.sb("pbuf", [128, 2 * 1024], BF16, st_att)
    scrA = (P.sb("scrA0", [64, 512], BF16, st_att),
            P.sb("scrA1", [64, 512], BF16, st_att))
    rrow = [P.sb(f"rrow{i}", [1, 512], F32, st_att) for i in range(2)]

    for i in range(2):
        P.op("gpsimd", lambda e, i=i: e.dma_start(
            out=kh[i][64:96, :], in_=kr_b[:]),
            reads=("kr_b",), writes=(f"kh{i}.r",), dma=True)

    EXP_SCALE = float(1.0 / math.sqrt(96.0))
    CL_HI = 1.0
    CL_LO = float(np.float32(np.exp(np.float64(-160.0) / math.sqrt(96.0))))
    nb80 = P.sb("nb80", [128, 1], F32, st_att)
    P.op("vector", lambda e: e.memset(nb80[:], float(-80.0 / math.sqrt(96.0))),
         writes=("nb80",))

    def prod_items(h):
        hb = h % 2
        qht, kht = qh[hb], kh[hb]
        qres, kres = f"qh{hb}", f"kh{hb}"

        def proj(wb, wres, wbase, wstep, wid, src, sres, rows, dst_fn, dres,
                 half):
            def mm(slot, half=half):
                pst, sbase, pres = slot
                fns = []
                for nqg in range(2):
                    nq = half * 2 + nqg
                    for lk in range(2):
                        fns.append(lambda e, nq=nq, nqg=nqg, lk=lk, pst=pst,
                                   sbase=sbase: e.matmul(
                            pst[:, sbase + nqg * 512:
                                sbase + (nqg + 1) * 512][0:rows, :],
                            wb[:, lk * wstep + wbase:
                               lk * wstep + wbase + wid],
                            src[:, lk * S + nq * 512: lk * S + (nq + 1) * 512],
                            start=(lk == 0), stop=(lk == 1)))
                P.group("tensor", fns, reads=(wres, sres), writes=pres)

            def post(slot, half=half):
                pst, sbase, pres = slot
                P.op("scalar", lambda e, pst=pst, sbase=sbase, half=half:
                     e.activation(dst_fn(half),
                                  pst[:, sbase:sbase + 1024][0:rows, :],
                                  AF.Identity),
                     reads=pres, writes=(dres,))
            return {"mm": mm, "post": post}

        def qc(half):
            return proj(w_uqb, "w_uqb", h * 64, D, 64, c_qb, "c_qb", 64,
                        lambda hf: qht[0:64, hf * 1024:(hf + 1) * 1024],
                        f"{qres}.c{half}", half)

        def qr(half):
            return proj(w_qrb, "w_qrb", h * 32, 512, 32, c_qb, "c_qb", 32,
                        lambda hf: qraw[:, hf * 1024:(hf + 1) * 1024],
                        f"qrw{half}", half)

        def kc_(half):
            return proj(w_ukb, "w_ukb", h * 64, D, 64, c_kvb, "c_kvb", 64,
                        lambda hf: kht[0:64, hf * 1024:(hf + 1) * 1024],
                        f"{kres}.c{half}", half)

        def sw(half):
            hc0, hc1 = half * 1024, (half + 1) * 1024

            def mm(slot):
                pst, sbase, pres = slot
                fns = []
                for nqg in range(2):
                    nq = half * 2 + nqg
                    fns.append(lambda e, nq=nq, nqg=nqg, pst=pst,
                               sbase=sbase: e.matmul(
                        pst[:, sbase + nqg * 512:
                            sbase + (nqg + 1) * 512][0:32, :],
                        swap32[:], qraw[:, nq * 512:(nq + 1) * 512],
                        start=True, stop=True))
                P.group("tensor", fns, reads=(f"qrw{half}", "swap32"),
                        writes=pres)

            def post(slot):
                pst, sbase, pres = slot
                P.op("vector", lambda e: e.tensor_tensor(
                    scr1[:, 0:1024], qraw[:, hc0:hc1], cos32[:, hc0:hc1],
                    ALU.mult),
                    reads=(f"qrw{half}", "cos32"), writes=("rp",))
                P.op("vector", lambda e, pst=pst, sbase=sbase:
                     e.tensor_tensor(
                         qraw[:, hc0:hc1], pst[:, sbase:sbase + 1024][0:32, :],
                         sig32[:, hc0:hc1], ALU.mult),
                     reads=pres + ("sig32", f"qrw{half}"),
                     writes=(f"qrw{half}",))
                P.op("vector", lambda e: e.tensor_tensor(
                    qraw[:, hc0:hc1], scr1[:, 0:1024], qraw[:, hc0:hc1],
                    ALU.add),
                    reads=("rp", f"qrw{half}"), writes=(f"qrw{half}",))
                P.op("gpsimd", lambda e: e.dma_start(
                    out=qht[64:96, hc0:hc1], in_=qraw[:, hc0:hc1]),
                    reads=(f"qrw{half}",), writes=(f"{qres}.r{half}",),
                    dma=True)
            return {"mm": mm, "post": post}

        return [qc(0), qr(0), qc(1), qr(1), sw(0), kc_(0), sw(1), kc_(1)]

    for i, it in enumerate(prod_items(0)):
        slot = (psA, (i % 2) * 1024, (f"psA.s{i % 2}",))
        it["mm"](slot)
        it["post"](slot)

    for h in range(H):
        hb = h % 2
        qht, kht = qh[hb], kh[hb]
        qres, kres = f"qh{hb}", f"kh{hb}"

        def qk_lhsT(kc, kht=kht):
            return kht[:, kc * 128:(kc + 1) * 128]

        def qk_rhs(lo, width, qht=qht):
            return qht[:, lo:lo + width]

        def qk_res(kc, half, qres=qres, kres=kres, hb=hb):
            return (f"{qres}.c{half}", f"{qres}.r{half}",
                    f"{kres}.c{kc // 8}", f"kh{hb}.r", "v_km")

        emit_attention_head(
            P, h, qk_lhsT, qk_rhs, qk_res,
            v_km, attn, psA, psB, pbuf, causal=True, exp_scale=EXP_SCALE,
            exp_bias=nb80[:], exp_bias_res="nb80", clamp=(CL_LO, CL_HI),
            mask0=mask0, scrA=scrA, rrow=rrow, astage=astage,
            extra_items=prod_items(h + 1) if h + 1 < H else ())

    st_att.close()
    P.full_barrier()
    st_o = ExitStack()
    w_ob = P.sb("w_ob", [128, 8 * D], BF16, st_o)
    P.op("gpsimd", lambda e: e.dma_start(
        out=w_ob[:].rearrange("p (k m) -> p k m", m=D),
        in_=w_o.rearrange("(k p) m -> p k m", p=128)),
        writes=("w_ob",), dma=True)
    pln = PostLN(P, "xs", xs, st_o, "po0",
                 nxt=hw.get("pre_stats", {}).get("lnq"))
    for i in range(NT):
        half = i % 2
        pres = f"psA.s{half}"
        fns = []
        for c in range(NCH):
            for n2 in range(2):
                fns.append(lambda e, i=i, c=c, n2=n2, half=half: e.matmul(
                    psA[:, half * 1024 + n2 * 512: half * 1024 + (n2 + 1) * 512],
                    attn[:, c * S + i * 128: c * S + (i + 1) * 128],
                    w_ob[:, c * D + n2 * 512: c * D + (n2 + 1) * 512],
                    start=(c == 0), stop=(c == 7)))
        P.group("tensor", fns, reads=("attn", "w_ob"), writes=(pres,))
        pln.chunk(i, psA[:, half * 1024:(half + 1) * 1024], pres)
    st_o.close()
    P.full_barrier()
    st_attn.close()
    P.full_barrier()
    st_blk.close()
    P.full_barrier()



def emit_quant_preproc(P, src_h, dst_h, dst_res, rows, cols, scale, st_w,
                       tag, ternary=None, bufs=None):
    if bufs is None:
        wf = [P.sb(f"{tag}_wf{i}", [128, cols], F32, st_w) for i in range(2)]
        wi = [P.sb(f"{tag}_wi{i}", [128, cols], I8, st_w) for i in range(2)]
        wb = [P.sb(f"{tag}_wb{i}", [128, cols], BF16, st_w) for i in range(2)]
    else:
        wf, wi, wb = bufs
    for c in range(rows // 128):
        s = c % 2
        f, ii, b = wf[s], wi[s], wb[s]
        rf, ri, rb = f"qp.wf{s}", f"qp.wi{s}", f"qp.wb{s}"
        P.op("sync", lambda e, c=c, f=f: e.dma_start(
            out=f[:, 0:cols], in_=src_h[c * 128:(c + 1) * 128, :]),
            writes=(rf,), dma=True)
        if ternary is None:
            P.op("vector", lambda e, f=f, ii=ii: e.tensor_scalar(
                ii[:, 0:cols], f[:, 0:cols], float(scale), None, ALU.mult),
                reads=(rf,), writes=(ri,))
            P.op("scalar", lambda e, ii=ii, b=b: e.activation(
                b[:, 0:cols], ii[:, 0:cols], AF.Identity,
                scale=float(1.0 / scale)),
                reads=(ri,), writes=(rb,))
        else:
            ainv_ap, ainv_res = ternary
            P.op("vector", lambda e, f=f, ii=ii: e.tensor_scalar(
                ii[:, 0:cols], f[:, 0:cols], ainv_ap, None, ALU.mult),
                reads=(rf,), sreads=(ainv_res,), writes=(ri,))
            P.op("vector", lambda e, ii=ii, b=b: e.tensor_scalar(
                b[:, 0:cols], ii[:, 0:cols], 1.0, -1.0, ALU.min, ALU.max),
                reads=(ri,), writes=(rb,))
        P.op("scalar", lambda e, c=c, b=b: e.dma_start(
            out=dst_h[c * 128:(c + 1) * 128, :], in_=b[:, 0:cols]),
            reads=(rb,), writes=(dst_res,), dma=True)


def emit_ffn_matmul_phase(P, xT, xT_res, hbuf, w1_hbm, w2_hbm, st_mm,
                          up_evac, down_pre, pln, down_bias=None,
                          down_bias_res=None, phase_out=None,
                          w1_src=None, w2_src=None, quant=None,
                          after_up=None):
    w1band = [P.sb(f"w1b{i}", [128, 8 * 512], BF16, st_mm) for i in range(2)]
    w2band = [P.sb(f"w2b{i}", [128, 2 * D], BF16, st_mm) for i in range(2)]
    wqf = [P.sb(f"wqf{i}", [128, 2048], F32, st_mm) for i in range(3)]
    wqi = [P.sb(f"wqi{i}", [128, 2048], I8, st_mm) for i in range(2)]
    ps_up = P.ps("ps_up", [128, 1024], F32, st_mm)
    ps_dn = [P.ps(f"ps_dn{i}", [128, 1024], F32, st_mm) for i in range(3)]
    if phase_out is not None:
        phase_out["ps_dn0"] = ps_dn[0]
        phase_out["ps_dn1"] = ps_dn[1]

    if isinstance(quant, tuple):
        quant1, quant2 = quant
    else:
        quant1 = quant2 = quant

    def dn_slot(qc):
        if qc < 3:
            return ps_dn[qc], (f"ps_dn{qc}",)
        return ps_up, ("ps_up.0", "ps_up.1")

    def load_w1_band(hg, bs, first):
        if not first:
            P.op("gpsimd", lambda e, hg=hg, bs=bs: e.dma_start(
                out=w1band[bs][:].rearrange("p (k m) -> p k m", m=512),
                in_=w1_hbm[:, hg * 512:(hg + 1) * 512].rearrange(
                    "(k p) m -> p k m", p=128)),
                reads=("w1q_hbm",), writes=(f"w1b{bs}",), dma=True)
            return
        for half in range(2):
            s = (hg * 2 + half) % 3
            si = (hg * 2 + half) % 2
            fr, ir = f"wqf{s}", f"wqi{si}"
            P.op("sync" if half == 0 else "gpsimd",
                 lambda e, hg=hg, half=half, s=s: e.dma_start(
                     out=wqf[s][:].rearrange("p (k m) -> p k m", m=256),
                     in_=w1_src[:, hg * 512 + half * 256:
                                hg * 512 + half * 256 + 256].rearrange(
                         "(k p) m -> p k m", p=128)),
                 writes=(fr,), dma=True)
            quant1(wqf[s][:], fr, wqi[si][:], ir,
                   w1band[bs][:].rearrange("p (k m) -> p k m", m=512)
                   [:, :, half * 256:(half + 1) * 256],
                   f"w1b{bs}")
        P.op("scalar", lambda e, hg=hg, bs=bs: e.dma_start(
            out=w1_hbm[:, hg * 512:(hg + 1) * 512].rearrange(
                "(k p) m -> p k m", p=128),
            in_=w1band[bs][:].rearrange("p (k m) -> p k m", m=512)),
            reads=(f"w1b{bs}",), writes=("w1q_hbm",), dma=True)

    def load_w2_chunk(hg, bs, first):
        if not first:
            P.op("gpsimd", lambda e, hg=hg, bs=bs: e.dma_start(
                out=w2band[bs][:].rearrange("p (k m) -> p k m", m=D),
                in_=w2_hbm[hg * 256:(hg + 1) * 256, :].rearrange(
                    "(k p) m -> p k m", p=128)),
                reads=("w2q_hbm",), writes=(f"w2b{bs}",), dma=True)
            return
        for half in range(2):
            s = (hg * 2 + half) % 3
            si = (hg * 2 + half) % 2
            fr, ir = f"wqf{s}", f"wqi{si}"
            P.op("sync" if half == 0 else "gpsimd",
                 lambda e, hg=hg, half=half, s=s: e.dma_start(
                     out=wqf[s][:, 0:1024],
                     in_=w2_src[hg * 256 + half * 128:
                                hg * 256 + half * 128 + 128, :]),
                 writes=(fr,), dma=True)
            quant2(wqf[s][:, 0:1024], fr, wqi[si][:, 0:1024], ir,
                   w2band[bs][:, half * D:(half + 1) * D], f"w2b{bs}")
        P.op("scalar", lambda e, hg=hg, bs=bs: e.dma_start(
            out=w2_hbm[hg * 256:(hg + 1) * 256, :].rearrange(
                "(k p) m -> p k m", p=128),
            in_=w2band[bs][:].rearrange("p (k m) -> p k m", m=D)),
            reads=(f"w2b{bs}",), writes=("w2q_hbm",), dma=True)

    for jt in range(4):
        for hg in range(8):
            bs = hg % 2
            load_w1_band(hg, bs, jt == 0)
            for hx in range(4):
                hc = hg * 4 + hx
                slot = hc % 2
                pres = f"ps_up.{slot}"
                fns = []
                for k in range(8):
                    fns.append(lambda e, jt=jt, k=k, slot=slot, bs=bs, hx=hx:
                               e.matmul(
                                   ps_up[:, slot * 512:(slot + 1) * 512],
                                   w1band[bs][:, k * 512 + hx * 128:
                                              k * 512 + hx * 128 + 128],
                                   xT[:, k * S + jt * 512:
                                      k * S + jt * 512 + 512],
                                   start=(k == 0), stop=(k == 7)))
                P.group("tensor", fns, reads=(f"w1b{bs}", xT_res),
                        writes=(pres,))
                up_evac(hc, ps_up[:, slot * 512:(slot + 1) * 512], pres)
                if after_up is not None:
                    after_up(hc)
        scale_maker = down_pre(jt) if down_pre is not None else None
        for hg in range(16):
            bs = hg % 2
            load_w2_chunk(hg, bs, jt == 0)
            for hx in range(2):
                hc = hg * 2 + hx
                for qc in range(4):
                    dnp, dres = dn_slot(qc)
                    fns = []
                    for n2 in range(2):
                        fns.append(lambda e, hc=hc, qc=qc, n2=n2, bs=bs,
                                   hx=hx, dnp=dnp: e.matmul(
                                       dnp[:, n2 * 512:(n2 + 1) * 512],
                                       hbuf[:, hc * 512 + qc * 128:
                                            hc * 512 + qc * 128 + 128],
                                       w2band[bs][:, hx * D + n2 * 512:
                                                  hx * D + (n2 + 1) * 512],
                                       start=(hc == 0), stop=(hc == 31)))
                    P.group("tensor", fns, reads=(f"w2b{bs}", f"hbuf.{hc}"),
                            writes=dres)
        for qc in (3, 0, 1, 2):
            i = jt * 4 + qc
            dnp, dres = dn_slot(qc)
            sc = scale_maker(qc) if scale_maker is not None else None
            if sc is not None:
                pln.chunk(i, dnp[:], dres, scale=sc[0], scale_res=sc[1])
            else:
                pln.chunk(i, dnp[:], dres, bias_bcast=down_bias,
                          bias_res=down_bias_res)


def mk_quant64(P):
    def quant(f_ap, f_res, i_ap, i_res, dst_ap, dst_res):
        P.op("vector", lambda e: e.tensor_scalar(
            i_ap, f_ap, 64.0, None, ALU.mult), reads=(f_res,), writes=(i_res,))
        P.op("scalar", lambda e: e.activation(
            dst_ap, i_ap, AF.Identity, scale=1.0 / 64.0),
            reads=(i_res,), writes=(dst_res,))
    return quant


def mk_ternary(P, ainv_ap, ainv_res):
    def quant(f_ap, f_res, i_ap, i_res, dst_ap, dst_res):
        P.op("vector", lambda e: e.tensor_scalar(
            i_ap, f_ap, ainv_ap, None, ALU.mult),
            reads=(f_res,), sreads=(ainv_res,), writes=(i_res,))
        P.op("vector", lambda e: e.tensor_scalar(
            dst_ap, i_ap, 1.0, -1.0, ALU.min, ALU.max),
            reads=(i_res,), writes=(dst_res,))
    return quant


def emit_block_ffn0(P, hw):
    xs = hw["xs"]
    w1t_h, b1_h, w2t_h, b2_h = hw["w1t"], hw["b1"], hw["w2t"], hw["b2"]
    identF = P.consts["identF"]
    w1q_hbm = P.nc.dram_tensor("w1q_scr", [D, DFF], BF16, kind="Internal").ap()
    w2q_hbm = P.nc.dram_tensor("w2q_scr", [DFF, D], BF16, kind="Internal").ap()
    st_blk = ExitStack()
    xqT = P.sb("xqT", [128, NCH * S], BF16, st_blk)
    B16 = P.sb("B16", [128, 32], F32, st_blk)
    B2C = P.sb("B2C", [128, D], BF16, st_blk)



    ps_small_stack = ExitStack()
    ps_small = P.ps("ps_small", [128, 512], F32, ps_small_stack)
    b1r = P.sb("b1r", [32, 128], F32, ps_small_stack)
    b1i = P.sb("b1i", [32, 128], I8, ps_small_stack)
    b1f = P.sb("b1f", [32, 128], F32, ps_small_stack)
    P.op("sync", lambda e: e.dma_start(
        out=b1r[:], in_=b1_h.rearrange("(c p) -> c p", p=128)),
        writes=("b1r",), dma=True)
    P.op("vector", lambda e: e.tensor_scalar(
        b1i[:], b1r[:], 64.0, None, ALU.mult), reads=("b1r",), writes=("b1i",))
    P.op("scalar", lambda e: e.activation(
        b1f[:], b1i[:], AF.Identity, scale=0.25), reads=("b1i",),
        writes=("b1f",))
    P.op("tensor", lambda e: e.transpose(
        ps_small[:, 0:32], b1f[:], identF[0:32, 0:32]),
        reads=("b1f", "c_identF"), writes=("ps_small",))
    P.op("vector", lambda e: e.tensor_copy(B16[:], ps_small[:, 0:32]),
         reads=("ps_small",), writes=("B16",))
    b2r = P.sb("b2r", [1, D], F32, ps_small_stack)
    b2i = P.sb("b2i", [1, D], I8, ps_small_stack)
    b2f = P.sb("b2f", [1, D], F32, ps_small_stack)
    P.op("sync", lambda e: e.dma_start(
        out=b2r[:], in_=b2_h.rearrange("(o d) -> o d", o=1)),
        writes=("b2r",), dma=True)
    P.op("vector", lambda e: e.tensor_scalar(
        b2i[:], b2r[:], 64.0, None, ALU.mult), reads=("b2r",), writes=("b2i",))
    P.op("scalar", lambda e: e.activation(
        b2f[:], b2i[:], AF.Identity, scale=1.0 / 64.0), reads=("b2i",),
        writes=("b2f",))
    for n2 in range(2):
        P.op("tensor", lambda e, n2=n2: e.matmul(
            ps_small[:, 0:512], P.consts["ones_f"][0:1, :],
            b2f[:, n2 * 512:(n2 + 1) * 512], start=True, stop=True),
            reads=("b2f", "c_ones_f"), writes=("ps_small",))
        P.op("vector", lambda e, n2=n2: e.tensor_copy(
            B2C[:, n2 * 512:(n2 + 1) * 512], ps_small[:, 0:512]),
            reads=("ps_small",), writes=("B2C",))
    ps_small_stack.close()
    P.full_barrier()

    st_ln = ExitStack()
    pool = hw.get("pre_stats")
    if pool:
        stats = pool["lnq"]
        emit_preln_stats(P, "xs", xs, stats, pre=True)
    else:
        stats = make_stats(P, st_ln, "lnq")
        emit_preln_stats(P, "xs", xs, stats)
    r16 = stats["r2"]
    P.op("vector", lambda e: e.tensor_scalar(
        r16[:], stats["rstd"][:], 16.0, None, ALU.mult),
        reads=("lnq.rstd",), writes=("lnq.r16",))
    xq_i8 = [P.sb(f"xq_i8{i}", [128, D], I8, st_ln) for i in range(2)]

    def xq_apply(t, dst, dres):
        ii = xq_i8[t % 2]
        ri = f"xqi8.{t % 2}"
        P.op("vector", lambda e, t=t, ii=ii: e.tensor_scalar(
            ii[:], xs[:, t * D:(t + 1) * D], stats["mean"][:, t:t + 1],
            r16[:, t:t + 1], ALU.subtract, ALU.mult),
            reads=("xs",), sreads=("lnq.mean", "lnq.r16"), writes=(ri,))
        P.op("scalar", lambda e, ii=ii, dst=dst: e.activation(
            dst[:], ii[:], AF.Identity, scale=1.0 / 16.0),
            reads=(ri,), writes=(dres,))

    tp_stack = ExitStack()
    emit_ln_transpose(P, xq_apply, xqT, "xqT", tp_stack)
    tp_stack.close()
    P.full_barrier()
    st_ln.close()
    P.full_barrier()

    st_mm = ExitStack()
    hbuf = P.sb("hbuf", [128, 32 * 512], BF16, st_mm)
    hscr = [P.sb(f"hscr{i}", [128, 512], I8, st_mm) for i in range(2)]
    pln = PostLN(P, "xs", xs, st_mm, "poq",
                 nxt=hw.get("pre_stats", {}).get("ln1"))

    def up_evac(hc, psum_ap, psum_res):
        ii = hscr[hc % 2]
        ri = f"hscr{hc % 2}"
        P.op("vector", lambda e, ii=ii, psum_ap=psum_ap, hc=hc:
             e.tensor_scalar(ii[:], psum_ap, 16.0, B16[:, hc:hc + 1],
                             ALU.mult, ALU.add),
             reads=(psum_res,), sreads=("B16",), writes=(ri,))
        P.op("scalar", lambda e, ii=ii, hc=hc: e.activation(
            hbuf[:, hc * 512:(hc + 1) * 512], ii[:], AF.Relu, scale=1.0 / 16.0),
            reads=(ri,), writes=(f"hbuf.{hc}",))

    emit_ffn_matmul_phase(P, xqT, "xqT", hbuf, w1q_hbm, w2q_hbm, st_mm,
                          up_evac, None, pln, down_bias=B2C,
                          down_bias_res="B2C", w1_src=w1t_h, w2_src=w2t_h,
                          quant=mk_quant64(P))
    st_mm.close()
    P.full_barrier()
    st_blk.close()
    P.full_barrier()



def emit_vec_to_partition(P, src_h, src_res, n, dst, dst_res, ps_ap, ps_res,
                          identF, post_scale=1.0, stack=None):
    nch = n // 128
    r = P.sb(dst_res + "_r", [nch, 128], F32, stack)
    P.op("sync", lambda e: e.dma_start(
        out=r[:], in_=src_h.rearrange("(c p) -> c p", p=128)),
        writes=(src_res,), dma=True)
    if post_scale != 1.0:
        P.op("vector", lambda e: e.tensor_scalar(
            r[:], r[:], float(post_scale), None, ALU.mult),
            reads=(src_res,), writes=(src_res,))
    P.op("tensor", lambda e: e.transpose(
        ps_ap[:, 0:nch], r[:], identF[0:nch, 0:nch]),
        reads=(src_res, "c_identF"), writes=(ps_res,))
    P.op("vector", lambda e: e.tensor_copy(dst[:], ps_ap[:, 0:nch]),
         reads=(ps_res,), writes=(dst_res,))


def emit_row_bcast(P, src_h, src_res, dst, dst_res, ps_ap, ps_res, n=D,
                   stack=None):
    row = P.sb(dst_res + "_row", [1, n], F32, stack)
    P.op("sync", lambda e: e.dma_start(
        out=row[:], in_=src_h.rearrange("(o d) -> o d", o=1)),
        writes=(src_res,), dma=True)
    for n2 in range(n // 512):
        P.op("tensor", lambda e, n2=n2: e.matmul(
            ps_ap[:, 0:512], P.consts["ones_f"][0:1, :],
            row[:, n2 * 512:(n2 + 1) * 512], start=True, stop=True),
            reads=(src_res, "c_ones_f"), writes=(ps_res,))
        P.op("vector", lambda e, n2=n2: e.tensor_copy(
            dst[:, n2 * 512:(n2 + 1) * 512], ps_ap[:, 0:512]),
            reads=(ps_res,), writes=(dst_res,))


def emit_block_l1(P, hw):
    xs = hw["xs"]
    qwt_h, kwt_h, vwt_h, owt_h = hw["qwt"], hw["kwt"], hw["vwt"], hw["owt"]
    qb_h, kb_h, vb_h, ob_h = hw["qb"], hw["kb"], hw["vb"], hw["ob"]
    identF = P.consts["identF"]
    st_blk = ExitStack()
    v_km = P.sb("v_km", [128, NT * H * 65], BF16, st_blk)
    P.op("vector", lambda e: e.memset(
        v_km[:].rearrange("p (k c) -> p k c", c=65)[:, :, 64:65], 1.0),
        writes=("v_km.ones",))

    qb8 = P.sb("qb8", [128, 8], F32, st_blk)
    kbp = P.sb("kbp", [128, 8], F32, st_blk)
    pss = ExitStack()
    ps_small = P.ps("ps_small", [128, 512], F32, pss)
    emit_vec_to_partition(P, qb_h, "qb_h", D, qb8, "qb8", ps_small, "ps_small",
                          identF, post_scale=0.125, stack=pss)
    emit_vec_to_partition(P, kb_h, "kb_h", D, kbp, "kbp", ps_small, "ps_small",
                          identF, stack=pss)
    pss.close()
    P.full_barrier()

    st_lnT = ExitStack()
    lnT = P.sb("lnT", [128, NCH * S], BF16, st_lnT)
    st_ln = ExitStack()
    pool = hw.get("pre_stats")
    if pool:
        stats = pool["ln1"]
        emit_preln_stats(P, "xs", xs, stats, pre=True)
    else:
        stats = make_stats(P, st_ln, "ln1")
        emit_preln_stats(P, "xs", xs, stats)
    tp_stack = ExitStack()
    emit_ln_transpose(P, mk_ln_apply(P, "xs", xs, stats), lnT, "lnT", tp_stack)
    tp_stack.close()
    P.full_barrier()
    st_ln.close()
    P.full_barrier()

    psA = P.ps("psA", [128, 2048], F32, st_blk)
    psB = P.ps("psB", [128, 2048], F32, st_blk)
    PSA = ("psA.s0", "psA.s1")
    PSB = tuple(f"psB.b{b}" for b in range(4))

    st_v = ExitStack()
    vwT = P.sb("vwT", [128, 8 * D], BF16, st_v)
    VBC = P.sb("VBC", [128, D], F32, st_v)
    emit_row_bcast(P, vb_h, "vb_h", VBC, "VBC", psB[:, 0:512], "psB.b0",
                   stack=st_v)
    P.op("gpsimd", lambda e: e.dma_start(
        out=vwT[:].rearrange("p (k m) -> p k m", m=D),
        in_=vwt_h.rearrange("(k p) m -> p k m", p=128)),
        writes=("vwT",), dma=True)
    for kc in range(NT):
        fns = []
        for dc in range(NCH):
            for n2 in range(2):
                fns.append(lambda e, kc=kc, dc=dc, n2=n2: e.matmul(
                    psA[:, n2 * 512:(n2 + 1) * 512],
                    lnT[:, dc * S + kc * 128: dc * S + (kc + 1) * 128],
                    vwT[:, dc * D + n2 * 512: dc * D + (n2 + 1) * 512],
                    start=(dc == 0), stop=(dc == 7)))
        P.group("tensor", fns, reads=("lnT", "vwT"), writes=("psA.s0",))
        P.op("vector", lambda e, kc=kc: e.tensor_tensor(
            v_km[:, kc * H * 65:(kc + 1) * H * 65].rearrange(
                "p (h c) -> p h c", c=65)[:, :, 0:64],
            psA[:, 0:1024].rearrange("p (h c) -> p h c", c=64),
            VBC[:].rearrange("p (h c) -> p h c", c=64), ALU.add),
            reads=("psA.s0", "v_km.ones", "VBC"), writes=("v_km",))
        if "alpha" in hw:
            for _ in range(2):
                hw["alpha"]["chunk"]()
    st_v.close()
    P.full_barrier()

    st_att = ExitStack()
    qwb = [P.sb(f"qwb{i}", [128, 8 * 128], BF16, st_att) for i in range(2)]
    kwb = [P.sb(f"kwb{i}", [128, 8 * 128], BF16, st_att) for i in range(2)]
    qpair_b = [P.sb(f"qpair{i}", [128, S], BF16, st_att) for i in range(2)]
    kpair_b = [P.sb(f"kpair{i}", [128, S], BF16, st_att) for i in range(2)]
    st_attn = ExitStack()
    attn = P.sb("attn", [128, NCH * S], BF16, st_attn, side="right")
    astage = P.sb("astage", [64, S], BF16, st_att)
    pbuf = P.sb("pbuf", [128, 2 * 1024], BF16, st_att)
    scrA = (P.sb("scrA0", [64, 512], BF16, st_att),
            P.sb("scrA1", [64, 512], BF16, st_att))
    rrow = [P.sb(f"rrow{i}", [1, 512], F32, st_att) for i in range(2)]

    mm2 = P.sb("mm2", [1, 4], F32, st_att)
    negb = [P.sb(f"negb{i}", [128, 1], F32, st_att) for i in range(2)]

    def pair_loads(j):
        bs = j % 2
        P.op("gpsimd", lambda e, j=j, bs=bs: e.dma_start(
            out=qwb[bs][:].rearrange("p (k m) -> p k m", m=128),
            in_=qwt_h[:, j * 128:(j + 1) * 128].rearrange(
                "(k p) m -> p k m", p=128)),
            writes=(f"qwb{bs}",), dma=True)
        P.op("gpsimd", lambda e, j=j, bs=bs: e.dma_start(
            out=kwb[bs][:].rearrange("p (k m) -> p k m", m=128),
            in_=kwt_h[:, j * 128:(j + 1) * 128].rearrange(
                "(k p) m -> p k m", p=128)),
            writes=(f"kwb{bs}",), dma=True)

    def pair_prod_items(j):
        bs = j % 2

        def proj(wb, wres, dst, dres, sc, bias, bres, half):
            def mm(slot, half=half, wb=wb):
                pst, sbase, pres = slot
                fns = []
                for nqg in range(2):
                    nq = half * 2 + nqg
                    for dc in range(NCH):
                        fns.append(lambda e, nq=nq, nqg=nqg, dc=dc, wb=wb,
                                   pst=pst, sbase=sbase: e.matmul(
                            pst[:, sbase + nqg * 512:
                                sbase + (nqg + 1) * 512],
                            wb[:, dc * 128:(dc + 1) * 128],
                            lnT[:, dc * S + nq * 512: dc * S + (nq + 1) * 512],
                            start=(dc == 0), stop=(dc == 7)))
                P.group("tensor", fns, reads=(wres, "lnT"), writes=pres)

            def post(slot, half=half, dst=dst):
                pst, sbase, pres = slot
                P.op("vector", lambda e, pst=pst, sbase=sbase, half=half:
                     e.tensor_scalar(dst[:, half * 1024:(half + 1) * 1024],
                                     pst[:, sbase:sbase + 1024], sc,
                                     bias[:, j:j + 1], ALU.mult, ALU.add),
                     reads=pres, sreads=(bres,), writes=(dres,))
            return {"mm": mm, "post": post}

        q0 = proj(qwb[bs], f"qwb{bs}", qpair_b[bs], f"qpair{bs}.0", 0.125,
                  qb8, "qb8", 0)
        q1 = proj(qwb[bs], f"qwb{bs}", qpair_b[bs], f"qpair{bs}.1", 0.125,
                  qb8, "qb8", 1)
        k0 = proj(kwb[bs], f"kwb{bs}", kpair_b[bs], f"kpair{bs}.0", 1.0,
                  kbp, "kbp", 0)
        k1 = proj(kwb[bs], f"kwb{bs}", kpair_b[bs], f"kpair{bs}.1", 1.0,
                  kbp, "kbp", 1)
        return [q0, k0, q1, k1]

    def pair_bounds(j):
        bs = j % 2
        qpair, kpair = qpair_b[bs], kpair_b[bs]
        qpr = (f"qpair{bs}.0", f"qpair{bs}.1")
        kpr = (f"kpair{bs}.0", f"kpair{bs}.1")
        bsq = pbuf
        BSQR = ("pbuf.s0", "pbuf.s1")
        for (srcp, sres, mcol) in ((qpair, qpr, 0), (kpair, kpr, 1)):
            P.op("vector", lambda e, srcp=srcp:
                 e.tensor_tensor(bsq[:], srcp[:], srcp[:], ALU.mult),
                 reads=sres, writes=BSQR)
            fns = []
            for nq in range(4):
                fns.append(lambda e, nq=nq: e.matmul(
                    psB[:, nq * 512:(nq + 1) * 512][0:1, :],
                    P.consts["ones_cb"][:],
                    bsq[:, nq * 512:(nq + 1) * 512],
                    start=True, stop=True))
            P.group("tensor", fns, reads=BSQR + ("c_ones_cb",), writes=PSB)
            P.op("vector", lambda e, mcol=mcol: e.reduce_max(
                mm2[:, mcol:mcol + 1], psB[:, 0:2048][0:1, :], axis=AX.X),
                reads=PSB, writes=("mm2",))
        P.op("vector", lambda e: e.tensor_tensor(
            mm2[:, 2:3], mm2[:, 0:1], mm2[:, 1:2], ALU.mult),
            reads=("mm2",), writes=("mm2",))
        P.op("scalar", lambda e: e.activation(
            mm2[:, 3:4], mm2[:, 2:3], AF.Sqrt),
            reads=("mm2",), writes=("mm2",))
        P.op("tensor", lambda e: e.matmul(
            psA[:, 0:1], P.consts["ones_f"][0:1, :],
            mm2[:, 3:4], start=True, stop=True),
            reads=("mm2", "c_ones_f"), writes=("psA.s0",))
        P.op("vector", lambda e, bs=bs: e.tensor_scalar(
            negb[bs][:], psA[:, 0:1], -1.0, None, ALU.mult),
            reads=("psA.s0",), writes=(f"negb{bs}",))

    pair_loads(0)
    for i, it in enumerate(pair_prod_items(0)):
        slot = (psA, (i % 2) * 1024, (f"psA.s{i % 2}",))
        it["mm"](slot)
        it["post"](slot)
    pair_bounds(0)
    for j in range(8):
        bs = j % 2
        qpair, kpair = qpair_b[bs], kpair_b[bs]
        if "alpha" in hw:
            for _ in range(12):
                hw["alpha"]["chunk"]()
        if j + 1 < 8:
            pair_loads(j + 1)
        for hh in range(2):
            h = 2 * j + hh
            base = hh * 64

            def qk_lhsT(kc, base=base, kpair=kpair):
                return kpair[base:base + 64, kc * 128:(kc + 1) * 128]

            def qk_rhs(lo, width, base=base, qpair=qpair):
                return qpair[base:base + 64, lo:lo + width]

            def qk_res(kc, half, bs=bs):
                return (f"qpair{bs}.{half}", f"kpair{bs}.{kc // 8}", "v_km")

            extra = pair_prod_items(j + 1) if (hh == 0 and j + 1 < 8) else ()
            emit_attention_head(
                P, h, qk_lhsT, qk_rhs, qk_res, v_km, attn,
                psA, psB, pbuf, causal=False, exp_scale=1.0,
                exp_bias=negb[bs][:], exp_bias_res=f"negb{bs}",
                clamp=None, mask0=None, scrA=scrA, rrow=rrow, astage=astage,
                extra_items=extra, evac_eng="vector")
            if hh == 0 and j + 1 < 8:
                pair_bounds(j + 1)

    st_att.close()
    P.full_barrier()
    st_lnT.close()
    P.full_barrier()
    st_o = ExitStack()
    owT = P.sb("owT", [128, 8 * D], BF16, st_o)
    P.op("gpsimd", lambda e: e.dma_start(
        out=owT[:].rearrange("p (k m) -> p k m", m=D),
        in_=owt_h.rearrange("(k p) m -> p k m", p=128)),
        writes=("owT",), dma=True)
    OBC = P.sb("OBC", [128, D], F32, st_o)
    emit_row_bcast(P, ob_h, "ob_h", OBC, "OBC", psB[:, 0:512], "psB.b0",
                   stack=st_o)
    pln = PostLN(P, "xs", xs, st_o, "po1",
                 nxt=hw.get("pre_stats", {}).get("lnd"))
    for i in range(NT):
        half = i % 2
        pres = f"psA.s{half}"
        fns = []
        for c in range(NCH):
            for n2 in range(2):
                fns.append(lambda e, i=i, c=c, n2=n2, half=half: e.matmul(
                    psA[:, half * 1024 + n2 * 512: half * 1024 + (n2 + 1) * 512],
                    attn[:, c * S + i * 128: c * S + (i + 1) * 128],
                    owT[:, c * D + n2 * 512: c * D + (n2 + 1) * 512],
                    start=(c == 0), stop=(c == 7)))
        P.group("tensor", fns, reads=("attn", "owT"), writes=(pres,))
        pln.chunk(i, psA[:, half * 1024:(half + 1) * 1024], pres,
                  bias_bcast=OBC, bias_res="OBC")
    st_o.close()
    P.full_barrier()
    st_attn.close()
    P.full_barrier()
    st_blk.close()
    P.full_barrier()



def mk_alpha(P, wupt_h, wdnt_h, st, st_ring):
    ainv_bc = [P.sb(f"ainv_bc{i}", [128, 1], F32, st, side="right")
               for i in range(2)]
    A = P.sb("alphaA", [128, 128], F32, st_ring, side="right")
    ring = [P.sb(f"al_wf{i}", [128, 512], F32, st_ring, side="right")
            for i in range(2)]
    idx = [0]

    def emit_abs(i):
        s = i % 2
        f = ring[s]
        rf = f"al_wf{s}"
        P.op("scalar", lambda e, f=f, i=i: e.activation(
            f[:], f[:], AF.Abs, accum_out=A[:, i:i + 1]),
            reads=(rf, "alphaA"), writes=(rf, "alphaA"))

    def emit_dma(i):
        s = i % 2
        f = ring[s]
        rf = f"al_wf{s}"
        if i < 64:
            c, q = divmod(i, 8)
            src = wupt_h[c * 128:(c + 1) * 128, q * 512:(q + 1) * 512]
        else:
            c, q = divmod(i - 64, 2)
            src = wdnt_h[c * 128:(c + 1) * 128, q * 512:(q + 1) * 512]
        P.op("scalar", lambda e, f=f, src=src: e.dma_start(
            out=f[:], in_=src), writes=(rf,), dma=True, dmaq="AL")

    def chunk():
        i = idx[0]
        if i >= 130:
            return
        idx[0] += 1
        if i >= 2:
            emit_abs(i - 2)
        if i < 128:
            emit_dma(i)

    return {"A": A, "ainv_bc": ainv_bc, "chunk": chunk, "idx": idx,
            "st_ring": st_ring}


def emit_block_ffn1(P, hw):
    xs = hw["xs"]
    wupt_h, wdnt_h = hw["wupt"], hw["wdnt"]
    identF = P.consts["identF"]
    wupq_hbm = P.nc.dram_tensor("wupq_scr", [D, DFF], BF16,
                                kind="Internal").ap()
    wdnq_hbm = P.nc.dram_tensor("wdnq_scr", [DFF, D], BF16,
                                kind="Internal").ap()

    al = hw["alpha"]
    while al["idx"][0] < 130:
        al["chunk"]()
    ainv_bc = al["ainv_bc"]
    A = al["A"]
    st_fin = ExitStack()
    arow = P.sb("arow", [1, 128], F32, st_fin)
    atot = P.sb("atot", [1, 4], F32, st_fin)
    ps_a = P.ps("ps_a", [128, 512], F32, st_fin)

    def alpha_finish(wi, lo, n, rows, cols):
        P.group("tensor", [lambda e, lo=lo, n=n: e.matmul(
            ps_a[:, 0:n][0:1, :], P.consts["ones_cf"][:], A[:, lo:lo + n],
            start=True, stop=True)],
            reads=("alphaA", "c_ones_cf"), writes=("ps_a",))
        P.op("vector", lambda e, lo=lo, n=n: e.tensor_copy(
            arow[:, lo:lo + n], ps_a[:, 0:n][0:1, :]),
            reads=("ps_a",), writes=(f"arow{wi}",))
        P.op("vector", lambda e, lo=lo, n=n, wi=wi: e.reduce_sum(
            atot[:, wi:wi + 1], arow[:, lo:lo + n], axis=AX.X),
            reads=(f"arow{wi}",), writes=(f"atot{wi}",))
        P.op("vector", lambda e, wi=wi: e.reciprocal(
            atot[:, 2 + wi:3 + wi], atot[:, wi:wi + 1]),
            reads=(f"atot{wi}",), writes=(f"arcp{wi}",))
        P.op("vector", lambda e, wi=wi, rows=rows, cols=cols: e.tensor_scalar(
            atot[:, 2 + wi:3 + wi], atot[:, 2 + wi:3 + wi],
            float(rows * cols), None, ALU.mult),
            reads=(f"arcp{wi}",), writes=(f"arcp{wi}",))
        P.op("tensor", lambda e, wi=wi: e.matmul(
            ps_a[:, 0:1], P.consts["ones_f"][0:1, :], atot[:, 2 + wi:3 + wi],
            start=True, stop=True),
            reads=(f"arcp{wi}", "c_ones_f"), writes=("ps_a",))
        P.op("vector", lambda e, wi=wi: e.tensor_copy(
            ainv_bc[wi][:], ps_a[:, 0:1]),
            reads=("ps_a",), writes=(f"ainv_bc{wi}",))

    alpha_finish(0, 0, 64, D, DFF)
    alpha_finish(1, 64, 64, DFF, D)
    st_fin.close()
    al["st_ring"].close()
    P.full_barrier()

    st_blk = ExitStack()
    xnT = P.sb("xnT", [128, NCH * S], BF16, st_blk)

    st_ln = ExitStack()
    pool = hw.get("pre_stats")
    if pool:
        stats = pool["lnd"]
        emit_preln_stats(P, "xs", xs, stats, pre=True)
    else:
        stats = make_stats(P, st_ln, "lnd")
        emit_preln_stats(P, "xs", xs, stats)
    sv = P.sb("lnd_sv", [128, NT], F32, st_ln)
    scl = P.sb("lnd_scl", [128, NT], F32, st_ln)
    nms = P.sb("lnd_nms", [128, NT], F32, st_ln)
    P.op("scalar", lambda e: e.activation(sv[:], stats["var"][:], AF.Sqrt),
         reads=("lnd.var",), writes=("lnd.sv",))
    P.op("vector", lambda e: e.tensor_tensor(
        sv[:], sv[:], stats["rstd"][:], ALU.mult),
        reads=("lnd.sv", "lnd.rstd"), writes=("lnd.rms",))
    P.op("vector", lambda e: e.tensor_scalar(
        sv[:], sv[:], float(EPS_RMS), None, ALU.add),
        reads=("lnd.rms",), writes=("lnd.d2",))
    P.op("vector", lambda e: e.reciprocal(scl[:], sv[:]),
         reads=("lnd.d2",), writes=("lnd.f",))
    P.op("vector", lambda e: e.tensor_tensor(
        scl[:], stats["rstd"][:], scl[:], ALU.mult),
        reads=("lnd.f", "lnd.rstd"), writes=("lnd.scl",))
    P.op("vector", lambda e: e.tensor_tensor(
        nms[:], stats["mean"][:], scl[:], ALU.mult),
        reads=("lnd.scl", "lnd.mean"), writes=("lnd.nms",))
    P.op("vector", lambda e: e.tensor_scalar(
        nms[:], nms[:], -1.0, None, ALU.mult),
        reads=("lnd.nms",), writes=("lnd.nms",))
    def xn_apply(t, dst, dres):
        P.op("scalar", lambda e, t=t, dst=dst: e.activation(
            dst[:], xs[:, t * D:(t + 1) * D], AF.Identity,
            bias=nms[:, t:t + 1], scale=scl[:, t:t + 1]),
            reads=("xs",), sreads=("lnd.nms", "lnd.scl"), writes=(dres,))

    tp_stack = ExitStack()
    emit_ln_transpose(P, xn_apply, xnT, "xnT", tp_stack)
    tp_stack.close()
    P.full_barrier()
    st_ln.close()
    P.full_barrier()

    st_mm = ExitStack()
    hbuf = P.sb("hbuf", [128, 32 * 512], BF16, st_mm)
    sqd = [P.sb(f"sqd{i}", [128, 512], F32, st_mm) for i in range(2)]
    r2row = P.sb("r2row", [1, 512], F32, st_mm)
    rstd2T = P.sb("rstd2T", [128, 4], F32, st_mm)
    pln = PostLN(P, "xs", xs, st_mm, "pod")

    def up_evac(hc, psum_ap, psum_res):
        P.op("scalar", lambda e, psum_ap=psum_ap, hc=hc: e.activation(
            hbuf[:, hc * 512:(hc + 1) * 512], psum_ap, AF.Gelu),
            reads=(psum_res,), writes=(f"hbuf.{hc}",))

    phase = {}

    def rms_mm(hc):
        ps_dn0 = phase["ps_dn0"]
        sq = sqd[hc % 2]
        rq = f"sqd{hc % 2}"
        P.op("tensor", lambda e, sq=sq, hc=hc, ps_dn0=ps_dn0: e.matmul(
            ps_dn0[:, 0:512][0:1, :], P.consts["ones_cf"][:], sq[:],
            start=(hc == 0), stop=(hc == 31)),
            reads=(rq, "c_ones_cf"), writes=("ps_dn0",))

    def after_up(hc):
        if hc > 0:
            rms_mm(hc - 1)
        sq = sqd[hc % 2]
        rq = f"sqd{hc % 2}"
        P.op("scalar", lambda e, sq=sq, hc=hc: e.activation(
            sq[:], hbuf[:, hc * 512:(hc + 1) * 512], AF.Square),
            reads=(f"hbuf.{hc}",), writes=(rq,))

    def down_pre(jt, _ps=[None]):
        rms_mm(31)
        ps_dn0 = phase["ps_dn0"]
        P.op("vector", lambda e: e.tensor_scalar(
            r2row[:], ps_dn0[:, 0:512][0:1, :], float(1.0 / DFF), None,
            ALU.mult), reads=("ps_dn0",), writes=("r2row",))
        P.op("scalar", lambda e: e.activation(
            r2row[:], r2row[:], AF.Sqrt), reads=("r2row",), writes=("r2row",))
        P.op("vector", lambda e: e.tensor_scalar(
            r2row[:], r2row[:], float(EPS_RMS), None, ALU.add),
            reads=("r2row",), writes=("r2row",))
        P.op("vector", lambda e: e.reciprocal(r2row[:], r2row[:]),
             reads=("r2row",), writes=("r2row",))
        for qc in range(4):
            P.op("tensor", lambda e, qc=qc: e.matmul(
                phase["ps_dn1"][:, 0:1], r2row[0:1, qc * 128:(qc + 1) * 128],
                P.consts["ones_f"][0:1, 0:1], start=True, stop=True),
                reads=("r2row", "c_ones_f"), writes=("ps_dn1",))
            P.op("vector", lambda e, qc=qc: e.tensor_copy(
                rstd2T[:, qc:qc + 1], phase["ps_dn1"][:, 0:1]),
                reads=("ps_dn1",), writes=("rstd2T",))
        return lambda qc: (rstd2T[:, qc:qc + 1], "rstd2T")

    emit_ffn_matmul_phase(P, xnT, "xnT", hbuf, wupq_hbm, wdnq_hbm, st_mm,
                          up_evac, down_pre, pln, phase_out=phase,
                          w1_src=wupt_h, w2_src=wdnt_h,
                          quant=(mk_ternary(P, ainv_bc[0][:], "ainv_bc0"),
                                 mk_ternary(P, ainv_bc[1][:], "ainv_bc1")),
                          after_up=after_up)
    st_mm.close()
    P.full_barrier()
    st_blk.close()
    P.full_barrier()



def build_fused():
    P = Prog()
    hw = {}
    hw["x"] = P.dram_in("x", [S, D], F32)
    for n, shp in (("w_dq", [D, DLAT]), ("w_uq", [DLAT, D]),
                   ("w_qr", [DLAT, 512]), ("w_dkv", [D, DLAT]),
                   ("w_uk", [DLAT, D]), ("w_uv", [DLAT, D]),
                   ("w_kr", [D, DHR]), ("w_o", [D, D]),
                   ("w1t", [D, DFF]), ("b1", [DFF]), ("w2t", [DFF, D]),
                   ("b2", [D]), ("qwt", [D, D]), ("kwt", [D, D]),
                   ("vwt", [D, D]), ("owt", [D, D]), ("qb", [D]), ("kb", [D]),
                   ("vb", [D]), ("ob", [D]), ("wupt", [D, DFF]),
                   ("wdnt", [DFF, D]), ("identF", [128, 128])):
        hw[n] = P.dram_in(n, shp, F32)
    for n, shp in (("identB", [128, 128]), ("cos32", [32, S]),
                   ("sig32", [32, S]), ("swap32", [32, 32]),
                   ("mask0", [128, 128])):
        hw[n] = P.dram_in(n, shp, BF16)
    out_h = P.dram_out("xout", [S, D], F32)

    identB = P.sb("s_identB", [128, 128], BF16)
    identF = P.sb("s_identF", [128, 128], F32)
    P.consts["identB"] = identB
    P.consts["identF"] = identF
    emit_consts(P, [("eps_ln", [128, 1], F32, EPS_LN),
                    ("ones_f", [1, 128], F32, 1.0),
                    ("ones_cb", [128, 1], BF16, 1.0),
                    ("ones_cf", [128, 1], F32, 1.0)])
    P.op("sync", lambda e: e.dma_start(out=identB[:], in_=hw["identB"]),
         writes=("c_identB",), dma=True)
    P.op("sync", lambda e: e.dma_start(out=identF[:], in_=hw["identF"]),
         writes=("c_identF",), dma=True)

    xs = P.sb("xs", [128, NT * D], F32, side="right")
    hw["xs"] = xs
    hw["pre_stats"] = {t: make_stats(P, None, t, with_scratch=False)
                       for t in ("lnq", "ln1", "lnd")}
    emit_load_x(P, hw["x"], xs)

    emit_block_l0(P, hw)
    emit_block_ffn0(P, hw)
    st_alpha = ExitStack()
    st_ring = ExitStack()
    hw["alpha"] = mk_alpha(P, hw["wupt"], hw["wdnt"], st_alpha, st_ring)
    emit_block_l1(P, hw)
    emit_block_ffn1(P, hw)
    st_alpha.close()
    P.full_barrier(include_bg=True)
    emit_store_x(P, "xs", xs, out_h)
    return P.finalize()


_BUILD_CACHE = {}


def _get(builder, name):
    if name not in _BUILD_CACHE:
        _BUILD_CACHE[name] = builder()
    return _BUILD_CACHE[name]


def kernel(**inputs):
    f32 = lambda a: np.ascontiguousarray(np.asarray(a, dtype=np.float32))
    x = f32(inputs["L0_attn_x"])
    _ = inputs.get("L1_attn_mask")
    cos32, sig32, swap32 = _rope_tables()
    identB = np.ascontiguousarray(np.eye(128, dtype=BF))
    identF = np.ascontiguousarray(np.eye(128, dtype=np.float32))
    mask0 = _mask_tri()

    w_qr = f32(inputs["L0_attn_W_QR"])[:, _qr_perm()]
    w_kr = f32(inputs["L0_attn_W_KR"])[:, _kr_perm()]

    common = dict(
        w_dq=f32(inputs["L0_attn_W_DQ"]), w_uq=f32(inputs["L0_attn_W_UQ"]),
        w_qr=np.ascontiguousarray(w_qr), w_dkv=f32(inputs["L0_attn_W_DKV"]),
        w_uk=f32(inputs["L0_attn_W_UK"]), w_uv=f32(inputs["L0_attn_W_UV"]),
        w_kr=np.ascontiguousarray(w_kr), w_o=f32(inputs["L0_attn_W_O"]),
        identB=identB, cos32=cos32, sig32=sig32, swap32=swap32, mask0=mask0,
        w1t=np.ascontiguousarray(f32(inputs["L0_ffn_w1"]).T),
        b1=f32(inputs["L0_ffn_b1"]),
        w2t=np.ascontiguousarray(f32(inputs["L0_ffn_w2"]).T),
        b2=f32(inputs["L0_ffn_b2"]), identF=identF,
        qwt=np.ascontiguousarray(f32(inputs["L1_attn_q_w"]).T),
        kwt=np.ascontiguousarray(f32(inputs["L1_attn_k_w"]).T),
        vwt=np.ascontiguousarray(f32(inputs["L1_attn_v_w"]).T),
        owt=np.ascontiguousarray(f32(inputs["L1_attn_out_w"]).T),
        qb=f32(inputs["L1_attn_q_b"]), kb=f32(inputs["L1_attn_k_b"]),
        vb=f32(inputs["L1_attn_v_b"]), ob=f32(inputs["L1_attn_out_b"]),
        wupt=np.ascontiguousarray(f32(inputs["L1_ffn_w_up"]).T),
        wdnt=np.ascontiguousarray(f32(inputs["L1_ffn_w_down"]).T))

    nc = _get(build_fused, "fused")
    in_maps = [dict(x=np.ascontiguousarray(x[b]), **common) for b in range(8)]
    res = run_bass_kernel_spmd(nc, in_maps, core_ids=list(range(8)))
    out = np.stack([r["xout"] for r in res.results], axis=0)
    return np.ascontiguousarray(out.astype(np.float32))


# version 25 (final)
# speedup vs baseline: 1.0281x; 1.0158x over previous
import math
from contextlib import ExitStack

import numpy as np
import ml_dtypes

import concourse.bass as bass
import concourse.mybir as mybir
from concourse.bass_utils import run_bass_kernel_spmd

F32 = mybir.dt.float32
BF16 = mybir.dt.bfloat16
I8 = mybir.dt.int8
AF = mybir.ActivationFunctionType
ALU = mybir.AluOpType
AX = mybir.AxisListType

S = 2048
D = 1024
NT = 16
NCH = 8
H = 16
DHR = 32
DLAT = 256
DFF = 4096
EPS_LN = 1e-5
EPS_RMS = 1e-8
BF = ml_dtypes.bfloat16

ENGS = ("sync", "gpsimd", "tensor", "vector", "scalar")
QSEM = {"sync": "LD", "gpsimd": "G", "tensor": "T", "vector": "V",
        "scalar": "Sc"}


class Prog:
    def __init__(self):
        self.nc = bass.Bass("TRN2", target_bir_lowering=False)
        self.ops = {e: [] for e in ENGS}
        self.ctx = ExitStack()
        self.sems = {}
        for name in ("T", "V", "Sc", "G"):
            self.sems[name] = [self.ctx.enter_context(self.nc.semaphore(name)), 0]
        self.NSLOT = 4
        for q in ("LD", "GD", "ST", "AL"):
            for i in range(self.NSLOT):
                n = f"{q}{i}"
                self.sems[n] = [self.ctx.enter_context(self.nc.semaphore(n)), 0]
        self.dma_idx = {"LD": 0, "GD": 0, "ST": 0, "AL": 0}
        self.consts = {}
        self.suppress_same_engine = False
        self._names = {}
        self.last_write = {}
        self.readers = {}

    def _uniq(self, name):
        n = self._names.get(name, 0)
        self._names[name] = n + 1
        return name if n == 0 else f"{name}_{n}"

    def sb(self, name, shape, dt, stack=None, side=None):
        return (stack or self.ctx).enter_context(
            self.nc.sbuf_tensor(self._uniq(name), shape, dt, side=side))

    def ps(self, name, shape, dt, stack=None):
        return (stack or self.ctx).enter_context(
            self.nc.psum_tensor(self._uniq(name), shape, dt))

    def dram_in(self, name, shape, dt):
        return self.nc.dram_tensor(name, shape, dt, kind="ExternalInput").ap()

    def dram_out(self, name, shape, dt):
        return self.nc.dram_tensor(name, shape, dt, kind="ExternalOutput").ap()

    def _compute_waits(self, mysem, reads, writes, sreads):
        need = {}

        def add(dep):
            if dep is None:
                return
            sn, v = dep
            if need.get(sn, 0) < v:
                need[sn] = v

        sup = getattr(self, "suppress_same_engine", True)
        for r in tuple(reads) + tuple(writes):
            lw = self.last_write.get(r)
            if lw is not None and (not sup or lw[0] != mysem):
                add(lw)
        for r in tuple(sreads):
            add(self.last_write.get(r))
        for r in tuple(writes):
            for sn, v in self.readers.get(r, {}).items():
                if not sup or sn != mysem:
                    add((sn, v))
        return sorted(need.items())

    def _record(self, mysem, count, reads, writes, sreads):
        for r in tuple(reads) + tuple(sreads):
            self.readers.setdefault(r, {})[mysem] = count
        for r in tuple(writes):
            self.last_write[r] = (mysem, count)
            self.readers[r] = {}

    def op(self, eng, f, reads=(), writes=(), sreads=(), dma=False,
           extra=(), dmaq=None):
        if dma:
            q = dmaq or {"sync": "LD", "gpsimd": "GD", "scalar": "ST"}[eng]
            slot = self.dma_idx[q] % self.NSLOT
            self.dma_idx[q] += 1
            sem = f"{q}{slot}"
            incv = 16
            gate = (sem, self.sems[sem][1])
        else:
            sem = QSEM[eng]
            incv = 1
            gate = None
        waits = self._compute_waits(sem if not dma else "__dma__",
                                    reads, writes, sreads)
        if gate is not None and gate[1] > 0:
            waits = list(waits) + [gate]
        for sn, v in extra:
            waits = list(waits) + [(sn, v)]
        self.sems[sem][1] += incv
        count = self.sems[sem][1]
        sems = self.sems

        def run(e, f=f, waits=tuple(waits), sem=sem, incv=incv):
            for sn, v in waits:
                e.wait_ge(sems[sn][0], v)
            f(e).then_inc(sems[sem][0], incv)

        self.ops[eng].append(run)
        self._record(sem, count, reads, writes, sreads)
        return (sem, count)

    def group(self, eng, fns, reads=(), writes=(), sreads=()):
        sem = QSEM[eng]
        waits = self._compute_waits(sem, reads, writes, sreads)
        self.sems[sem][1] += 1
        count = self.sems[sem][1]
        sems = self.sems

        def run(e, fns=tuple(fns), waits=tuple(waits), sem=sem):
            for sn, v in waits:
                e.wait_ge(sems[sn][0], v)
            for f in fns[:-1]:
                f(e)
            fns[-1](e).then_inc(sems[sem][0], 1)

        self.ops[eng].append(run)
        self._record(sem, count, reads, writes, sreads)
        return (sem, count)

    def barrier_wait(self, eng, deps):
        sems = self.sems
        wl = tuple(d for d in deps if d is not None)

        def run(e, wl=wl):
            for sn, v in wl:
                e.wait_ge(sems[sn][0], v)

        self.ops[eng].append(run)

    def dep(self, res):
        return self.last_write.get(res)

    def full_barrier(self, include_bg=False):
        deps = [(n, s[1]) for n, s in self.sems.items()
                if s[1] > 0 and (include_bg or not n.startswith("AL"))]
        for eng in ENGS:
            self.barrier_wait(eng, deps)

    def finalize(self):
        with self.nc.Block() as block:
            for eng in ENGS:
                fns = self.ops[eng]
                getattr(block, eng)(lambda e, fns=fns: [f(e) for f in fns])
        self.ctx.close()
        return self.nc



def _rope_tables():
    j = np.arange(16, dtype=np.float64)
    inv = 10000.0 ** (-j / 16.0)
    t = np.arange(S, dtype=np.float64)
    ang = inv[:, None] * t[None, :]
    cos = np.cos(ang)
    sin = np.sin(ang)
    cos32 = np.ascontiguousarray(np.concatenate([cos, cos], 0).astype(BF))
    sig32 = np.ascontiguousarray(np.concatenate([-sin, sin], 0).astype(BF))
    P = np.zeros((32, 32), dtype=BF)
    for k in range(16):
        P[16 + k, k] = 1.0
        P[k, 16 + k] = 1.0
    return cos32, sig32, P


def _qr_perm():
    perm = []
    for h in range(H):
        perm += [h * 32 + 2 * k for k in range(16)]
        perm += [h * 32 + 2 * k + 1 for k in range(16)]
    return np.array(perm)


def _kr_perm():
    return np.array([2 * k for k in range(16)] + [2 * k + 1 for k in range(16)])


def _mask_tri():
    k = np.arange(128)[:, None]
    q = np.arange(128)[None, :]
    return np.ascontiguousarray((q >= k).astype(BF))


def _pieces(lo, hi, grid=512):
    out = []
    while lo < hi:
        n = min(hi - lo, grid - (lo % grid))
        out.append((lo, n))
        lo += n
    return out



def emit_consts(P, specs):
    for name, shape, dt, valv in specs:
        t = P.sb("c_" + name, shape, dt)
        P.op("vector", lambda e, t=t, v=valv: e.memset(t[:], v),
             writes=("c_" + name,))
        P.consts[name] = t


def emit_load_x(P, x_h, xs, res="xs"):
    P.op("sync", lambda e: e.dma_start(
        out=xs[:].rearrange("p (t d) -> p t d", d=D),
        in_=x_h.rearrange("(t p) d -> p t d", p=128)),
        writes=(res,), dma=True)


def make_stats(P, stack, tag, with_scratch=True):
    d = {"tag": tag}
    for n in ("sum", "ssq", "mean", "var", "m2", "sqv", "rstd", "nmr", "r2"):
        d[n] = P.sb(f"{tag}_{n}", [128, NT], F32, stack)
    if with_scratch:
        d["sq0"] = P.sb(f"{tag}_sq0", [128, D], F32, stack)
        d["sq1"] = P.sb(f"{tag}_sq1", [128, D], F32, stack)
    return d


def emit_preln_stats(P, xs_res, xs, st, pre=False):
    tg = st["tag"]
    if not pre:
        P.op("vector", lambda e: e.reduce_sum(
            st["sum"][:], xs[:].rearrange("p (t d) -> p t d", d=D), axis=AX.X),
            reads=(xs_res,), writes=(tg + ".sum",))
        for t in range(NT):
            scr = st["sq0"] if t % 2 == 0 else st["sq1"]
            r = tg + (".sq0" if t % 2 == 0 else ".sq1")
            P.op("scalar", lambda e, t=t, scr=scr: e.activation(
                scr[:], xs[:, t * D:(t + 1) * D], AF.Square),
                reads=(xs_res,), writes=(r,))
            P.op("vector", lambda e, t=t, scr=scr: e.reduce_sum(
                st["ssq"][:, t:t + 1], scr[:], axis=AX.X),
                reads=(r,), writes=(tg + f".ssq{t}",))
    P.op("vector", lambda e: e.tensor_scalar(
        st["mean"][:], st["sum"][:], 1.0 / D, None, ALU.mult),
        reads=(tg + ".sum",), writes=(tg + ".mean",))
    P.op("vector", lambda e: e.tensor_scalar(
        st["var"][:], st["ssq"][:], 1.0 / D, None, ALU.mult),
        reads=tuple(tg + f".ssq{t}" for t in range(NT)),
        writes=(tg + ".var",))
    P.op("vector", lambda e: e.tensor_tensor(
        st["m2"][:], st["mean"][:], st["mean"][:], ALU.mult),
        reads=(tg + ".mean",), writes=(tg + ".m2",))
    P.op("vector", lambda e: e.tensor_tensor(
        st["var"][:], st["var"][:], st["m2"][:], ALU.subtract),
        reads=(tg + ".var", tg + ".m2"), writes=(tg + ".var",))
    P.op("scalar", lambda e: e.activation(
        st["sqv"][:], st["var"][:], AF.Sqrt, bias=P.consts["eps_ln"][:]),
        reads=(tg + ".var",), sreads=("c_eps_ln",), writes=(tg + ".sqv",))
    P.op("vector", lambda e: e.reciprocal(st["rstd"][:], st["sqv"][:]),
         reads=(tg + ".sqv",), writes=(tg + ".rstd",))
    P.op("vector", lambda e: e.tensor_tensor(
        st["nmr"][:], st["mean"][:], st["rstd"][:], ALU.mult),
        reads=(tg + ".mean", tg + ".rstd"), writes=(tg + ".nmr",))
    P.op("vector", lambda e: e.tensor_scalar(
        st["nmr"][:], st["nmr"][:], -1.0, None, ALU.mult),
        reads=(tg + ".nmr",), writes=(tg + ".nmr",))


def mk_ln_apply(P, xs_res, xs, st, bias_key="nmr", scale_key="rstd"):
    tg = st["tag"]

    def apply_tile(t, dst, dres):
        P.op("scalar", lambda e, t=t, dst=dst: e.activation(
            dst[:], xs[:, t * D:(t + 1) * D], AF.Identity,
            bias=st[bias_key][:, t:t + 1], scale=st[scale_key][:, t:t + 1]),
            reads=(xs_res,), sreads=(tg + "." + bias_key, tg + "." + scale_key),
            writes=(dres,))

    return apply_tile


def emit_ln_transpose(P, apply_tile, fmT, fm_res, tp_stack):
    pt = [P.ps(f"tp_pt{i}", [128, 1024], BF16, tp_stack) for i in range(2)]
    lnr = [P.sb(f"tp_ln{i}", [128, D], BF16, tp_stack) for i in range(2)]
    identB = P.consts["identB"]
    for t in range(NT):
        s = t % 2
        lt, lres = lnr[s], f"tp_ln{s}"
        pp, pres = pt[s], f"tp_pt{s}"
        apply_tile(t, lt, lres)
        fns = []
        for c in range(NCH):
            fns.append(lambda e, c=c, lt=lt, pp=pp: e.transpose(
                pp[:, c * 128:(c + 1) * 128],
                lt[:, c * 128:(c + 1) * 128], identB[:]))
        P.group("tensor", fns, reads=(lres, "c_identB"), writes=(pres,))
        P.op("vector", lambda e, t=t, pp=pp: e.tensor_copy(
            fmT[:].rearrange("p (c k) -> p c k", k=S)[:, :, t * 128:(t + 1) * 128],
            pp[:].rearrange("p (c k) -> p c k", k=128)),
            reads=(pres,), writes=(fm_res,))


def emit_store_x(P, xs_res, xs, out_h):
    P.op("scalar", lambda e: e.dma_start(
        out=out_h.rearrange("(t p) d -> p t d", p=128),
        in_=xs[:].rearrange("p (t d) -> p t d", d=D)),
        reads=(xs_res,), dma=True)
    dep = P.dep_st()
    P.barrier_wait("sync", dep)
    P.barrier_wait("tensor", dep)
    P.barrier_wait("vector", dep)


def _dep_st(self):
    return [(f"ST{i}", self.sems[f"ST{i}"][1]) for i in range(self.NSLOT)
            if self.sems[f"ST{i}"][1] > 0]


Prog.dep_st = _dep_st


class PostLN:

    def __init__(self, P, xs_res, xs, stack, tag, nxt=None):
        self.P, self.xs, self.xs_res = P, xs, xs_res
        self.st = make_stats(P, stack, tag)
        self.o = [P.sb(tag + "_o0", [128, D], F32, stack),
                  P.sb(tag + "_o1", [128, D], F32, stack)]
        self.tag = tag
        self.nxt = nxt

    def chunk(self, i, psum_ap, psum_res, bias_bcast=None, bias_res=None,
              scale=None, scale_res=None):
        P, st, tg = self.P, self.st, self.tag
        if isinstance(psum_res, str):
            psum_res = (psum_res,)
        o = self.o[i % 2]
        ores = f"{tg}.o{i % 2}"
        if bias_bcast is not None:
            P.op("vector", lambda e: e.tensor_tensor(
                o[:], psum_ap, bias_bcast[:], ALU.add),
                reads=psum_res + (bias_res,), writes=(ores,))
        elif scale is not None:
            P.op("scalar", lambda e: e.activation(
                o[:], psum_ap, AF.Identity, scale=scale),
                reads=psum_res, sreads=(scale_res,), writes=(ores,))
        else:
            P.op("vector", lambda e: e.tensor_copy(o[:], psum_ap),
                 reads=psum_res, writes=(ores,))
        P.op("vector", lambda e, i=i: e.reduce_sum(
            st["sum"][:, i:i + 1], o[:], axis=AX.X),
            reads=(ores,), writes=(tg + ".sum",))
        P.op("scalar", lambda e: e.activation(
            st["sq0"][:], o[:], AF.Square),
            reads=(ores,), writes=(tg + ".sq0",))
        P.op("vector", lambda e, i=i: e.reduce_sum(
            st["ssq"][:, i:i + 1], st["sq0"][:], axis=AX.X),
            reads=(tg + ".sq0",), writes=(tg + ".ssq",))
        P.op("vector", lambda e, i=i: e.tensor_scalar(
            st["mean"][:, i:i + 1], st["sum"][:, i:i + 1], 1.0 / D, None,
            ALU.mult), reads=(tg + ".sum",), writes=(tg + ".mean",))
        P.op("vector", lambda e, i=i: e.tensor_scalar(
            st["var"][:, i:i + 1], st["ssq"][:, i:i + 1], 1.0 / D, None,
            ALU.mult), reads=(tg + ".ssq",), writes=(tg + ".var",))
        P.op("vector", lambda e, i=i: e.tensor_tensor(
            st["m2"][:, i:i + 1], st["mean"][:, i:i + 1],
            st["mean"][:, i:i + 1], ALU.mult),
            reads=(tg + ".mean",), writes=(tg + ".m2",))
        P.op("vector", lambda e, i=i: e.tensor_tensor(
            st["var"][:, i:i + 1], st["var"][:, i:i + 1],
            st["m2"][:, i:i + 1], ALU.subtract),
            reads=(tg + ".var", tg + ".m2"), writes=(tg + ".var",))
        P.op("scalar", lambda e, i=i: e.activation(
            st["sqv"][:, i:i + 1], st["var"][:, i:i + 1], AF.Sqrt,
            bias=P.consts["eps_ln"][:]),
            reads=(tg + ".var",), sreads=("c_eps_ln",), writes=(tg + ".sqv",))
        P.op("vector", lambda e, i=i: e.reciprocal(
            st["rstd"][:, i:i + 1], st["sqv"][:, i:i + 1]),
            reads=(tg + ".sqv",), writes=(tg + ".rstd",))
        P.op("vector", lambda e, o=o, i=i: e.tensor_scalar(
            st["sq1"][:], o[:], st["mean"][:, i:i + 1],
            st["rstd"][:, i:i + 1], ALU.subtract, ALU.mult),
            reads=(ores,), sreads=(tg + ".mean", tg + ".rstd"),
            writes=(tg + ".sq1",))
        P.op("vector", lambda e, i=i: e.tensor_tensor(
            self.xs[:, i * D:(i + 1) * D], self.xs[:, i * D:(i + 1) * D],
            st["sq1"][:], ALU.add),
            reads=(tg + ".sq1", self.xs_res), writes=(self.xs_res,))
        if self.nxt is not None:
            nx, ng = self.nxt, self.nxt["tag"]
            P.op("vector", lambda e, i=i: e.reduce_sum(
                nx["sum"][:, i:i + 1], self.xs[:, i * D:(i + 1) * D],
                axis=AX.X), reads=(self.xs_res,), writes=(ng + ".sum",))
            P.op("scalar", lambda e, i=i: e.activation(
                st["sq0"][:], self.xs[:, i * D:(i + 1) * D], AF.Square,
                accum_out=nx["ssq"][:, i:i + 1]),
                reads=(self.xs_res,),
                writes=(tg + ".sq0", ng + f".ssq{i}"))



def emit_attention_head(P, h, qk_lhsT, qk_rhs, qk_res, v_km, attn, psA, psB,
                        pbuf, causal, exp_scale, exp_bias, exp_bias_res,
                        clamp, mask0, scrA, rrow, astage, extra_items=(),
                        evac_eng="scalar"):
    items = []
    half_last = {}
    for half in range(2):
        for kc in range(NT):
            q_lo, q_hi = half * 1024, half * 1024 + 1024
            q_start = max(q_lo, kc * 128) if causal else q_lo
            if q_start < q_hi:
                items.append((kc, half, q_start, q_hi))
                half_last[half] = len(items) - 1
    N = len(items)
    ex = list(extra_items)
    M = N + len(ex)

    def slot_of(n):
        s = n % 3
        half = items[n][1] if n < N else 1
        unsafe = (n < N and half == 1 and n < half_last[0] + 7)
        if s == 2 and unsafe:
            s = n % 2
        if s < 2:
            return psA, s * 1024, (f"psA.s{s}",)
        w = (1 - half) * 1024
        return psB, w, (f"psB.b{2 * (1 - half)}", f"psB.b{2 * (1 - half) + 1}")

    def emit_scores(n):
        kc, half, q_start, q_hi = items[n]
        pst, sbase, pres = slot_of(n)
        base = sbase + (q_start - half * 1024)
        end = sbase + 1024
        fns = []
        for (lo, nn) in _pieces(base, end):
            gq = half * 1024 + (lo - sbase)
            fns.append(lambda e, kc=kc, lo=lo, nn=nn, gq=gq, pst=pst: e.matmul(
                pst[:, lo:lo + nn], qk_lhsT(kc), qk_rhs(gq, nn),
                start=True, stop=True))
        P.group("tensor", fns, reads=qk_res(kc, half), writes=pres)

    def emit_expmask(n):
        kc, half, q_start, q_hi = items[n]
        pst, sbase, pres = slot_of(n)
        base = sbase + (q_start - half * 1024)
        end = sbase + 1024
        pslot = n % 2
        pbres = f"pbuf.s{pslot}"
        pb = pbuf[:, pslot * 1024 + (q_start - half * 1024): pslot * 1024 + 1024]
        sc_ap = pst[:, base:end]
        kw = {"scale": exp_scale}
        sr = ()
        if exp_bias is not None:
            kw["bias"] = exp_bias
            sr = (exp_bias_res,)
        P.op("scalar", lambda e, sc_ap=sc_ap, pb=pb, kw=kw: e.activation(
            pb, sc_ap, AF.Exp, **kw),
            reads=pres, sreads=sr, writes=(pbres,))
        if clamp is not None:
            P.op("vector", lambda e, pb=pb: e.tensor_scalar(
                pb, pb, clamp[1], clamp[0], ALU.min, ALU.max),
                reads=(pbres,), writes=(pbres,))
        if causal:
            dlo = kc * 128
            if q_start <= dlo < q_hi:
                loc = dlo - q_start
                P.op("vector", lambda e, pb=pb, loc=loc: e.tensor_tensor(
                    pb[:, loc:loc + 128], pb[:, loc:loc + 128], mask0[:],
                    ALU.mult), reads=(pbres, "c_mask0"), writes=(pbres,))

    def emit_pv(n):
        kc, half, q_start, q_hi = items[n]
        pslot = n % 2
        pbres = f"pbuf.s{pslot}"
        pb = pbuf[:, pslot * 1024 + (q_start - half * 1024): pslot * 1024 + 1024]
        fns = []
        wr = []
        for (gq, nn) in _pieces(q_start, q_hi):
            blk = gq // 512
            if causal:
                firstkc = 0
                lastkc = min(15, (blk * 512 + 511) // 128)
            else:
                firstkc, lastkc = 0, 15
            wr.append(f"psB.b{blk}")
            fns.append(lambda e, kc=kc, gq=gq, nn=nn, h=h, pb=pb,
                       q_start=q_start, lastkc=lastkc: e.matmul(
                           psB[:, gq:gq + nn][0:65, :],
                           v_km[:, kc * (H * 65) + h * 65:
                                kc * (H * 65) + h * 65 + 65],
                           pb[:, gq - q_start: gq - q_start + nn],
                           start=(kc == 0), stop=(kc == lastkc)))
        P.group("tensor", fns, reads=(pbres, "v_km"), writes=tuple(wr))

    odd = (h % 2 == 1)

    def emit_norm_s1(b):
        bres = f"psB.b{b}"
        rr = rrow[b % 2]
        P.op("vector", lambda e, b=b, rr=rr: e.reciprocal(
            rr[:], psB[64:65, b * 512:(b + 1) * 512]),
            reads=(bres,), writes=(f"rrow{b % 2}",))
        scr = scrA[b % 2]
        sres = f"scrA{b % 2}"
        if evac_eng == "scalar":
            P.op("scalar", lambda e, b=b, scr=scr: e.activation(
                scr[:], psB[0:64, b * 512:(b + 1) * 512], AF.Identity),
                reads=(bres,), writes=(sres,))
        else:
            P.op("vector", lambda e, b=b, scr=scr: e.tensor_copy(
                scr[:], psB[0:64, b * 512:(b + 1) * 512]),
                reads=(bres,), writes=(sres,))

    def emit_norm_s2(b):
        bres = f"psB.b{b}"
        rr = rrow[b % 2]
        scr = scrA[b % 2]
        sres = f"scrA{b % 2}"
        P.op("tensor", lambda e, b=b, rr=rr: e.matmul(
            psB[0:64, b * 512:(b + 1) * 512], P.consts["ones_f"][0:1, 0:64],
            rr[:], start=True, stop=True),
            reads=(f"rrow{b % 2}", "c_ones_f", sres), writes=(bres,))
        col0 = (h // 2) * S + b * 512
        if not odd:
            P.op("vector", lambda e, scr=scr, col0=col0, b=b:
                 e.tensor_tensor(attn[0:64, col0:col0 + 512],
                                 scr[:], psB[0:64, b * 512:(b + 1) * 512],
                                 ALU.mult),
                 reads=(sres, bres), writes=("attn",))
        else:
            P.op("vector", lambda e, scr=scr, b=b:
                 e.tensor_tensor(astage[:, b * 512:(b + 1) * 512],
                                 scr[:], psB[0:64, b * 512:(b + 1) * 512],
                                 ALU.mult),
                 reads=(sres, bres), writes=("astage",))

    norm_s1q = []
    norm_s2q = []

    def norm_tick(n):
        if norm_s2q and norm_s2q[0][0] <= n:
            emit_norm_s2(norm_s2q.pop(0)[1])
        if norm_s1q:
            b = norm_s1q.pop(0)
            emit_norm_s1(b)
            norm_s2q.append((n + 2, b))

    for n in range(M + 2):
        if 1 <= n < M + 1:
            m = n - 1
            if m < N:
                emit_expmask(m)
            else:
                ex[m - N]["post"](slot_of(m))
        if n < M:
            if n < N:
                emit_scores(n)
            else:
                ex[n - N]["mm"](slot_of(n))
        if n >= 2:
            if n - 2 < N:
                emit_pv(n - 2)
            norm_tick(n)
            if n - 2 == half_last.get(0):
                norm_s1q += [0, 1]
            if n - 2 == half_last.get(1):
                norm_s1q += [2, 3]
    for b in norm_s1q:
        emit_norm_s1(b)
        norm_s2q.append((0, b))
    for _, b in norm_s2q:
        emit_norm_s2(b)
    if odd:
        col0 = (h // 2) * S
        P.op("gpsimd", lambda e, col0=col0: e.dma_start(
            out=attn[64:128, col0:col0 + S], in_=astage[:]),
            reads=("astage",), writes=("attn",), dma=True)



def emit_block_l0(P, hw):
    xs = hw["xs"]
    w_dq, w_uq, w_qr = hw["w_dq"], hw["w_uq"], hw["w_qr"]
    w_dkv, w_uk, w_uv = hw["w_dkv"], hw["w_uk"], hw["w_uv"]
    w_kr, w_o = hw["w_kr"], hw["w_o"]
    st_blk = ExitStack()
    c_qb = P.sb("c_qb", [128, 2 * S], BF16, st_blk)
    c_kvb = P.sb("c_kvb", [128, 2 * S], BF16, st_blk)
    v_km = P.sb("v_km", [128, NT * H * 65], BF16, st_blk)
    kr_b = P.sb("kr_b", [32, S], BF16, st_blk)
    mask0 = P.sb("s_mask0", [128, 128], BF16, st_blk)
    cos32 = P.sb("s_cos32", [32, S], BF16, st_blk)
    sig32 = P.sb("s_sig32", [32, S], BF16, st_blk)
    swap32 = P.sb("s_swap32", [32, 32], BF16, st_blk)
    scr1 = P.sb("ropes1", [32, 1024], BF16, st_blk)
    P.op("vector", lambda e: e.memset(
        v_km[:].rearrange("p (k c) -> p k c", c=65)[:, :, 64:65], 1.0),
        writes=("v_km.ones",))
    for dst, srch, r in ((mask0, hw["mask0"], "c_mask0"),
                         (cos32, hw["cos32"], "cos32"),
                         (sig32, hw["sig32"], "sig32"),
                         (swap32, hw["swap32"], "swap32")):
        P.op("sync", lambda e, dst=dst, srch=srch: e.dma_start(
            out=dst[:], in_=srch), writes=(r,), dma=True)

    st_pre = ExitStack()
    w_dqb = P.sb("w_dqb", [128, 8 * DLAT], BF16, st_pre)
    w_dkvb = P.sb("w_dkvb", [128, 8 * DLAT], BF16, st_pre)
    w_krb = P.sb("w_krb", [128, 8 * DHR], BF16, st_pre)
    w_uvb = P.sb("w_uvb", [128, 2 * D], BF16, st_pre)
    for dst, src, dd, r in ((w_dqb, w_dq, DLAT, "w_dqb"),
                           (w_dkvb, w_dkv, DLAT, "w_dkvb"),
                           (w_krb, w_kr, DHR, "w_krb"),
                           (w_uvb, w_uv, D, "w_uvb")):
        P.op("gpsimd", lambda e, dst=dst, src=src, dd=dd: e.dma_start(
            out=dst[:].rearrange("p (k m) -> p k m", m=dd),
            in_=src.rearrange("(k p) m -> p k m", p=128)),
            writes=(r,), dma=True)

    stats = make_stats(P, st_pre, "ln0")
    emit_preln_stats(P, "xs", xs, stats)
    lnT = P.sb("lnT", [128, NCH * S], BF16, st_pre)
    tp_stack = ExitStack()
    emit_ln_transpose(P, mk_ln_apply(P, "xs", xs, stats), lnT, "lnT", tp_stack)
    tp_stack.close()
    P.full_barrier()

    psA = P.ps("psA", [128, 2048], F32, st_blk)
    psB = P.ps("psB", [128, 2048], F32, st_blk)
    PSA = ("psA.s0", "psA.s1")
    PSB = tuple(f"psB.b{b}" for b in range(4))

    for (wt, wres, dstb, dres) in ((w_dqb, "w_dqb", c_qb, "c_qb"),
                                   (w_dkvb, "w_dkvb", c_kvb, "c_kvb")):
        for m in range(2):
            fns = []
            for nq in range(4):
                for k in range(8):
                    fns.append(lambda e, m=m, nq=nq, k=k, wt=wt: e.matmul(
                        psA[:, nq * 512:(nq + 1) * 512],
                        wt[:, k * DLAT + m * 128: k * DLAT + (m + 1) * 128],
                        lnT[:, k * S + nq * 512: k * S + (nq + 1) * 512],
                        start=(k == 0), stop=(k == 7)))
            P.group("tensor", fns, reads=(wres, "lnT"), writes=PSA)
            P.op("vector", lambda e, dstb=dstb, m=m: e.tensor_copy(
                dstb[:, m * S:(m + 1) * S], psA[:]),
                reads=PSA, writes=(dres,))

    krr = P.sb("krraw", [32, S], BF16, st_pre)
    fns = []
    for nq in range(4):
        for k in range(8):
            fns.append(lambda e, nq=nq, k=k: e.matmul(
                psA[:, nq * 512:(nq + 1) * 512][0:32, :],
                w_krb[:, k * DHR:(k + 1) * DHR],
                lnT[:, k * S + nq * 512: k * S + (nq + 1) * 512],
                start=(k == 0), stop=(k == 7)))
    P.group("tensor", fns, reads=("w_krb", "lnT"), writes=PSA)
    P.op("vector", lambda e: e.tensor_copy(krr[:], psA[:, 0:2048][0:32, :]),
         reads=PSA, writes=("krraw",))
    fns = [lambda e, nq=nq: e.matmul(
        psB[:, nq * 512:(nq + 1) * 512][0:32, :],
        swap32[:], krr[:, nq * 512:(nq + 1) * 512],
        start=True, stop=True) for nq in range(4)]
    P.group("tensor", fns, reads=("krraw", "swap32"), writes=PSB)
    for hf in range(2):
        c0, c1 = hf * 1024, (hf + 1) * 1024
        P.op("vector", lambda e, c0=c0, c1=c1: e.tensor_tensor(
            scr1[:, 0:1024], krr[:, c0:c1], cos32[:, c0:c1], ALU.mult),
            reads=("krraw", "cos32"), writes=("rp",))
        P.op("vector", lambda e, c0=c0, c1=c1: e.tensor_tensor(
            kr_b[:, c0:c1], psB[:, c0:c1][0:32, :], sig32[:, c0:c1],
            ALU.mult),
            reads=PSB + ("sig32",), writes=("kr_b",))
        P.op("vector", lambda e, c0=c0, c1=c1: e.tensor_tensor(
            kr_b[:, c0:c1], scr1[:, 0:1024], kr_b[:, c0:c1], ALU.add),
            reads=("rp", "kr_b"), writes=("kr_b",))

    for kc in range(NT):
        fns = []
        for lk in range(2):
            for n2 in range(2):
                fns.append(lambda e, kc=kc, lk=lk, n2=n2: e.matmul(
                    psA[:, n2 * 512:(n2 + 1) * 512],
                    c_kvb[:, lk * S + kc * 128: lk * S + (kc + 1) * 128],
                    w_uvb[:, lk * D + n2 * 512: lk * D + (n2 + 1) * 512],
                    start=(lk == 0), stop=(lk == 1)))
        P.group("tensor", fns, reads=("c_kvb", "w_uvb"),
                writes=("psA.s0",))
        P.op("vector", lambda e, kc=kc: e.tensor_copy(
            v_km[:, kc * H * 65:(kc + 1) * H * 65].rearrange(
                "p (h c) -> p h c", c=65)[:, :, 0:64],
            psA[:, 0:1024].rearrange("p (h c) -> p h c", c=64)),
            reads=("psA.s0", "v_km.ones"), writes=("v_km",))
    st_pre.close()
    P.full_barrier()

    st_att = ExitStack()
    w_uqb = P.sb("w_uqb", [128, 2 * D], BF16, st_att)
    w_qrb = P.sb("w_qrb", [128, 2 * 512], BF16, st_att)
    w_ukb = P.sb("w_ukb", [128, 2 * D], BF16, st_att)
    for dst, srcw, dd, r in ((w_uqb, w_uq, D, "w_uqb"),
                            (w_qrb, w_qr, 512, "w_qrb"),
                            (w_ukb, w_uk, D, "w_ukb")):
        P.op("gpsimd", lambda e, dst=dst, srcw=srcw, dd=dd: e.dma_start(
            out=dst[:].rearrange("p (k m) -> p k m", m=dd),
            in_=srcw.rearrange("(k p) m -> p k m", p=128)),
            writes=(r,), dma=True)

    qh = [P.sb(f"qh{i}", [96, S], BF16, st_att) for i in range(2)]
    kh = [P.sb(f"kh{i}", [96, S], BF16, st_att) for i in range(2)]
    qraw = P.sb("qraw", [32, S], BF16, st_att)
    qrstage = qraw
    st_attn = ExitStack()
    attn = P.sb("attn", [128, NCH * S], BF16, st_attn, side="right")
    astage = P.sb("astage", [64, S], BF16, st_att)
    pbuf = P.sb("pbuf", [128, 2 * 1024], BF16, st_att)
    scrA = (P.sb("scrA0", [64, 512], BF16, st_att),
            P.sb("scrA1", [64, 512], BF16, st_att))
    rrow = [P.sb(f"rrow{i}", [1, 512], F32, st_att) for i in range(2)]

    for i in range(2):
        P.op("gpsimd", lambda e, i=i: e.dma_start(
            out=kh[i][64:96, :], in_=kr_b[:]),
            reads=("kr_b",), writes=(f"kh{i}.r",), dma=True)

    EXP_SCALE = float(1.0 / math.sqrt(96.0))
    CL_HI = 1.0
    CL_LO = float(np.float32(np.exp(np.float64(-160.0) / math.sqrt(96.0))))
    nb80 = P.sb("nb80", [128, 1], F32, st_att)
    P.op("vector", lambda e: e.memset(nb80[:], float(-80.0 / math.sqrt(96.0))),
         writes=("nb80",))

    def prod_items(h):
        hb = h % 2
        qht, kht = qh[hb], kh[hb]
        qres, kres = f"qh{hb}", f"kh{hb}"

        def proj(wb, wres, wbase, wstep, wid, src, sres, rows, dst_fn, dres,
                 half):
            def mm(slot, half=half):
                pst, sbase, pres = slot
                fns = []
                for nqg in range(2):
                    nq = half * 2 + nqg
                    for lk in range(2):
                        fns.append(lambda e, nq=nq, nqg=nqg, lk=lk, pst=pst,
                                   sbase=sbase: e.matmul(
                            pst[:, sbase + nqg * 512:
                                sbase + (nqg + 1) * 512][0:rows, :],
                            wb[:, lk * wstep + wbase:
                               lk * wstep + wbase + wid],
                            src[:, lk * S + nq * 512: lk * S + (nq + 1) * 512],
                            start=(lk == 0), stop=(lk == 1)))
                P.group("tensor", fns, reads=(wres, sres), writes=pres)

            def post(slot, half=half):
                pst, sbase, pres = slot
                P.op("scalar", lambda e, pst=pst, sbase=sbase, half=half:
                     e.activation(dst_fn(half),
                                  pst[:, sbase:sbase + 1024][0:rows, :],
                                  AF.Identity),
                     reads=pres, writes=(dres,))
            return {"mm": mm, "post": post}

        def qc(half):
            return proj(w_uqb, "w_uqb", h * 64, D, 64, c_qb, "c_qb", 64,
                        lambda hf: qht[0:64, hf * 1024:(hf + 1) * 1024],
                        f"{qres}.c{half}", half)

        def qr(half):
            return proj(w_qrb, "w_qrb", h * 32, 512, 32, c_qb, "c_qb", 32,
                        lambda hf: qraw[:, hf * 1024:(hf + 1) * 1024],
                        f"qrw{half}", half)

        def kc_(half):
            return proj(w_ukb, "w_ukb", h * 64, D, 64, c_kvb, "c_kvb", 64,
                        lambda hf: kht[0:64, hf * 1024:(hf + 1) * 1024],
                        f"{kres}.c{half}", half)

        def sw(half):
            hc0, hc1 = half * 1024, (half + 1) * 1024

            def mm(slot):
                pst, sbase, pres = slot
                fns = []
                for nqg in range(2):
                    nq = half * 2 + nqg
                    fns.append(lambda e, nq=nq, nqg=nqg, pst=pst,
                               sbase=sbase: e.matmul(
                        pst[:, sbase + nqg * 512:
                            sbase + (nqg + 1) * 512][0:32, :],
                        swap32[:], qraw[:, nq * 512:(nq + 1) * 512],
                        start=True, stop=True))
                P.group("tensor", fns, reads=(f"qrw{half}", "swap32"),
                        writes=pres)

            def post(slot):
                pst, sbase, pres = slot
                P.op("vector", lambda e: e.tensor_tensor(
                    scr1[:, 0:1024], qraw[:, hc0:hc1], cos32[:, hc0:hc1],
                    ALU.mult),
                    reads=(f"qrw{half}", "cos32"), writes=("rp",))
                P.op("vector", lambda e, pst=pst, sbase=sbase:
                     e.tensor_tensor(
                         qraw[:, hc0:hc1], pst[:, sbase:sbase + 1024][0:32, :],
                         sig32[:, hc0:hc1], ALU.mult),
                     reads=pres + ("sig32", f"qrw{half}"),
                     writes=(f"qrw{half}",))
                P.op("vector", lambda e: e.tensor_tensor(
                    qraw[:, hc0:hc1], scr1[:, 0:1024], qraw[:, hc0:hc1],
                    ALU.add),
                    reads=("rp", f"qrw{half}"), writes=(f"qrw{half}",))
                P.op("gpsimd", lambda e: e.dma_start(
                    out=qht[64:96, hc0:hc1], in_=qraw[:, hc0:hc1]),
                    reads=(f"qrw{half}",), writes=(f"{qres}.r{half}",),
                    dma=True)
            return {"mm": mm, "post": post}

        return [qc(0), qr(0), qc(1), qr(1), sw(0), kc_(0), sw(1), kc_(1)]

    for i, it in enumerate(prod_items(0)):
        slot = (psA, (i % 2) * 1024, (f"psA.s{i % 2}",))
        it["mm"](slot)
        it["post"](slot)

    for h in range(H):
        hb = h % 2
        qht, kht = qh[hb], kh[hb]
        qres, kres = f"qh{hb}", f"kh{hb}"

        def qk_lhsT(kc, kht=kht):
            return kht[:, kc * 128:(kc + 1) * 128]

        def qk_rhs(lo, width, qht=qht):
            return qht[:, lo:lo + width]

        def qk_res(kc, half, qres=qres, kres=kres, hb=hb):
            return (f"{qres}.c{half}", f"{qres}.r{half}",
                    f"{kres}.c{kc // 8}", f"kh{hb}.r", "v_km")

        emit_attention_head(
            P, h, qk_lhsT, qk_rhs, qk_res,
            v_km, attn, psA, psB, pbuf, causal=True, exp_scale=EXP_SCALE,
            exp_bias=nb80[:], exp_bias_res="nb80", clamp=(CL_LO, CL_HI),
            mask0=mask0, scrA=scrA, rrow=rrow, astage=astage,
            extra_items=prod_items(h + 1) if h + 1 < H else ())

    st_att.close()
    P.full_barrier()
    st_o = ExitStack()
    w_ob = P.sb("w_ob", [128, 8 * D], BF16, st_o)
    P.op("gpsimd", lambda e: e.dma_start(
        out=w_ob[:].rearrange("p (k m) -> p k m", m=D),
        in_=w_o.rearrange("(k p) m -> p k m", p=128)),
        writes=("w_ob",), dma=True)
    pln = PostLN(P, "xs", xs, st_o, "po0",
                 nxt=hw.get("pre_stats", {}).get("lnq"))
    for i in range(NT):
        half = i % 2
        pres = f"psA.s{half}"
        fns = []
        for c in range(NCH):
            for n2 in range(2):
                fns.append(lambda e, i=i, c=c, n2=n2, half=half: e.matmul(
                    psA[:, half * 1024 + n2 * 512: half * 1024 + (n2 + 1) * 512],
                    attn[:, c * S + i * 128: c * S + (i + 1) * 128],
                    w_ob[:, c * D + n2 * 512: c * D + (n2 + 1) * 512],
                    start=(c == 0), stop=(c == 7)))
        P.group("tensor", fns, reads=("attn", "w_ob"), writes=(pres,))
        pln.chunk(i, psA[:, half * 1024:(half + 1) * 1024], pres)
    st_o.close()
    P.full_barrier()
    st_attn.close()
    P.full_barrier()
    st_blk.close()
    P.full_barrier()



def emit_quant_preproc(P, src_h, dst_h, dst_res, rows, cols, scale, st_w,
                       tag, ternary=None, bufs=None):
    if bufs is None:
        wf = [P.sb(f"{tag}_wf{i}", [128, cols], F32, st_w) for i in range(2)]
        wi = [P.sb(f"{tag}_wi{i}", [128, cols], I8, st_w) for i in range(2)]
        wb = [P.sb(f"{tag}_wb{i}", [128, cols], BF16, st_w) for i in range(2)]
    else:
        wf, wi, wb = bufs
    for c in range(rows // 128):
        s = c % 2
        f, ii, b = wf[s], wi[s], wb[s]
        rf, ri, rb = f"qp.wf{s}", f"qp.wi{s}", f"qp.wb{s}"
        P.op("sync", lambda e, c=c, f=f: e.dma_start(
            out=f[:, 0:cols], in_=src_h[c * 128:(c + 1) * 128, :]),
            writes=(rf,), dma=True)
        if ternary is None:
            P.op("vector", lambda e, f=f, ii=ii: e.tensor_scalar(
                ii[:, 0:cols], f[:, 0:cols], float(scale), None, ALU.mult),
                reads=(rf,), writes=(ri,))
            P.op("scalar", lambda e, ii=ii, b=b: e.activation(
                b[:, 0:cols], ii[:, 0:cols], AF.Identity,
                scale=float(1.0 / scale)),
                reads=(ri,), writes=(rb,))
        else:
            ainv_ap, ainv_res = ternary
            P.op("vector", lambda e, f=f, ii=ii: e.tensor_scalar(
                ii[:, 0:cols], f[:, 0:cols], ainv_ap, None, ALU.mult),
                reads=(rf,), sreads=(ainv_res,), writes=(ri,))
            P.op("vector", lambda e, ii=ii, b=b: e.tensor_scalar(
                b[:, 0:cols], ii[:, 0:cols], 1.0, -1.0, ALU.min, ALU.max),
                reads=(ri,), writes=(rb,))
        P.op("scalar", lambda e, c=c, b=b: e.dma_start(
            out=dst_h[c * 128:(c + 1) * 128, :], in_=b[:, 0:cols]),
            reads=(rb,), writes=(dst_res,), dma=True)


def emit_ffn_matmul_phase(P, xT, xT_res, hbuf, w1_hbm, w2_hbm, st_mm,
                          up_evac, down_pre, pln, down_bias=None,
                          down_bias_res=None, phase_out=None,
                          w1_src=None, w2_src=None, quant=None,
                          after_up=None):
    w1band = [P.sb(f"w1b{i}", [128, 8 * 512], BF16, st_mm) for i in range(2)]
    w2band = [P.sb(f"w2b{i}", [128, 2 * D], BF16, st_mm) for i in range(2)]
    wqf = [P.sb(f"wqf{i}", [128, 2048], F32, st_mm) for i in range(3)]
    wqi = [P.sb(f"wqi{i}", [128, 2048], I8, st_mm) for i in range(2)]
    ps_up = P.ps("ps_up", [128, 1024], F32, st_mm)
    ps_dn = [P.ps(f"ps_dn{i}", [128, 1024], F32, st_mm) for i in range(3)]
    if phase_out is not None:
        phase_out["ps_dn0"] = ps_dn[0]
        phase_out["ps_dn1"] = ps_dn[1]

    if isinstance(quant, tuple):
        quant1, quant2 = quant
    else:
        quant1 = quant2 = quant

    def dn_slot(qc):
        if qc < 3:
            return ps_dn[qc], (f"ps_dn{qc}",)
        return ps_up, ("ps_up.0", "ps_up.1")

    def load_w1_band(hg, bs, first):
        if not first:
            P.op("gpsimd", lambda e, hg=hg, bs=bs: e.dma_start(
                out=w1band[bs][:].rearrange("p (k m) -> p k m", m=512),
                in_=w1_hbm[:, hg * 512:(hg + 1) * 512].rearrange(
                    "(k p) m -> p k m", p=128)),
                reads=("w1q_hbm",), writes=(f"w1b{bs}",), dma=True)
            return
        for half in range(2):
            s = (hg * 2 + half) % 3
            si = (hg * 2 + half) % 2
            fr, ir = f"wqf{s}", f"wqi{si}"
            P.op("sync" if half == 0 else "gpsimd",
                 lambda e, hg=hg, half=half, s=s: e.dma_start(
                     out=wqf[s][:].rearrange("p (k m) -> p k m", m=256),
                     in_=w1_src[:, hg * 512 + half * 256:
                                hg * 512 + half * 256 + 256].rearrange(
                         "(k p) m -> p k m", p=128)),
                 writes=(fr,), dma=True)
            quant1(wqf[s][:], fr, wqi[si][:], ir,
                   w1band[bs][:].rearrange("p (k m) -> p k m", m=512)
                   [:, :, half * 256:(half + 1) * 256],
                   f"w1b{bs}")
        P.op("scalar", lambda e, hg=hg, bs=bs: e.dma_start(
            out=w1_hbm[:, hg * 512:(hg + 1) * 512].rearrange(
                "(k p) m -> p k m", p=128),
            in_=w1band[bs][:].rearrange("p (k m) -> p k m", m=512)),
            reads=(f"w1b{bs}",), writes=("w1q_hbm",), dma=True)

    def load_w2_chunk(hg, bs, first):
        if not first:
            P.op("gpsimd", lambda e, hg=hg, bs=bs: e.dma_start(
                out=w2band[bs][:].rearrange("p (k m) -> p k m", m=D),
                in_=w2_hbm[hg * 256:(hg + 1) * 256, :].rearrange(
                    "(k p) m -> p k m", p=128)),
                reads=("w2q_hbm",), writes=(f"w2b{bs}",), dma=True)
            return
        for half in range(2):
            s = (hg * 2 + half) % 3
            si = (hg * 2 + half) % 2
            fr, ir = f"wqf{s}", f"wqi{si}"
            P.op("sync" if half == 0 else "gpsimd",
                 lambda e, hg=hg, half=half, s=s: e.dma_start(
                     out=wqf[s][:, 0:1024],
                     in_=w2_src[hg * 256 + half * 128:
                                hg * 256 + half * 128 + 128, :]),
                 writes=(fr,), dma=True)
            quant2(wqf[s][:, 0:1024], fr, wqi[si][:, 0:1024], ir,
                   w2band[bs][:, half * D:(half + 1) * D], f"w2b{bs}")
        P.op("scalar", lambda e, hg=hg, bs=bs: e.dma_start(
            out=w2_hbm[hg * 256:(hg + 1) * 256, :].rearrange(
                "(k p) m -> p k m", p=128),
            in_=w2band[bs][:].rearrange("p (k m) -> p k m", m=D)),
            reads=(f"w2b{bs}",), writes=("w2q_hbm",), dma=True)

    for jt in range(4):
        for hg in range(8):
            bs = hg % 2
            load_w1_band(hg, bs, jt == 0)
            for hx in range(4):
                hc = hg * 4 + hx
                slot = hc % 2
                pres = f"ps_up.{slot}"
                fns = []
                for k in range(8):
                    fns.append(lambda e, jt=jt, k=k, slot=slot, bs=bs, hx=hx:
                               e.matmul(
                                   ps_up[:, slot * 512:(slot + 1) * 512],
                                   w1band[bs][:, k * 512 + hx * 128:
                                              k * 512 + hx * 128 + 128],
                                   xT[:, k * S + jt * 512:
                                      k * S + jt * 512 + 512],
                                   start=(k == 0), stop=(k == 7)))
                P.group("tensor", fns, reads=(f"w1b{bs}", xT_res),
                        writes=(pres,))
                up_evac(hc, ps_up[:, slot * 512:(slot + 1) * 512], pres)
                if after_up is not None:
                    after_up(hc)
        scale_maker = down_pre(jt) if down_pre is not None else None
        for hg in range(16):
            bs = hg % 2
            load_w2_chunk(hg, bs, jt == 0)
            for hx in range(2):
                hc = hg * 2 + hx
                for qc in range(4):
                    dnp, dres = dn_slot(qc)
                    fns = []
                    for n2 in range(2):
                        fns.append(lambda e, hc=hc, qc=qc, n2=n2, bs=bs,
                                   hx=hx, dnp=dnp: e.matmul(
                                       dnp[:, n2 * 512:(n2 + 1) * 512],
                                       hbuf[:, hc * 512 + qc * 128:
                                            hc * 512 + qc * 128 + 128],
                                       w2band[bs][:, hx * D + n2 * 512:
                                                  hx * D + (n2 + 1) * 512],
                                       start=(hc == 0), stop=(hc == 31)))
                    P.group("tensor", fns, reads=(f"w2b{bs}", f"hbuf.{hc}"),
                            writes=dres)
        for qc in (3, 0, 1, 2):
            i = jt * 4 + qc
            dnp, dres = dn_slot(qc)
            sc = scale_maker(qc) if scale_maker is not None else None
            if sc is not None:
                pln.chunk(i, dnp[:], dres, scale=sc[0], scale_res=sc[1])
            else:
                pln.chunk(i, dnp[:], dres, bias_bcast=down_bias,
                          bias_res=down_bias_res)


def mk_quant64(P):
    def quant(f_ap, f_res, i_ap, i_res, dst_ap, dst_res):
        P.op("vector", lambda e: e.tensor_scalar(
            i_ap, f_ap, 64.0, None, ALU.mult), reads=(f_res,), writes=(i_res,))
        P.op("scalar", lambda e: e.activation(
            dst_ap, i_ap, AF.Identity, scale=1.0 / 64.0),
            reads=(i_res,), writes=(dst_res,))
    return quant


def mk_ternary(P, ainv_ap, ainv_res):
    def quant(f_ap, f_res, i_ap, i_res, dst_ap, dst_res):
        P.op("vector", lambda e: e.tensor_scalar(
            i_ap, f_ap, ainv_ap, None, ALU.mult),
            reads=(f_res,), sreads=(ainv_res,), writes=(i_res,))
        P.op("vector", lambda e: e.tensor_scalar(
            dst_ap, i_ap, 1.0, -1.0, ALU.min, ALU.max),
            reads=(i_res,), writes=(dst_res,))
    return quant


def emit_block_ffn0(P, hw):
    xs = hw["xs"]
    w1t_h, b1_h, w2t_h, b2_h = hw["w1t"], hw["b1"], hw["w2t"], hw["b2"]
    identF = P.consts["identF"]
    w1q_hbm = P.nc.dram_tensor("w1q_scr", [D, DFF], BF16, kind="Internal").ap()
    w2q_hbm = P.nc.dram_tensor("w2q_scr", [DFF, D], BF16, kind="Internal").ap()
    st_blk = ExitStack()
    xqT = P.sb("xqT", [128, NCH * S], BF16, st_blk)
    B16 = P.sb("B16", [128, 32], F32, st_blk)
    B2C = P.sb("B2C", [128, D], BF16, st_blk)



    ps_small_stack = ExitStack()
    ps_small = P.ps("ps_small", [128, 512], F32, ps_small_stack)
    b1r = P.sb("b1r", [32, 128], F32, ps_small_stack)
    b1i = P.sb("b1i", [32, 128], I8, ps_small_stack)
    b1f = P.sb("b1f", [32, 128], F32, ps_small_stack)
    P.op("sync", lambda e: e.dma_start(
        out=b1r[:], in_=b1_h.rearrange("(c p) -> c p", p=128)),
        writes=("b1r",), dma=True)
    P.op("vector", lambda e: e.tensor_scalar(
        b1i[:], b1r[:], 64.0, None, ALU.mult), reads=("b1r",), writes=("b1i",))
    P.op("scalar", lambda e: e.activation(
        b1f[:], b1i[:], AF.Identity, scale=0.25), reads=("b1i",),
        writes=("b1f",))
    P.op("tensor", lambda e: e.transpose(
        ps_small[:, 0:32], b1f[:], identF[0:32, 0:32]),
        reads=("b1f", "c_identF"), writes=("ps_small",))
    P.op("vector", lambda e: e.tensor_copy(B16[:], ps_small[:, 0:32]),
         reads=("ps_small",), writes=("B16",))
    b2r = P.sb("b2r", [1, D], F32, ps_small_stack)
    b2i = P.sb("b2i", [1, D], I8, ps_small_stack)
    b2f = P.sb("b2f", [1, D], F32, ps_small_stack)
    P.op("sync", lambda e: e.dma_start(
        out=b2r[:], in_=b2_h.rearrange("(o d) -> o d", o=1)),
        writes=("b2r",), dma=True)
    P.op("vector", lambda e: e.tensor_scalar(
        b2i[:], b2r[:], 64.0, None, ALU.mult), reads=("b2r",), writes=("b2i",))
    P.op("scalar", lambda e: e.activation(
        b2f[:], b2i[:], AF.Identity, scale=1.0 / 64.0), reads=("b2i",),
        writes=("b2f",))
    for n2 in range(2):
        P.op("tensor", lambda e, n2=n2: e.matmul(
            ps_small[:, 0:512], P.consts["ones_f"][0:1, :],
            b2f[:, n2 * 512:(n2 + 1) * 512], start=True, stop=True),
            reads=("b2f", "c_ones_f"), writes=("ps_small",))
        P.op("vector", lambda e, n2=n2: e.tensor_copy(
            B2C[:, n2 * 512:(n2 + 1) * 512], ps_small[:, 0:512]),
            reads=("ps_small",), writes=("B2C",))
    ps_small_stack.close()
    P.full_barrier()

    st_ln = ExitStack()
    pool = hw.get("pre_stats")
    if pool:
        stats = pool["lnq"]
        emit_preln_stats(P, "xs", xs, stats, pre=True)
    else:
        stats = make_stats(P, st_ln, "lnq")
        emit_preln_stats(P, "xs", xs, stats)
    r16 = stats["r2"]
    P.op("vector", lambda e: e.tensor_scalar(
        r16[:], stats["rstd"][:], 16.0, None, ALU.mult),
        reads=("lnq.rstd",), writes=("lnq.r16",))
    xq_i8 = [P.sb(f"xq_i8{i}", [128, D], I8, st_ln) for i in range(2)]

    def xq_apply(t, dst, dres):
        ii = xq_i8[t % 2]
        ri = f"xqi8.{t % 2}"
        P.op("vector", lambda e, t=t, ii=ii: e.tensor_scalar(
            ii[:], xs[:, t * D:(t + 1) * D], stats["mean"][:, t:t + 1],
            r16[:, t:t + 1], ALU.subtract, ALU.mult),
            reads=("xs",), sreads=("lnq.mean", "lnq.r16"), writes=(ri,))
        P.op("scalar", lambda e, ii=ii, dst=dst: e.activation(
            dst[:], ii[:], AF.Identity, scale=1.0 / 16.0),
            reads=(ri,), writes=(dres,))

    tp_stack = ExitStack()
    emit_ln_transpose(P, xq_apply, xqT, "xqT", tp_stack)
    tp_stack.close()
    P.full_barrier()
    st_ln.close()
    P.full_barrier()

    st_mm = ExitStack()
    hbuf = P.sb("hbuf", [128, 32 * 512], BF16, st_mm)
    hscr = [P.sb(f"hscr{i}", [128, 512], I8, st_mm) for i in range(2)]
    pln = PostLN(P, "xs", xs, st_mm, "poq",
                 nxt=hw.get("pre_stats", {}).get("ln1"))

    def up_evac(hc, psum_ap, psum_res):
        ii = hscr[hc % 2]
        ri = f"hscr{hc % 2}"
        P.op("vector", lambda e, ii=ii, psum_ap=psum_ap, hc=hc:
             e.tensor_scalar(ii[:], psum_ap, 16.0, B16[:, hc:hc + 1],
                             ALU.mult, ALU.add),
             reads=(psum_res,), sreads=("B16",), writes=(ri,))
        P.op("scalar", lambda e, ii=ii, hc=hc: e.activation(
            hbuf[:, hc * 512:(hc + 1) * 512], ii[:], AF.Relu, scale=1.0 / 16.0),
            reads=(ri,), writes=(f"hbuf.{hc}",))

    emit_ffn_matmul_phase(P, xqT, "xqT", hbuf, w1q_hbm, w2q_hbm, st_mm,
                          up_evac, None, pln, down_bias=B2C,
                          down_bias_res="B2C", w1_src=w1t_h, w2_src=w2t_h,
                          quant=mk_quant64(P))
    st_mm.close()
    P.full_barrier()
    st_blk.close()
    P.full_barrier()



def emit_vec_to_partition(P, src_h, src_res, n, dst, dst_res, ps_ap, ps_res,
                          identF, post_scale=1.0, stack=None):
    nch = n // 128
    r = P.sb(dst_res + "_r", [nch, 128], F32, stack)
    P.op("sync", lambda e: e.dma_start(
        out=r[:], in_=src_h.rearrange("(c p) -> c p", p=128)),
        writes=(src_res,), dma=True)
    if post_scale != 1.0:
        P.op("vector", lambda e: e.tensor_scalar(
            r[:], r[:], float(post_scale), None, ALU.mult),
            reads=(src_res,), writes=(src_res,))
    P.op("tensor", lambda e: e.transpose(
        ps_ap[:, 0:nch], r[:], identF[0:nch, 0:nch]),
        reads=(src_res, "c_identF"), writes=(ps_res,))
    P.op("vector", lambda e: e.tensor_copy(dst[:], ps_ap[:, 0:nch]),
         reads=(ps_res,), writes=(dst_res,))


def emit_row_bcast(P, src_h, src_res, dst, dst_res, ps_ap, ps_res, n=D,
                   stack=None):
    row = P.sb(dst_res + "_row", [1, n], F32, stack)
    P.op("sync", lambda e: e.dma_start(
        out=row[:], in_=src_h.rearrange("(o d) -> o d", o=1)),
        writes=(src_res,), dma=True)
    for n2 in range(n // 512):
        P.op("tensor", lambda e, n2=n2: e.matmul(
            ps_ap[:, 0:512], P.consts["ones_f"][0:1, :],
            row[:, n2 * 512:(n2 + 1) * 512], start=True, stop=True),
            reads=(src_res, "c_ones_f"), writes=(ps_res,))
        P.op("vector", lambda e, n2=n2: e.tensor_copy(
            dst[:, n2 * 512:(n2 + 1) * 512], ps_ap[:, 0:512]),
            reads=(ps_res,), writes=(dst_res,))


def emit_block_l1(P, hw):
    xs = hw["xs"]
    qwt_h, kwt_h, vwt_h, owt_h = hw["qwt"], hw["kwt"], hw["vwt"], hw["owt"]
    qb_h, kb_h, vb_h, ob_h = hw["qb"], hw["kb"], hw["vb"], hw["ob"]
    identF = P.consts["identF"]
    st_blk = ExitStack()
    v_km = P.sb("v_km", [128, NT * H * 65], BF16, st_blk)
    P.op("vector", lambda e: e.memset(
        v_km[:].rearrange("p (k c) -> p k c", c=65)[:, :, 64:65], 1.0),
        writes=("v_km.ones",))

    if "qb8" in hw:
        qb8, kbp = hw["qb8"], hw["kbp"]
    else:
        qb8 = P.sb("qb8", [128, 8], F32, st_blk)
        kbp = P.sb("kbp", [128, 8], F32, st_blk)
        pss = ExitStack()
        ps_small = P.ps("ps_small", [128, 512], F32, pss)
        emit_vec_to_partition(P, qb_h, "qb_h", D, qb8, "qb8", ps_small,
                              "ps_small", identF, post_scale=0.125, stack=pss)
        emit_vec_to_partition(P, kb_h, "kb_h", D, kbp, "kbp", ps_small,
                              "ps_small", identF, stack=pss)
        pss.close()
        P.full_barrier()

    st_lnT = ExitStack()
    lnT = P.sb("lnT", [128, NCH * S], BF16, st_lnT)
    st_ln = ExitStack()
    pool = hw.get("pre_stats")
    if pool:
        stats = pool["ln1"]
        emit_preln_stats(P, "xs", xs, stats, pre=True)
    else:
        stats = make_stats(P, st_ln, "ln1")
        emit_preln_stats(P, "xs", xs, stats)
    tp_stack = ExitStack()
    emit_ln_transpose(P, mk_ln_apply(P, "xs", xs, stats), lnT, "lnT", tp_stack)
    tp_stack.close()
    P.full_barrier()
    st_ln.close()
    P.full_barrier()

    psA = P.ps("psA", [128, 2048], F32, st_blk)
    psB = P.ps("psB", [128, 2048], F32, st_blk)
    PSA = ("psA.s0", "psA.s1")
    PSB = tuple(f"psB.b{b}" for b in range(4))

    st_v = ExitStack()
    vwT = P.sb("vwT", [128, 8 * D], BF16, st_v)
    VBC = P.sb("VBC", [128, D], F32, st_v)
    emit_row_bcast(P, vb_h, "vb_h", VBC, "VBC", psB[:, 0:512], "psB.b0",
                   stack=st_v)
    P.op("gpsimd", lambda e: e.dma_start(
        out=vwT[:].rearrange("p (k m) -> p k m", m=D),
        in_=vwt_h.rearrange("(k p) m -> p k m", p=128)),
        writes=("vwT",), dma=True)
    for kc in range(NT):
        fns = []
        for dc in range(NCH):
            for n2 in range(2):
                fns.append(lambda e, kc=kc, dc=dc, n2=n2: e.matmul(
                    psA[:, n2 * 512:(n2 + 1) * 512],
                    lnT[:, dc * S + kc * 128: dc * S + (kc + 1) * 128],
                    vwT[:, dc * D + n2 * 512: dc * D + (n2 + 1) * 512],
                    start=(dc == 0), stop=(dc == 7)))
        P.group("tensor", fns, reads=("lnT", "vwT"), writes=("psA.s0",))
        P.op("vector", lambda e, kc=kc: e.tensor_tensor(
            v_km[:, kc * H * 65:(kc + 1) * H * 65].rearrange(
                "p (h c) -> p h c", c=65)[:, :, 0:64],
            psA[:, 0:1024].rearrange("p (h c) -> p h c", c=64),
            VBC[:].rearrange("p (h c) -> p h c", c=64), ALU.add),
            reads=("psA.s0", "v_km.ones", "VBC"), writes=("v_km",))
        if "alpha" in hw:
            for _ in range(2):
                hw["alpha"]["chunk"]()
    st_v.close()
    P.full_barrier()

    st_att = ExitStack()
    qwb = [P.sb(f"qwb{i}", [128, 8 * 128], BF16, st_att) for i in range(2)]
    kwb = [P.sb(f"kwb{i}", [128, 8 * 128], BF16, st_att) for i in range(2)]
    qpair_b = [P.sb(f"qpair{i}", [128, S], BF16, st_att) for i in range(2)]
    kpair_b = [P.sb(f"kpair{i}", [128, S], BF16, st_att) for i in range(2)]
    st_attn = ExitStack()
    attn = P.sb("attn", [128, NCH * S], BF16, st_attn, side="right")
    astage = P.sb("astage", [64, S], BF16, st_att)
    pbuf = P.sb("pbuf", [128, 2 * 1024], BF16, st_att)
    scrA = (P.sb("scrA0", [64, 512], BF16, st_att),
            P.sb("scrA1", [64, 512], BF16, st_att))
    rrow = [P.sb(f"rrow{i}", [1, 512], F32, st_att) for i in range(2)]

    mm2 = P.sb("mm2", [1, 4], F32, st_att)
    negb = [P.sb(f"negb{i}", [128, 1], F32, st_att) for i in range(2)]

    def pair_loads(j):
        bs = j % 2
        P.op("gpsimd", lambda e, j=j, bs=bs: e.dma_start(
            out=qwb[bs][:].rearrange("p (k m) -> p k m", m=128),
            in_=qwt_h[:, j * 128:(j + 1) * 128].rearrange(
                "(k p) m -> p k m", p=128)),
            writes=(f"qwb{bs}",), dma=True)
        P.op("gpsimd", lambda e, j=j, bs=bs: e.dma_start(
            out=kwb[bs][:].rearrange("p (k m) -> p k m", m=128),
            in_=kwt_h[:, j * 128:(j + 1) * 128].rearrange(
                "(k p) m -> p k m", p=128)),
            writes=(f"kwb{bs}",), dma=True)

    def pair_prod_items(j):
        bs = j % 2

        def proj(wb, wres, dst, dres, sc, bias, bres, half):
            def mm(slot, half=half, wb=wb):
                pst, sbase, pres = slot
                fns = []
                for nqg in range(2):
                    nq = half * 2 + nqg
                    for dc in range(NCH):
                        fns.append(lambda e, nq=nq, nqg=nqg, dc=dc, wb=wb,
                                   pst=pst, sbase=sbase: e.matmul(
                            pst[:, sbase + nqg * 512:
                                sbase + (nqg + 1) * 512],
                            wb[:, dc * 128:(dc + 1) * 128],
                            lnT[:, dc * S + nq * 512: dc * S + (nq + 1) * 512],
                            start=(dc == 0), stop=(dc == 7)))
                P.group("tensor", fns, reads=(wres, "lnT"), writes=pres)

            def post(slot, half=half, dst=dst):
                pst, sbase, pres = slot
                P.op("vector", lambda e, pst=pst, sbase=sbase, half=half:
                     e.tensor_scalar(dst[:, half * 1024:(half + 1) * 1024],
                                     pst[:, sbase:sbase + 1024], sc,
                                     bias[:, j:j + 1], ALU.mult, ALU.add),
                     reads=pres, sreads=(bres,), writes=(dres,))
            return {"mm": mm, "post": post}

        q0 = proj(qwb[bs], f"qwb{bs}", qpair_b[bs], f"qpair{bs}.0", 0.125,
                  qb8, "qb8", 0)
        q1 = proj(qwb[bs], f"qwb{bs}", qpair_b[bs], f"qpair{bs}.1", 0.125,
                  qb8, "qb8", 1)
        k0 = proj(kwb[bs], f"kwb{bs}", kpair_b[bs], f"kpair{bs}.0", 1.0,
                  kbp, "kbp", 0)
        k1 = proj(kwb[bs], f"kwb{bs}", kpair_b[bs], f"kpair{bs}.1", 1.0,
                  kbp, "kbp", 1)
        return [q0, k0, q1, k1]

    def pair_bounds(j):
        bs = j % 2
        qpair, kpair = qpair_b[bs], kpair_b[bs]
        qpr = (f"qpair{bs}.0", f"qpair{bs}.1")
        kpr = (f"kpair{bs}.0", f"kpair{bs}.1")
        bsq = pbuf
        BSQR = ("pbuf.s0", "pbuf.s1")
        for (srcp, sres, mcol) in ((qpair, qpr, 0), (kpair, kpr, 1)):
            P.op("vector", lambda e, srcp=srcp:
                 e.tensor_tensor(bsq[:], srcp[:], srcp[:], ALU.mult),
                 reads=sres, writes=BSQR)
            fns = []
            for nq in range(4):
                fns.append(lambda e, nq=nq: e.matmul(
                    psB[:, nq * 512:(nq + 1) * 512][0:1, :],
                    P.consts["ones_cb"][:],
                    bsq[:, nq * 512:(nq + 1) * 512],
                    start=True, stop=True))
            P.group("tensor", fns, reads=BSQR + ("c_ones_cb",), writes=PSB)
            P.op("vector", lambda e, mcol=mcol: e.reduce_max(
                mm2[:, mcol:mcol + 1], psB[:, 0:2048][0:1, :], axis=AX.X),
                reads=PSB, writes=("mm2",))
        P.op("vector", lambda e: e.tensor_tensor(
            mm2[:, 2:3], mm2[:, 0:1], mm2[:, 1:2], ALU.mult),
            reads=("mm2",), writes=("mm2",))
        P.op("scalar", lambda e: e.activation(
            mm2[:, 3:4], mm2[:, 2:3], AF.Sqrt),
            reads=("mm2",), writes=("mm2",))
        P.op("tensor", lambda e: e.matmul(
            psA[:, 0:1], P.consts["ones_f"][0:1, :],
            mm2[:, 3:4], start=True, stop=True),
            reads=("mm2", "c_ones_f"), writes=("psA.s0",))
        P.op("vector", lambda e, bs=bs: e.tensor_scalar(
            negb[bs][:], psA[:, 0:1], -1.0, None, ALU.mult),
            reads=("psA.s0",), writes=(f"negb{bs}",))

    pair_loads(0)
    for i, it in enumerate(pair_prod_items(0)):
        slot = (psA, (i % 2) * 1024, (f"psA.s{i % 2}",))
        it["mm"](slot)
        it["post"](slot)
    pair_bounds(0)
    for j in range(8):
        bs = j % 2
        qpair, kpair = qpair_b[bs], kpair_b[bs]
        if "alpha" in hw:
            for _ in range(12):
                hw["alpha"]["chunk"]()
        if j + 1 < 8:
            pair_loads(j + 1)
        for hh in range(2):
            h = 2 * j + hh
            base = hh * 64

            def qk_lhsT(kc, base=base, kpair=kpair):
                return kpair[base:base + 64, kc * 128:(kc + 1) * 128]

            def qk_rhs(lo, width, base=base, qpair=qpair):
                return qpair[base:base + 64, lo:lo + width]

            def qk_res(kc, half, bs=bs):
                return (f"qpair{bs}.{half}", f"kpair{bs}.{kc // 8}", "v_km")

            extra = pair_prod_items(j + 1) if (hh == 0 and j + 1 < 8) else ()
            emit_attention_head(
                P, h, qk_lhsT, qk_rhs, qk_res, v_km, attn,
                psA, psB, pbuf, causal=False, exp_scale=1.0,
                exp_bias=negb[bs][:], exp_bias_res=f"negb{bs}",
                clamp=None, mask0=None, scrA=scrA, rrow=rrow, astage=astage,
                extra_items=extra, evac_eng="vector")
            if hh == 0 and j + 1 < 8:
                pair_bounds(j + 1)

    st_att.close()
    P.full_barrier()
    st_lnT.close()
    P.full_barrier()
    st_o = ExitStack()
    owT = P.sb("owT", [128, 8 * D], BF16, st_o)
    P.op("gpsimd", lambda e: e.dma_start(
        out=owT[:].rearrange("p (k m) -> p k m", m=D),
        in_=owt_h.rearrange("(k p) m -> p k m", p=128)),
        writes=("owT",), dma=True)
    OBC = P.sb("OBC", [128, D], F32, st_o)
    emit_row_bcast(P, ob_h, "ob_h", OBC, "OBC", psB[:, 0:512], "psB.b0",
                   stack=st_o)
    pln = PostLN(P, "xs", xs, st_o, "po1",
                 nxt=hw.get("pre_stats", {}).get("lnd"))
    for i in range(NT):
        half = i % 2
        pres = f"psA.s{half}"
        fns = []
        for c in range(NCH):
            for n2 in range(2):
                fns.append(lambda e, i=i, c=c, n2=n2, half=half: e.matmul(
                    psA[:, half * 1024 + n2 * 512: half * 1024 + (n2 + 1) * 512],
                    attn[:, c * S + i * 128: c * S + (i + 1) * 128],
                    owT[:, c * D + n2 * 512: c * D + (n2 + 1) * 512],
                    start=(c == 0), stop=(c == 7)))
        P.group("tensor", fns, reads=("attn", "owT"), writes=(pres,))
        pln.chunk(i, psA[:, half * 1024:(half + 1) * 1024], pres,
                  bias_bcast=OBC, bias_res="OBC")
    st_o.close()
    P.full_barrier()
    st_attn.close()
    P.full_barrier()
    st_blk.close()
    P.full_barrier()



def mk_alpha(P, wupt_h, wdnt_h, st, st_ring):
    ainv_bc = [P.sb(f"ainv_bc{i}", [128, 1], F32, st, side="right")
               for i in range(2)]
    A = P.sb("alphaA", [128, 128], F32, st_ring, side="right")
    ring = [P.sb(f"al_wf{i}", [128, 512], F32, st_ring, side="right")
            for i in range(2)]
    idx = [0]

    def emit_abs(i):
        s = i % 2
        f = ring[s]
        rf = f"al_wf{s}"
        P.op("scalar", lambda e, f=f, i=i: e.activation(
            f[:], f[:], AF.Abs, accum_out=A[:, i:i + 1]),
            reads=(rf, "alphaA"), writes=(rf, "alphaA"))

    def emit_dma(i):
        s = i % 2
        f = ring[s]
        rf = f"al_wf{s}"
        if i < 64:
            c, q = divmod(i, 8)
            src = wupt_h[c * 128:(c + 1) * 128, q * 512:(q + 1) * 512]
        else:
            c, q = divmod(i - 64, 2)
            src = wdnt_h[c * 128:(c + 1) * 128, q * 512:(q + 1) * 512]
        P.op("scalar", lambda e, f=f, src=src: e.dma_start(
            out=f[:], in_=src), writes=(rf,), dma=True, dmaq="AL")

    def chunk():
        i = idx[0]
        if i >= 130:
            return
        idx[0] += 1
        if i >= 2:
            emit_abs(i - 2)
        if i < 128:
            emit_dma(i)

    return {"A": A, "ainv_bc": ainv_bc, "chunk": chunk, "idx": idx,
            "st_ring": st_ring}


def emit_block_ffn1(P, hw):
    xs = hw["xs"]
    wupt_h, wdnt_h = hw["wupt"], hw["wdnt"]
    identF = P.consts["identF"]
    wupq_hbm = P.nc.dram_tensor("wupq_scr", [D, DFF], BF16,
                                kind="Internal").ap()
    wdnq_hbm = P.nc.dram_tensor("wdnq_scr", [DFF, D], BF16,
                                kind="Internal").ap()

    al = hw["alpha"]
    while al["idx"][0] < 130:
        al["chunk"]()
    ainv_bc = al["ainv_bc"]
    A = al["A"]
    st_fin = ExitStack()
    arow = P.sb("arow", [1, 128], F32, st_fin)
    atot = P.sb("atot", [1, 4], F32, st_fin)
    ps_a = P.ps("ps_a", [128, 512], F32, st_fin)

    def alpha_finish(wi, lo, n, rows, cols):
        P.group("tensor", [lambda e, lo=lo, n=n: e.matmul(
            ps_a[:, 0:n][0:1, :], P.consts["ones_cf"][:], A[:, lo:lo + n],
            start=True, stop=True)],
            reads=("alphaA", "c_ones_cf"), writes=("ps_a",))
        P.op("vector", lambda e, lo=lo, n=n: e.tensor_copy(
            arow[:, lo:lo + n], ps_a[:, 0:n][0:1, :]),
            reads=("ps_a",), writes=(f"arow{wi}",))
        P.op("vector", lambda e, lo=lo, n=n, wi=wi: e.reduce_sum(
            atot[:, wi:wi + 1], arow[:, lo:lo + n], axis=AX.X),
            reads=(f"arow{wi}",), writes=(f"atot{wi}",))
        P.op("vector", lambda e, wi=wi: e.reciprocal(
            atot[:, 2 + wi:3 + wi], atot[:, wi:wi + 1]),
            reads=(f"atot{wi}",), writes=(f"arcp{wi}",))
        P.op("vector", lambda e, wi=wi, rows=rows, cols=cols: e.tensor_scalar(
            atot[:, 2 + wi:3 + wi], atot[:, 2 + wi:3 + wi],
            float(rows * cols), None, ALU.mult),
            reads=(f"arcp{wi}",), writes=(f"arcp{wi}",))
        P.op("tensor", lambda e, wi=wi: e.matmul(
            ps_a[:, 0:1], P.consts["ones_f"][0:1, :], atot[:, 2 + wi:3 + wi],
            start=True, stop=True),
            reads=(f"arcp{wi}", "c_ones_f"), writes=("ps_a",))
        P.op("vector", lambda e, wi=wi: e.tensor_copy(
            ainv_bc[wi][:], ps_a[:, 0:1]),
            reads=("ps_a",), writes=(f"ainv_bc{wi}",))

    alpha_finish(0, 0, 64, D, DFF)
    alpha_finish(1, 64, 64, DFF, D)
    st_fin.close()
    al["st_ring"].close()
    P.full_barrier()

    st_blk = ExitStack()
    xnT = P.sb("xnT", [128, NCH * S], BF16, st_blk)

    st_ln = ExitStack()
    pool = hw.get("pre_stats")
    if pool:
        stats = pool["lnd"]
        emit_preln_stats(P, "xs", xs, stats, pre=True)
    else:
        stats = make_stats(P, st_ln, "lnd")
        emit_preln_stats(P, "xs", xs, stats)
    sv = P.sb("lnd_sv", [128, NT], F32, st_ln)
    scl = P.sb("lnd_scl", [128, NT], F32, st_ln)
    nms = P.sb("lnd_nms", [128, NT], F32, st_ln)
    P.op("scalar", lambda e: e.activation(sv[:], stats["var"][:], AF.Sqrt),
         reads=("lnd.var",), writes=("lnd.sv",))
    P.op("vector", lambda e: e.tensor_tensor(
        sv[:], sv[:], stats["rstd"][:], ALU.mult),
        reads=("lnd.sv", "lnd.rstd"), writes=("lnd.rms",))
    P.op("vector", lambda e: e.tensor_scalar(
        sv[:], sv[:], float(EPS_RMS), None, ALU.add),
        reads=("lnd.rms",), writes=("lnd.d2",))
    P.op("vector", lambda e: e.reciprocal(scl[:], sv[:]),
         reads=("lnd.d2",), writes=("lnd.f",))
    P.op("vector", lambda e: e.tensor_tensor(
        scl[:], stats["rstd"][:], scl[:], ALU.mult),
        reads=("lnd.f", "lnd.rstd"), writes=("lnd.scl",))
    P.op("vector", lambda e: e.tensor_tensor(
        nms[:], stats["mean"][:], scl[:], ALU.mult),
        reads=("lnd.scl", "lnd.mean"), writes=("lnd.nms",))
    P.op("vector", lambda e: e.tensor_scalar(
        nms[:], nms[:], -1.0, None, ALU.mult),
        reads=("lnd.nms",), writes=("lnd.nms",))
    def xn_apply(t, dst, dres):
        P.op("scalar", lambda e, t=t, dst=dst: e.activation(
            dst[:], xs[:, t * D:(t + 1) * D], AF.Identity,
            bias=nms[:, t:t + 1], scale=scl[:, t:t + 1]),
            reads=("xs",), sreads=("lnd.nms", "lnd.scl"), writes=(dres,))

    tp_stack = ExitStack()
    emit_ln_transpose(P, xn_apply, xnT, "xnT", tp_stack)
    tp_stack.close()
    P.full_barrier()
    st_ln.close()
    P.full_barrier()

    st_mm = ExitStack()
    hbuf = P.sb("hbuf", [128, 32 * 512], BF16, st_mm)
    sqd = [P.sb(f"sqd{i}", [128, 512], F32, st_mm) for i in range(2)]
    r2row = P.sb("r2row", [1, 512], F32, st_mm)
    rstd2T = P.sb("rstd2T", [128, 4], F32, st_mm)
    pln = PostLN(P, "xs", xs, st_mm, "pod")

    def up_evac(hc, psum_ap, psum_res):
        P.op("scalar", lambda e, psum_ap=psum_ap, hc=hc: e.activation(
            hbuf[:, hc * 512:(hc + 1) * 512], psum_ap, AF.Gelu),
            reads=(psum_res,), writes=(f"hbuf.{hc}",))

    phase = {}

    def rms_mm(hc):
        ps_dn0 = phase["ps_dn0"]
        sq = sqd[hc % 2]
        rq = f"sqd{hc % 2}"
        P.op("tensor", lambda e, sq=sq, hc=hc, ps_dn0=ps_dn0: e.matmul(
            ps_dn0[:, 0:512][0:1, :], P.consts["ones_cf"][:], sq[:],
            start=(hc == 0), stop=(hc == 31)),
            reads=(rq, "c_ones_cf"), writes=("ps_dn0",))

    def after_up(hc):
        if hc > 0:
            rms_mm(hc - 1)
        sq = sqd[hc % 2]
        rq = f"sqd{hc % 2}"
        P.op("scalar", lambda e, sq=sq, hc=hc: e.activation(
            sq[:], hbuf[:, hc * 512:(hc + 1) * 512], AF.Square),
            reads=(f"hbuf.{hc}",), writes=(rq,))

    def down_pre(jt, _ps=[None]):
        rms_mm(31)
        ps_dn0 = phase["ps_dn0"]
        P.op("vector", lambda e: e.tensor_scalar(
            r2row[:], ps_dn0[:, 0:512][0:1, :], float(1.0 / DFF), None,
            ALU.mult), reads=("ps_dn0",), writes=("r2row",))
        P.op("scalar", lambda e: e.activation(
            r2row[:], r2row[:], AF.Sqrt), reads=("r2row",), writes=("r2row",))
        P.op("vector", lambda e: e.tensor_scalar(
            r2row[:], r2row[:], float(EPS_RMS), None, ALU.add),
            reads=("r2row",), writes=("r2row",))
        P.op("vector", lambda e: e.reciprocal(r2row[:], r2row[:]),
             reads=("r2row",), writes=("r2row",))
        for qc in range(4):
            P.op("tensor", lambda e, qc=qc: e.matmul(
                phase["ps_dn1"][:, 0:1], r2row[0:1, qc * 128:(qc + 1) * 128],
                P.consts["ones_f"][0:1, 0:1], start=True, stop=True),
                reads=("r2row", "c_ones_f"), writes=("ps_dn1",))
            P.op("vector", lambda e, qc=qc: e.tensor_copy(
                rstd2T[:, qc:qc + 1], phase["ps_dn1"][:, 0:1]),
                reads=("ps_dn1",), writes=("rstd2T",))
        return lambda qc: (rstd2T[:, qc:qc + 1], "rstd2T")

    emit_ffn_matmul_phase(P, xnT, "xnT", hbuf, wupq_hbm, wdnq_hbm, st_mm,
                          up_evac, down_pre, pln, phase_out=phase,
                          w1_src=wupt_h, w2_src=wdnt_h,
                          quant=(mk_ternary(P, ainv_bc[0][:], "ainv_bc0"),
                                 mk_ternary(P, ainv_bc[1][:], "ainv_bc1")),
                          after_up=after_up)
    st_mm.close()
    P.full_barrier()
    st_blk.close()
    P.full_barrier()



def build_fused():
    P = Prog()
    hw = {}
    hw["x"] = P.dram_in("x", [S, D], F32)
    for n, shp in (("w_dq", [D, DLAT]), ("w_uq", [DLAT, D]),
                   ("w_qr", [DLAT, 512]), ("w_dkv", [D, DLAT]),
                   ("w_uk", [DLAT, D]), ("w_uv", [DLAT, D]),
                   ("w_kr", [D, DHR]), ("w_o", [D, D]),
                   ("w1t", [D, DFF]), ("b1", [DFF]), ("w2t", [DFF, D]),
                   ("b2", [D]), ("qwt", [D, D]), ("kwt", [D, D]),
                   ("vwt", [D, D]), ("owt", [D, D]), ("qb", [D]), ("kb", [D]),
                   ("vb", [D]), ("ob", [D]), ("wupt", [D, DFF]),
                   ("wdnt", [DFF, D]), ("identF", [128, 128])):
        hw[n] = P.dram_in(n, shp, F32)
    for n, shp in (("identB", [128, 128]), ("cos32", [32, S]),
                   ("sig32", [32, S]), ("swap32", [32, 32]),
                   ("mask0", [128, 128])):
        hw[n] = P.dram_in(n, shp, BF16)
    out_h = P.dram_out("xout", [S, D], F32)

    identB = P.sb("s_identB", [128, 128], BF16)
    identF = P.sb("s_identF", [128, 128], F32)
    P.consts["identB"] = identB
    P.consts["identF"] = identF
    emit_consts(P, [("eps_ln", [128, 1], F32, EPS_LN),
                    ("ones_f", [1, 128], F32, 1.0),
                    ("ones_cb", [128, 1], BF16, 1.0),
                    ("ones_cf", [128, 1], F32, 1.0)])
    P.op("sync", lambda e: e.dma_start(out=identB[:], in_=hw["identB"]),
         writes=("c_identB",), dma=True)
    P.op("sync", lambda e: e.dma_start(out=identF[:], in_=hw["identF"]),
         writes=("c_identF",), dma=True)

    xs = P.sb("xs", [128, NT * D], F32, side="right")
    hw["xs"] = xs
    hw["pre_stats"] = {t: make_stats(P, None, t, with_scratch=False)
                       for t in ("lnq", "ln1", "lnd")}
    emit_load_x(P, hw["x"], xs)

    hw["qb8"] = P.sb("qb8", [128, 8], F32)
    hw["kbp"] = P.sb("kbp", [128, 8], F32)
    pss = ExitStack()
    ps_small = P.ps("ps_small", [128, 512], F32, pss)
    emit_vec_to_partition(P, hw["qb"], "qb_h", D, hw["qb8"], "qb8", ps_small,
                          "ps_small", identF, post_scale=0.125, stack=pss)
    emit_vec_to_partition(P, hw["kb"], "kb_h", D, hw["kbp"], "kbp", ps_small,
                          "ps_small", identF, stack=pss)
    pss.close()
    P.full_barrier()

    emit_block_l0(P, hw)
    emit_block_ffn0(P, hw)
    st_alpha = ExitStack()
    st_ring = ExitStack()
    hw["alpha"] = mk_alpha(P, hw["wupt"], hw["wdnt"], st_alpha, st_ring)
    emit_block_l1(P, hw)
    emit_block_ffn1(P, hw)
    st_alpha.close()
    P.full_barrier(include_bg=True)
    emit_store_x(P, "xs", xs, out_h)
    return P.finalize()


_BUILD_CACHE = {}


def _get(builder, name):
    if name not in _BUILD_CACHE:
        _BUILD_CACHE[name] = builder()
    return _BUILD_CACHE[name]


def kernel(**inputs):
    f32 = lambda a: np.ascontiguousarray(np.asarray(a, dtype=np.float32))
    x = f32(inputs["L0_attn_x"])
    _ = inputs.get("L1_attn_mask")
    cos32, sig32, swap32 = _rope_tables()
    identB = np.ascontiguousarray(np.eye(128, dtype=BF))
    identF = np.ascontiguousarray(np.eye(128, dtype=np.float32))
    mask0 = _mask_tri()

    w_qr = f32(inputs["L0_attn_W_QR"])[:, _qr_perm()]
    w_kr = f32(inputs["L0_attn_W_KR"])[:, _kr_perm()]

    common = dict(
        w_dq=f32(inputs["L0_attn_W_DQ"]), w_uq=f32(inputs["L0_attn_W_UQ"]),
        w_qr=np.ascontiguousarray(w_qr), w_dkv=f32(inputs["L0_attn_W_DKV"]),
        w_uk=f32(inputs["L0_attn_W_UK"]), w_uv=f32(inputs["L0_attn_W_UV"]),
        w_kr=np.ascontiguousarray(w_kr), w_o=f32(inputs["L0_attn_W_O"]),
        identB=identB, cos32=cos32, sig32=sig32, swap32=swap32, mask0=mask0,
        w1t=np.ascontiguousarray(f32(inputs["L0_ffn_w1"]).T),
        b1=f32(inputs["L0_ffn_b1"]),
        w2t=np.ascontiguousarray(f32(inputs["L0_ffn_w2"]).T),
        b2=f32(inputs["L0_ffn_b2"]), identF=identF,
        qwt=np.ascontiguousarray(f32(inputs["L1_attn_q_w"]).T),
        kwt=np.ascontiguousarray(f32(inputs["L1_attn_k_w"]).T),
        vwt=np.ascontiguousarray(f32(inputs["L1_attn_v_w"]).T),
        owt=np.ascontiguousarray(f32(inputs["L1_attn_out_w"]).T),
        qb=f32(inputs["L1_attn_q_b"]), kb=f32(inputs["L1_attn_k_b"]),
        vb=f32(inputs["L1_attn_v_b"]), ob=f32(inputs["L1_attn_out_b"]),
        wupt=np.ascontiguousarray(f32(inputs["L1_ffn_w_up"]).T),
        wdnt=np.ascontiguousarray(f32(inputs["L1_ffn_w_down"]).T))

    nc = _get(build_fused, "fused")
    in_maps = [dict(x=np.ascontiguousarray(x[b]), **common) for b in range(8)]
    res = run_bass_kernel_spmd(nc, in_maps, core_ids=list(range(8)))
    out = np.stack([r["xout"] for r in res.results], axis=0)
    return np.ascontiguousarray(out.astype(np.float32))
